# version 01
# baseline (speedup 1.0000x reference)
import contextlib
import numpy as np
import ml_dtypes
import concourse.bass as bass
import concourse.mybir as mybir
from concourse.bass_utils import run_bass_kernel_spmd

F32 = mybir.dt.float32
BF16 = mybir.dt.bfloat16
I32 = mybir.dt.int32
AF = mybir.ActivationFunctionType
ALU = mybir.AluOpType
AX = mybir.AxisListType

B_, T, D = 2, 4096, 2048
HID = 5632
NH, DH = 16, 128
TOK = 1024
NTB = TOK // 128
NDC = D // 128
NHC = HID // 128
NP = 4
LN_EPS = 1e-5
Z_EPS = 1e-6

STAGE = 4
DBG = ()


class Bld:
    def __init__(self, nc, stack):
        self.nc = nc
        self.stack = stack
        self.cnt = {}
        self.sems = {}

    def sem(self, name):
        if name not in self.sems:
            self.sems[name] = self.stack.enter_context(self.nc.semaphore(name))
            self.cnt[name] = 0
        return self.sems[name]

    def inc(self, ins, name, n=1):
        ins.then_inc(self.sem(name), n)
        self.cnt[name] += n
        return self.cnt[name]

    def val(self, name):
        self.sem(name)
        return self.cnt[name]

    def wait(self, eng, name, v):
        assert 0 <= v <= self.cnt.get(name, 0), ("wait-ahead", name, v, self.cnt.get(name))
        if v > 0:
            eng.wait_ge(self.sem(name), v)


def build_core_program(core):
    nc = bass.Bass("TRN2", target_bir_lowering=False)
    g = core % 4
    to0 = g * TOK
    RG = [[0, 1, 2, 3], [4, 5, 6, 7]]

    X0 = nc.dram_tensor("x0", [T, D], F32, kind="ExternalInput").ap()
    X0M = nc.dram_tensor("x0m", [TOK, D], F32, kind="ExternalInput").ap()
    EXT = {}
    for nm, sh, dt in [("wq_a", [512, D], F32), ("wk_a", [512, D], F32),
                       ("wv_a", [512, D], F32), ("wo_a", [D, D], F32),
                       ("qw_b", [D, D], F32), ("kw_b", [D, D], F32),
                       ("vw_b", [D, D], F32), ("ow_b", [D, D], F32),
                       ("qb_b", [D], F32), ("kb_b", [D], F32), ("vb_b", [D], F32),
                       ("ob_b", [D], F32), ("wg_i", [HID, D], I32),
                       ("wu_i", [HID, D], I32), ("wd_i", [D, HID], I32),
                       ("sg", [HID], F32), ("su", [HID], F32), ("sd", [D], F32)]:
        EXT[nm] = nc.dram_tensor(nm, sh, dt, kind="ExternalInput").ap()
    Y = nc.dram_tensor("y", [TOK, D], F32, kind="ExternalOutput").ap()

    def dram(name, shape, dt=BF16):
        return nc.dram_tensor(name, shape, dt, kind="Internal")

    CAST_SRC = {"wqa": "wq_a", "wka": "wk_a", "wva": "wv_a", "woa": "wo_a",
                "g": "wg_i", "u": "wu_i", "d": "wd_i",
                "qwb": "qw_b", "kwb": "kw_b", "vwb": "vw_b", "owb": "ow_b"}
    WSHAPE = {"wqa": [512, D], "wka": [512, D], "wva": [512, D], "woa": [D, D],
              "g": [HID, D], "u": [HID, D], "d": [D, HID],
              "qwb": [D, D], "kwb": [D, D], "vwb": [D, D], "owb": [D, D]}
    wbf = {k: dram(k + "_bf", s) for k, s in WSHAPE.items()}
    wT = {k: dram(k + "_T", [s[1], s[0]]) for k, s in WSHAPE.items() if k != "woa"}
    ag_in = dram("ag_in", [NP * DH, T])
    ag_out = dram("ag_out", [NH * DH, T])
    ar_in = dram("ar_in", [NH * DH + NH, DH], F32)
    ar_out = dram("ar_out", [NH * DH + NH, DH], F32)

    ident_bf = nc.inline_tensor(np.eye(128, dtype=ml_dtypes.bfloat16), "c_ident_bf")
    ident_f = nc.inline_tensor(np.eye(128, dtype=np.float32), "c_ident_f")
    ones_row_f = nc.inline_tensor(np.ones((1, 128), dtype=np.float32), "c_ones_row")
    row128_f = nc.inline_tensor(np.full((1, 128), 128.0, dtype=np.float32), "c_row128")
    one11_f = nc.inline_tensor(np.ones((1, 1), dtype=np.float32), "c_one11")
    ones_col_bf = nc.inline_tensor(np.ones((128, 1), dtype=ml_dtypes.bfloat16), "c_ones_col")
    eps_col_f = nc.inline_tensor(np.full((128, 1), LN_EPS, dtype=np.float32), "c_eps_col")

    stack = contextlib.ExitStack()
    b = Bld(nc, stack)

    def sball(name, shape, dtype=BF16, st=None):
        return (st or stack).enter_context(nc.sbuf_tensor(name, shape, dtype))

    def psall(name, shape, st=None, dtype=F32):
        return (st or stack).enter_context(nc.psum_tensor(name, shape, dtype))

    ident_bf_sb = sball("ident_bf_sb", [128, 128])
    ident_f_sb = sball("ident_f_sb", [128, 128], F32)
    ones_row_sb = sball("ones_row_sb", [1, 128], F32)
    row128_sb = sball("row128_sb", [1, 128], F32)
    one11_sb = sball("one11_sb", [1, 1], F32)
    ones_col_sb = sball("ones_col_sb", [128, 1])
    eps_sb = sball("eps_sb", [128, 1], F32)
    qb_sb = sball("qb_sb", [128, 16], F32)
    kb_sb = sball("kb_sb", [128, 16], F32)
    vb_sb = sball("vb_sb", [128, 16], F32)
    sg_sb = sball("sg_sb", [128, NHC], F32)
    su_sb = sball("su_sb", [128, NHC], F32)
    sd_sb = sball("sd_sb", [128, 16], F32)
    st_a = sball("st_a", [128, 8], F32)
    WSEG = HID // 2
    wstg_st = contextlib.ExitStack()
    wstg = [sball(f"wstg{i}", [128, WSEG], BF16, wstg_st) for i in range(2)]

    dbg_outs = {}

    with nc.Block() as blk:
        @blk.sync
        def _(sync):
            for dst, src in [(ident_bf_sb, ident_bf), (ident_f_sb, ident_f),
                             (ones_row_sb, ones_row_f), (row128_sb, row128_f),
                             (one11_sb, one11_f), (ones_col_sb, ones_col_bf),
                             (eps_sb, eps_col_f)]:
                ins = sync.dma_start(out=dst[:], in_=src.ap())
                b.inc(ins, "ko", 16)
            for dst, src, n in [(qb_sb, EXT["qb_b"], 16), (kb_sb, EXT["kb_b"], 16),
                                (vb_sb, EXT["vb_b"], 16), (sg_sb, EXT["sg"], NHC),
                                (su_sb, EXT["su"], NHC), (sd_sb, EXT["sd"], 16)]:
                ins = sync.dma_start(out=dst[:, 0:n],
                                     in_=bass.AP(src.tensor, 0, [[1, 128], [128, n]]),
                                     allow_slow_non_contiguous=True)
                b.inc(ins, "ko", 16)
        konst_v = b.val("ko")

        cast_done = {}

        @blk.gpsimd
        def _(gpsimd):
            for key in ["wqa", "wka", "wva", "g", "woa", "u", "d",
                        "qwb", "kwb", "vwb", "owb"]:
                ins = gpsimd.dma_start(out=wbf[key].ap(), in_=EXT[CAST_SRC[key]])
                cast_done[key] = b.inc(ins, "ca_" + key, 16)

        for engname in ("tensor", "vector", "scalar"):
            @getattr(blk, engname)
            def _(eng):
                b.wait(eng, "ko", konst_v)

        prep_done = {}
        wr_hist = []

        def prep_wT(key, prev_key):
            rows, cols = WSHAPE[key]
            n = cols // 128
            segs = [(s, min(WSEG, rows - s)) for s in range(0, rows, WSEG)]
            ci = [0]
            for i in range(n):
                for r0, rn in segs:
                    j = ci[0]
                    ci[0] += 1

                    @blk.sync
                    def _(sync, key=key, i=i, j=j, r0=r0, rn=rn, prev_key=prev_key):
                        if j == 0:
                            b.wait(sync, "ca_" + key, 16)
                            if prev_key is not None:
                                b.wait(sync, "wr_" + prev_key,
                                       b.val("wr_" + prev_key))
                        else:
                            b.wait(sync, "wr_" + key, b.val("wr_" + key))
                        ins = sync.dma_start(
                            out=wstg[j % 2][:, 0:rn],
                            in_=wbf[key].ap()[r0:r0 + rn, i * 128:(i + 1) * 128],
                            transpose=True)
                        b.inc(ins, "wp", 16)
                    sv = b.val("wp")

                    @blk.gpsimd
                    def _(gpsimd, key=key, i=i, j=j, r0=r0, rn=rn, sv=sv):
                        b.wait(gpsimd, "wp", sv)
                        ins = gpsimd.dma_start(
                            out=wT[key].ap()[i * 128:(i + 1) * 128, r0:r0 + rn],
                            in_=wstg[j % 2][:, 0:rn])
                        b.inc(ins, "wr_" + key, 16)
            prep_done[key] = b.val("wr_" + key)

        _pk = [None] + ["wqa", "wka", "wva", "g", "u", "d", "qwb", "kwb", "vwb"]
        _keys = ["wqa", "wka", "wva", "g", "u", "d", "qwb", "kwb", "vwb", "owb"]
        for key, pk in zip(_keys, _pk):
            prep_wT(key, pk)
        preps_all = [("wr_" + k, prep_done[k]) for k in _keys]

        def ln_tile(x_ap, out_tile, par, sqt, colsum_to=None, scalar_gate=0,
                    x_gate=None):
            c = par * 4

            @blk.vector
            def _(vector):
                ins = vector.reduce_sum(st_a[:, c:c + 1], x_ap, axis=AX.X)
                b.inc(ins, "dv")

            @blk.scalar
            def _(scalar):
                b.wait(scalar, "pe", scalar_gate)
                if x_gate is not None:
                    b.wait(scalar, x_gate[0], x_gate[1])
                ins = scalar.activation(sqt[:], x_ap, AF.Square,
                                        accum_out=st_a[:, c + 1:c + 2])
                b.inc(ins, "ac")
            v0, a0 = b.val("dv"), b.val("ac")

            @blk.vector
            def _(vector):
                b.wait(vector, "dv", v0)
                b.wait(vector, "ac", a0)
                ins = vector.tensor_scalar_mul(st_a[:, c:c + 1], st_a[:, c:c + 1], 1.0 / D)
                b.inc(ins, "dv")
                b.wait(vector, "dv", b.val("dv"))
                ins = vector.tensor_mul(st_a[:, c + 2:c + 3], st_a[:, c:c + 1],
                                        st_a[:, c:c + 1])
                b.inc(ins, "dv")
                b.wait(vector, "dv", b.val("dv"))
                ins = vector.tensor_scalar(st_a[:, c + 1:c + 2], st_a[:, c + 1:c + 2],
                                           1.0 / D, None, ALU.mult)
                b.inc(ins, "dv")
                b.wait(vector, "dv", b.val("dv"))
                ins = vector.tensor_sub(st_a[:, c + 1:c + 2], st_a[:, c + 1:c + 2],
                                        st_a[:, c + 2:c + 3])
                b.inc(ins, "dv")
            v1 = b.val("dv")

            @blk.scalar
            def _(scalar):
                b.wait(scalar, "dv", v1)
                ins = scalar.activation(st_a[:, c + 2:c + 3], st_a[:, c + 1:c + 2],
                                        AF.Ln, bias=eps_sb[:, 0:1])
                b.inc(ins, "ac")
                b.wait(scalar, "ac", b.val("ac"))
                ins = scalar.activation(st_a[:, c + 3:c + 4], st_a[:, c + 2:c + 3],
                                        AF.Exp, scale=-0.5)
                b.inc(ins, "ac")
            a1 = b.val("ac")

            @blk.vector
            def _(vector):
                b.wait(vector, "ac", a1)
                ins = vector.tensor_scalar(out_tile[:], x_ap, st_a[:, c:c + 1],
                                           st_a[:, c + 3:c + 4], ALU.subtract, ALU.mult)
                b.inc(ins, "dv")
                if colsum_to is not None:
                    t, cc = colsum_to
                    b.wait(vector, "dv", b.val("dv"))
                    ins = vector.reduce_sum(t[:, cc:cc + 1], out_tile[:], axis=AX.X)
                    b.inc(ins, "dv")
            return b.val("dv")

        tstate = {"n": 0, "hist": []}

        def transpose_blocks(tp_ps, srcs, dsts, pe_waits=(), f32=False):
            ident = ident_f_sb if f32 else ident_bf_sb
            for i, ((s_t, s_c), (d_t, d_c)) in enumerate(zip(srcs, dsts)):
                par = tstate["n"] % 2
                tstate["n"] += 1

                @blk.tensor
                def _(tensor, s_t=s_t, s_c=s_c, par=par, i=i):
                    for nm, v in (pe_waits if i == 0 else ()):
                        b.wait(tensor, nm, v)
                    if len(tstate["hist"]) >= 2:
                        b.wait(tensor, "dv", tstate["hist"][-2])
                    ins = tensor.matmul(tp_ps[:, par * 512:par * 512 + 128],
                                        s_t[:, s_c:s_c + 128], ident[:],
                                        start=True, stop=True)
                    b.inc(ins, "pe")
                pv = b.val("pe")

                @blk.vector
                def _(vector, d_t=d_t, d_c=d_c, par=par, pv=pv):
                    b.wait(vector, "pe", pv)
                    ins = vector.tensor_copy(d_t[:, d_c:d_c + 128],
                                             tp_ps[:, par * 512:par * 512 + 128])
                    tstate["hist"].append(b.inc(ins, "dv"))
            return b.val("dv")

        def barrier(dma_waits=()):
            for engname in ("tensor", "vector", "scalar"):
                @getattr(blk, engname)
                def _(eng):
                    ins = eng.nop()
                    b.inc(ins, "bar")
            t = b.val("bar")
            for engname in ("tensor", "vector", "scalar", "sync", "gpsimd"):
                @getattr(blk, engname)
                def _(eng):
                    b.wait(eng, "bar", t)
                    for nm, v in dma_waits:
                        b.wait(eng, nm, v)

        def emit_dbg(name):
            if name not in DBG:
                return
            out = nc.dram_tensor("dbg_" + name, [TOK, D], F32, kind="ExternalOutput").ap()
            dbg_outs[name] = out
            wv = b.val("dv")

            @blk.gpsimd
            def _(gpsimd):
                b.wait(gpsimd, "dv", wv)
                for tb in range(NTB):
                    ins = gpsimd.dma_start(out=out[tb * 128:(tb + 1) * 128, :],
                                           in_=x_res[:, tb * D:(tb + 1) * D])
                    b.inc(ins, "dbg", 16)

        pa = contextlib.ExitStack()
        q_fm = sball("q_fm", [128, NP * T], BF16, pa)
        k_fm = sball("k_fm", [128, NP * T], BF16, pa)
        v_tm = sball("v_tm", [128, NP * 32 * 128], BF16, pa)
        tp_ps = psall("tp_psA", [128, 1024], pa)

        pa1 = contextlib.ExitStack()
        lnx_fm = sball("lnx_fm", [128, NDC * 512], BF16, pa1)
        v_fms = sball("v_fms", [128, NP * 512], BF16, pa1)
        x0t = [sball(f"x0t{i}", [128, D], F32, pa1) for i in range(2)]
        sqt_a = sball("sqt_a", [128, D], BF16, pa1)
        lnq_tm = [sball(f"lnq{i}", [128, D], BF16, pa1) for i in range(2)]
        aring = [sball(f"aring{i}", [128, NDC * 128], BF16, pa1) for i in range(2)]
        pj_ps = psall("pj_psA", [128, 2048], pa1)

        x0_free = [0, 0]
        lnq_free = [0, 0]
        aring_free = []
        pj_free = []
        a2_done = [0]

        for sl in range(8):
            for tt in range(4):
                tile = sl * 4 + tt
                par = tile % 2

                @blk.sync
                def _(sync, tile=tile, par=par):
                    b.wait(sync, "dv", x0_free[par])
                    ins = sync.dma_start(out=x0t[par][:],
                                         in_=X0[tile * 128:(tile + 1) * 128, :])
                    b.inc(ins, "x0w%d" % par, 16)
                xv = b.val("x0w%d" % par)

                @blk.vector
                def _(vector, par=par, xv=xv):
                    b.wait(vector, "x0w%d" % par, xv)
                    b.wait(vector, "pe", lnq_free[par])
                lv = ln_tile(x0t[par][:], lnq_tm[par], par, sqt_a,
                             x_gate=("x0w%d" % par, xv))
                x0_free[par] = lv
                transpose_blocks(
                    tp_ps,
                    [(lnq_tm[par], dc * 128) for dc in range(NDC)],
                    [(lnx_fm, dc * 512 + tt * 128) for dc in range(NDC)],
                    pe_waits=[("dv", lv), ("pe", a2_done[0])],
                )

                lnq_free[par] = b.val("pe")
            slab_tp = b.val("dv")

            for iw, key in enumerate(["wqa", "wka", "wva"]):
                for oc in range(NP):
                    li = len(aring_free)
                    slot = li % 2

                    @blk.sync
                    def _(sync, key=key, oc=oc, li=li, slot=slot):
                        b.wait(sync, "wr_" + key, prep_done[key])
                        if li >= 2:
                            b.wait(sync, "pe", aring_free[li - 2])
                        ins = sync.dma_start(
                            out=aring[slot][:].rearrange("p (a c) -> p a c", c=128),
                            in_=wT[key].ap()[0:D, oc * 128:(oc + 1) * 128]
                                .rearrange("(a p) c -> p a c", p=128))
                        b.inc(ins, "aw%d" % slot, 16)
                    wv = b.val("aw%d" % slot)
                    gi = sl * 12 + iw * NP + oc
                    bank = gi % 4

                    @blk.tensor
                    def _(tensor, slot=slot, gi=gi, bank=bank, wv=wv, slab_tp=slab_tp):
                        b.wait(tensor, "aw%d" % slot, wv)
                        b.wait(tensor, "dv", slab_tp)
                        if gi >= 4:
                            b.wait(tensor, "dv", pj_free[gi - 4])
                        for dc in range(NDC):
                            ins = tensor.matmul(
                                pj_ps[:, bank * 512:(bank + 1) * 512],
                                aring[slot][:, dc * 128:(dc + 1) * 128],
                                lnx_fm[:, dc * 512:(dc + 1) * 512],
                                start=(dc == 0), stop=(dc == NDC - 1))
                        b.inc(ins, "pe")
                    aring_free.append(b.val("pe"))
                    mv = b.val("pe")

                    @blk.vector
                    def _(vector, iw=iw, oc=oc, sl=sl, bank=bank, mv=mv):
                        b.wait(vector, "pe", mv)
                        dst = [q_fm, k_fm, v_fms][iw]
                        col = (oc * T + sl * 512) if iw < 2 else (oc * 512)
                        if iw == 0:
                            ins = vector.tensor_scalar_mul(
                                dst[:, col:col + 512],
                                pj_ps[:, bank * 512:(bank + 1) * 512],
                                1.0 / np.sqrt(float(DH)))
                        else:
                            ins = vector.tensor_copy(dst[:, col:col + 512],
                                                     pj_ps[:, bank * 512:(bank + 1) * 512])
                        pj_free.append(b.inc(ins, "dv"))
            vv = b.val("dv")
            transpose_blocks(
                tp_ps,
                [(v_fms, p * 512 + kk * 128) for p in range(NP) for kk in range(4)],
                [(v_tm, (p * 32 + (sl * 4 + kk)) * 128) for p in range(NP)
                 for kk in range(4)],
                pe_waits=[("dv", vv)],
            )

            a2_done[0] = b.val("pe")

        pa1.close()
        import os as _os
        if _os.environ.get("KMEM"):
            print("after A2 close:", nc.sbuf_bytes_remaining)
        flash_dv0 = b.val("dv")

        s_ps = [psall("s_ps0", [128, 1024], pa), psall("s_ps1", [128, 1024], pa)]
        pv_ps = psall("pv_ps", [128, 512], pa)
        rb_ps = psall("rb_ps", [128, 512], pa)
        stage = [sball("stageA", [128, T], F32, pa)]
        p_sb = [sball(f"p_sb{i}", [128, T], BF16, pa) for i in range(4)]
        pt_sb = sball("pt_sb", [128, 32 * 512], BF16, pa)
        ao_sb = [sball(f"ao_sb{i}", [128, 512], BF16, pa) for i in range(2)]
        sums_sb = sball("sums_sb", [128, 32], F32, pa)
        nmax_sb = sball("nmax_sb", [128, 4], F32, pa)
        rrow_sb = sball("rrow_sb", [1, 512], F32, pa)
        bcast_sb = sball("bcast_sb", [128, 512], F32, pa)

        scp_v = []
        exp_v = []
        ptdone_v = []
        ptev_v = []
        pv_done = []
        pv_ev = []
        ag_wr = []
        rr_seq = []
        rr_pe = []

        for p in range(NP):
            qc = p * T
            for qs in range(8):
                gqs = p * 8 + qs
                for qb in range(4):
                    qi = qs * 4 + qb
                    m = p * 32 + qi
                    par = 0
                    for j4 in range(4):
                        n = m * 4 + j4
                        grp = n % 2

                        @blk.tensor
                        def _(tensor, qi=qi, j4=j4, grp=grp, n=n, qc=qc):
                            if n >= 2:
                                nm, v = scp_v[n - 2]
                                b.wait(tensor, nm, v)
                            else:
                                b.wait(tensor, "dv", flash_dv0)
                            for jj in range(2):
                                kchunk = j4 * 2 + jj
                                ins = tensor.matmul(
                                    s_ps[grp][:, jj * 512:(jj + 1) * 512],
                                    q_fm[:, qc + qi * 128:qc + (qi + 1) * 128],
                                    k_fm[:, qc + kchunk * 512:qc + (kchunk + 1) * 512],
                                    start=True, stop=True)
                            b.inc(ins, "pe")
                        sv = b.val("pe")

                        if n % 2 == 0:
                            @blk.vector
                            def _(vector, sv=sv, grp=grp, par=par, j4=j4, m=m):
                                b.wait(vector, "pe", sv)
                                if m >= 1 and j4 < 2:
                                    b.wait(vector, "ac", exp_v[m - 1])
                                ins = vector.tensor_copy(
                                    stage[par][:, j4 * 1024:(j4 + 1) * 1024],
                                    s_ps[grp][:])
                                scp_v.append(("dv", b.inc(ins, "dv")))
                        else:
                            @blk.scalar
                            def _(scalar, sv=sv, grp=grp, par=par, j4=j4, m=m):
                                b.wait(scalar, "pe", sv)
                                if m >= 1 and j4 < 2:
                                    b.wait(scalar, "ac", exp_v[m - 1])
                                ins = scalar.activation(
                                    stage[par][:, j4 * 1024:(j4 + 1) * 1024],
                                    s_ps[grp][:], AF.Identity)
                                scp_v.append(("ac", b.inc(ins, "ac")))
                    cp_dv = max(v for nm, v in scp_v[m * 4:m * 4 + 4] if nm == "dv")
                    cp_ac = max(v for nm, v in scp_v[m * 4:m * 4 + 4] if nm == "ac")

                    @blk.vector
                    def _(vector, par=par, qb=qb, m=m, cp_dv=cp_dv, cp_ac=cp_ac):
                        b.wait(vector, "dv", cp_dv)
                        b.wait(vector, "ac", cp_ac)
                        if m >= 4:
                            b.wait(vector, "ac", exp_v[m - 4])
                        ins = vector.reduce_max(nmax_sb[:, qb:qb + 1], stage[par][:],
                                                axis=AX.X, negate=True)
                        b.inc(ins, "dv")
                    mxv = b.val("dv")

                    @blk.scalar
                    def _(scalar, par=par, qb=qb, qi=qi, p=p, gqs=gqs, mxv=mxv):
                        b.wait(scalar, "dv", mxv)
                        if gqs >= 1:
                            b.wait(scalar, "pe", ptdone_v[gqs - 1])
                        if p >= 1:
                            b.wait(scalar, "pe", rr_pe[(p - 1) * 8 + qi // 4])
                        ins = scalar.activation(p_sb[qb][:], stage[par][:], AF.Exp,
                                                bias=nmax_sb[:, qb:qb + 1],
                                                accum_out=sums_sb[:, qi:qi + 1])
                        exp_v.append(b.inc(ins, "ac"))
                ev = exp_v[-1]

                for kc in range(32):
                    t = gqs * 32 + kc
                    par2 = kc % 2

                    @blk.tensor
                    def _(tensor, kc=kc, par2=par2, ev=ev, t=t):
                        if kc == 0:
                            b.wait(tensor, "ac", ev)
                        if t >= 2:
                            b.wait(tensor, "dv", ptev_v[t - 2])
                        for qb in range(4):
                            ins = tensor.matmul(
                                tp_ps[:, par2 * 512 + qb * 128:par2 * 512 + (qb + 1) * 128],
                                p_sb[qb][:, kc * 128:(kc + 1) * 128], ident_bf_sb[:],
                                start=True, stop=True)
                        b.inc(ins, "pe")
                    tv = b.val("pe")

                    @blk.vector
                    def _(vector, kc=kc, par2=par2, tv=tv, gqs=gqs):
                        b.wait(vector, "pe", tv)
                        if gqs >= 1 and kc == 0:
                            b.wait(vector, "pe", pv_done[gqs - 1])
                        ins = vector.tensor_copy(pt_sb[:, kc * 512:(kc + 1) * 512],
                                                 tp_ps[:, par2 * 512:(par2 + 1) * 512])
                        ptev_v.append(b.inc(ins, "dv"))

                ptdone_v.append(b.val("pe"))
                ptv = ptev_v[-1]

                @blk.tensor
                def _(tensor, p=p, gqs=gqs, ptv=ptv):
                    b.wait(tensor, "dv", ptv)
                    if gqs >= 1:
                        b.wait(tensor, "dv", pv_ev[gqs - 1])
                    for kc in range(32):
                        ins = tensor.matmul(
                            pv_ps[:],
                            v_tm[:, (p * 32 + kc) * 128:(p * 32 + kc + 1) * 128],
                            pt_sb[:, kc * 512:(kc + 1) * 512],
                            start=(kc == 0), stop=(kc == 31))
                    b.inc(ins, "pe")
                pv_done.append(b.val("pe"))

                @blk.tensor
                def _(tensor, qs=qs, gqs=gqs):
                    if gqs >= 1:
                        b.wait(tensor, "dv", rr_seq[gqs - 1])
                    for qb in range(4):
                        qi = qs * 4 + qb
                        ins = tensor.matmul(rb_ps[0:1, qb * 128:(qb + 1) * 128],
                                            sums_sb[:, qi:qi + 1], ident_f_sb[:],
                                            start=True, stop=True)
                    b.inc(ins, "pe")
                rr_pe.append(b.val("pe"))
                rv = rr_pe[-1]

                @blk.vector
                def _(vector, rv=rv):
                    b.wait(vector, "pe", rv)
                    ins = vector.reciprocal(rrow_sb[:], rb_ps[0:1, 0:512])
                    b.inc(ins, "dv")
                rev = b.val("dv")

                @blk.tensor
                def _(tensor, rev=rev):
                    b.wait(tensor, "dv", rev)
                    ins = tensor.matmul(rb_ps[:, 0:512], ones_row_sb[:], rrow_sb[:],
                                        start=True, stop=True)
                    b.inc(ins, "pe")
                bv = b.val("pe")
                pvv = pv_done[-1]

                @blk.vector
                def _(vector, bv=bv, pvv=pvv, gqs=gqs):
                    b.wait(vector, "pe", bv)
                    ins = vector.tensor_copy(bcast_sb[:], rb_ps[:, 0:512])
                    rr_seq.append(b.inc(ins, "dv"))
                    b.wait(vector, "pe", pvv)
                    b.wait(vector, "dv", b.val("dv"))
                    if gqs >= 2:
                        b.wait(vector, "agw", b.val("agw"))
                    ins = vector.tensor_mul(ao_sb[gqs % 2][:], pv_ps[:], bcast_sb[:])
                    pv_ev.append(b.inc(ins, "dv"))
                aov = pv_ev[-1]

                @blk.gpsimd
                def _(gpsimd, p=p, qs=qs, gqs=gqs, aov=aov):
                    b.wait(gpsimd, "dv", aov)
                    ins = gpsimd.dma_start(
                        out=ag_in.ap()[p * 128:(p + 1) * 128, qs * 512:(qs + 1) * 512],
                        in_=ao_sb[gqs % 2][:])
                    ag_wr.append(b.inc(ins, "agw", 16))

        flash_done = pv_ev[-1]
        ag_all = ag_wr[-1]
        pa.close()
        wstg_st.close()
        import os as _os2
        if _os2.environ.get("KMEM"):
            print("after flash close:", nc.sbuf_bytes_remaining)

        @blk.gpsimd
        def _(gpsimd):
            b.wait(gpsimd, "agw", ag_all)
            ins = gpsimd.collective_compute(
                "AllGather", ALU.bypass, RG, [ag_in.ap().opt()], [ag_out.ap().opt()])
            b.inc(ins, "coll")
        coll_ag = b.val("coll")

        x_res = sball("x_res", [128, NTB * D], F32)
        pa6 = contextlib.ExitStack()
        og_sb = sball("og_sb", [128, NH * 1024], BF16, pa6)
        woT_sb = sball("woT_sb", [128, NH * D], BF16, pa6)
        x0s = [sball(f"x0s{i}", [128, D], F32, pa6) for i in range(2)]
        op_ps = psall("op_psA", [128, 2048], pa6)

        @blk.sync
        def _(sync):
            for nm, v in preps_all:
                b.wait(sync, nm, v)
            b.wait(sync, "ca_woa", 16)
            b.wait(sync, "dv", flash_done)
            for h in range(NH):
                ins = sync.dma_start(out=woT_sb[:, h * D:(h + 1) * D],
                                     in_=wbf["woa"].ap()[0:D, h * 128:(h + 1) * 128],
                                     transpose=True)
                b.inc(ins, "wow", 16)
        woT_v = b.val("wow")

        @blk.gpsimd
        def _(gpsimd):
            b.wait(gpsimd, "coll", coll_ag)
            pid = gpsimd.partition_id()
            to0v = gpsimd.compute_val((pid % 4) * TOK)
            for h in range(NH):
                ins = gpsimd.dma_start(
                    out=og_sb[:, h * 1024:(h + 1) * 1024],
                    in_=ag_out.ap()[h * 128:(h + 1) * 128, bass.ds(to0v, TOK)])
                b.inc(ins, "ogw", 16)
        og_v = b.val("ogw")

        op_free = []
        for tb in range(NTB):
            par = tb % 2

            @blk.sync
            def _(sync, tb=tb, par=par):
                b.wait(sync, "dv", flash_done)
                if tb >= 2:
                    b.wait(sync, "dv", op_free[tb - 2])
                ins = sync.dma_start(out=x0s[par][:],
                                     in_=X0M[tb * 128:(tb + 1) * 128, :])
                b.inc(ins, "xsw%d" % par, 16)
            xv = b.val("xsw%d" % par)

            @blk.tensor
            def _(tensor, tb=tb, xv=xv):
                if tb == 0:
                    b.wait(tensor, "ogw", og_v)
                    b.wait(tensor, "wow", woT_v)
                if tb >= 1:
                    b.wait(tensor, "dv", op_free[tb - 1])
                for h in range(NH):
                    for ods in range(4):
                        ins = tensor.matmul(
                            op_ps[:, ods * 512:(ods + 1) * 512],
                            og_sb[:, h * 1024 + tb * 128:h * 1024 + (tb + 1) * 128],
                            woT_sb[:, h * D + ods * 512:h * D + (ods + 1) * 512],
                            start=(h == 0), stop=(h == NH - 1))
                b.inc(ins, "pe")
            ov = b.val("pe")

            @blk.vector
            def _(vector, tb=tb, par=par, ov=ov, xv=xv):
                b.wait(vector, "pe", ov)
                b.wait(vector, "xsw%d" % par, xv)
                for ods in range(4):
                    ins = vector.tensor_add(
                        x_res[:, tb * D + ods * 512:tb * D + (ods + 1) * 512],
                        op_ps[:, ods * 512:(ods + 1) * 512],
                        x0s[par][:, ods * 512:(ods + 1) * 512])
                op_free.append(b.inc(ins, "dv"))
        pa6.close()
        import os as _os3
        if _os3.environ.get("KMEM"):
            print("after A6 close:", nc.sbuf_bytes_remaining)
        emit_dbg("x1")

        def emit_ffn(tag):
            ps = contextlib.ExitStack()
            sb = lambda n, s, d=BF16: sball(n + tag, s, d, ps)
            lnt = [sb(f"lnt{i}", [128, D]) for i in range(2)]
            sqt = sb("sqt", [128, D])
            lnf = sb("lnf", [128, NDC * 1024])
            cs_cols = sb("cs_cols", [128, NTB], F32)
            csr = sb("csr", [1, 1024], F32)
            cs_bc = sb("cs_bc", [128, 1024], F32)
            cshr = sb("cshr", [1, 512], F32)
            csh_bc = sb("csh_bc", [128, 512], F32)
            h_fm = sb("h_fm", [128, NHC * 512])
            tg_sc = [sb(f"tg{i}", [128, 512], F32) for i in range(2)]
            tu_sc = [sb(f"tu{i}", [128, 512], F32) for i in range(2)]
            gsig = [sb(f"gsig{i}", [128, 512]) for i in range(2)]
            dnf = sb("dnf", [128, 2048], F32)
            wring = [sb(f"wring{i}", [128, NDC * 128]) for i in range(3)]
            dring = [sb(f"dring{i}", [128, (NHC // 2) * 128]) for i in range(2)]
            tp2 = psall("tp2" + tag, [128, 1024], ps)
            gu_ps = psall("gu" + tag, [128, 2048], ps)
            dn_ps = psall("dn" + tag, [128, 1024], ps)

            lnt_free = [0, 0]
            for tb in range(NTB):
                par = tb % 2

                @blk.vector
                def _(vector, par=par):
                    b.wait(vector, "pe", lnt_free[par])
                lv = ln_tile(x_res[:, tb * D:(tb + 1) * D], lnt[par], par, sqt,
                             colsum_to=(cs_cols, tb))
                transpose_blocks(
                    tp2,
                    [(lnt[par], dc * 128) for dc in range(NDC)],
                    [(lnf, dc * 1024 + tb * 128) for dc in range(NDC)],
                    pe_waits=[("dv", lv)],
                )
                lnt_free[par] = b.val("pe")
            lnf_v = b.val("dv")

            @blk.tensor
            def _(tensor, lnf_v=lnf_v):
                b.wait(tensor, "dv", lnf_v)
                for tb in range(NTB):
                    ins = tensor.matmul(gu_ps[0:1, tb * 128:(tb + 1) * 128],
                                        cs_cols[:, tb:tb + 1], ident_f_sb[:],
                                        start=True, stop=True)
                b.inc(ins, "pe")
            csv = b.val("pe")

            @blk.vector
            def _(vector, csv=csv):
                b.wait(vector, "pe", csv)
                ins = vector.tensor_copy(csr[:], gu_ps[0:1, 0:1024])
                b.inc(ins, "dv")
            csrv = b.val("dv")

            @blk.tensor
            def _(tensor, csrv=csrv):
                b.wait(tensor, "dv", csrv)
                tensor.matmul(gu_ps[:, 0:512], row128_sb[:], csr[0:1, 0:512],
                              start=True, stop=True)
                ins = tensor.matmul(gu_ps[:, 512:1024], row128_sb[:], csr[0:1, 512:1024],
                                    start=True, stop=True)
                b.inc(ins, "pe")
            csbv = b.val("pe")

            @blk.vector
            def _(vector, csbv=csbv):
                b.wait(vector, "pe", csbv)
                ins = vector.tensor_copy(cs_bc[:], gu_ps[:, 0:1024])
                b.inc(ins, "dv")
            csbc_v = b.val("dv")

            wring_free = []
            dring_free = []
            gu_free = []
            dn_free = []
            th_gate_wait = csbc_v
            for th in range(2):
                for oc in range(NHC):
                    idx = th * NHC + oc
                    for iw, key in enumerate(["g", "u"]):
                        li = idx * 2 + iw
                        slot = li % 3

                        @blk.sync
                        def _(sync, key=key, oc=oc, li=li, slot=slot):
                            b.wait(sync, "wr_" + key, prep_done[key])
                            if li >= 3:
                                b.wait(sync, "pe", wring_free[li - 3])
                            ins = sync.dma_start(
                                out=wring[slot][:].rearrange("p (a c) -> p a c", c=128),
                                in_=wT[key].ap()[0:D, oc * 128:(oc + 1) * 128]
                                    .rearrange("(a p) c -> p a c", p=128))
                            b.inc(ins, "fw%d" % slot, 16)
                    wv = [b.val("fw%d" % ((idx * 2 + iw) % 3)) for iw in range(2)]
                    par = idx % 2

                    @blk.tensor
                    def _(tensor, th=th, idx=idx, par=par, wv=wv,
                          th_gate_wait=th_gate_wait):
                        for iw in range(2):
                            b.wait(tensor, "fw%d" % ((idx * 2 + iw) % 3), wv[iw])
                        if idx % NHC == 0:
                            b.wait(tensor, "dv", th_gate_wait)
                        if idx >= 2:
                            b.wait(tensor, "dv", gu_free[idx - 2])
                        for iw in range(2):
                            slot = (idx * 2 + iw) % 3
                            bank = iw * 2 + par
                            for dc in range(NDC):
                                ins = tensor.matmul(
                                    gu_ps[:, bank * 512:(bank + 1) * 512],
                                    wring[slot][:, dc * 128:(dc + 1) * 128],
                                    lnf[:, dc * 1024 + th * 512:dc * 1024 + (th + 1) * 512],
                                    start=(dc == 0), stop=(dc == NDC - 1))
                            b.inc(ins, "pe")
                        wring_free.append(b.val("pe"))
                        wring_free.append(b.val("pe"))
                    gv = b.val("pe")

                    @blk.vector
                    def _(vector, th=th, par=par, gv=gv):
                        b.wait(vector, "pe", gv)
                        vector.tensor_sub(tg_sc[par][:],
                                          gu_ps[:, par * 512:(par + 1) * 512],
                                          cs_bc[:, th * 512:(th + 1) * 512])
                        ins = vector.tensor_sub(tu_sc[par][:],
                                                gu_ps[:, (2 + par) * 512:(3 + par) * 512],
                                                cs_bc[:, th * 512:(th + 1) * 512])
                        b.inc(ins, "dv")
                    sv = b.val("dv")

                    @blk.scalar
                    def _(scalar, oc=oc, par=par, sv=sv):
                        b.wait(scalar, "dv", sv)
                        ins = scalar.activation(gsig[par][:], tg_sc[par][:], AF.Silu,
                                                scale=sg_sb[:, oc:oc + 1])
                        b.inc(ins, "ac")
                    av = b.val("ac")

                    @blk.vector
                    def _(vector, oc=oc, par=par, av=av):
                        b.wait(vector, "ac", av)
                        ins = vector.scalar_tensor_tensor(
                            h_fm[:, oc * 512:(oc + 1) * 512], tu_sc[par][:],
                            su_sb[:, oc:oc + 1], gsig[par][:], ALU.mult, ALU.mult)
                        gu_free.append(b.inc(ins, "dv"))
                h_v = gu_free[-1]

                @blk.tensor
                def _(tensor, h_v=h_v):
                    b.wait(tensor, "dv", h_v)
                    for hc in range(NHC):
                        ins = tensor.matmul(gu_ps[0:1, 0:512], ones_col_sb[:],
                                            h_fm[:, hc * 512:(hc + 1) * 512],
                                            start=(hc == 0), stop=(hc == NHC - 1))
                    b.inc(ins, "pe")
                chv = b.val("pe")

                @blk.vector
                def _(vector, chv=chv):
                    b.wait(vector, "pe", chv)
                    ins = vector.tensor_scalar_mul(cshr[:], gu_ps[0:1, 0:512], 128.0)
                    b.inc(ins, "dv")
                chrv = b.val("dv")

                @blk.tensor
                def _(tensor, chrv=chrv):
                    b.wait(tensor, "dv", chrv)
                    ins = tensor.matmul(gu_ps[:, 512:1024], ones_row_sb[:], cshr[:],
                                        start=True, stop=True)
                    b.inc(ins, "pe")
                chbv = b.val("pe")

                @blk.vector
                def _(vector, chbv=chbv):
                    b.wait(vector, "pe", chbv)
                    ins = vector.tensor_copy(csh_bc[:], gu_ps[:, 512:1024])
                    b.inc(ins, "dv")
                cshbc_v = b.val("dv")
                th_gate_wait = cshbc_v

                HH = NHC // 2
                for dimc in range(NDC):
                    di = th * NDC + dimc
                    bank = di % 2
                    dwvs = []
                    for hf in range(2):
                        hi = di * 2 + hf

                        @blk.sync
                        def _(sync, dimc=dimc, hf=hf, hi=hi):
                            b.wait(sync, "wr_d", prep_done["d"])
                            if hi >= 2:
                                b.wait(sync, "pe", dring_free[hi - 2])
                            ins = sync.dma_start(
                                out=dring[hf][:].rearrange("p (a c) -> p a c", c=128),
                                in_=wT["d"].ap()[hf * HH * 128:(hf + 1) * HH * 128,
                                                 dimc * 128:(dimc + 1) * 128]
                                    .rearrange("(a p) c -> p a c", p=128))
                            b.inc(ins, "dw%d" % hf, 16)
                        dwvs.append(b.val("dw%d" % hf))

                    for hf in range(2):
                        @blk.tensor
                        def _(tensor, di=di, hf=hf, bank=bank, dwv=dwvs[hf],
                              cshbc_v=cshbc_v):
                            b.wait(tensor, "dw%d" % hf, dwv)
                            if di % NDC == 0 and hf == 0:
                                b.wait(tensor, "dv", cshbc_v)
                            if di >= 2 and hf == 0:
                                b.wait(tensor, "dv", dn_free[di - 2])
                            for hc2 in range(HH):
                                hc = hf * HH + hc2
                                ins = tensor.matmul(
                                    dn_ps[:, bank * 512:(bank + 1) * 512],
                                    dring[hf][:, hc2 * 128:(hc2 + 1) * 128],
                                    h_fm[:, hc * 512:(hc + 1) * 512],
                                    start=(hc == 0), stop=(hc == NHC - 1))
                            b.inc(ins, "pe")
                            dring_free.append(b.val("pe"))
                    dmv = b.val("pe")

                    @blk.vector
                    def _(vector, dimc=dimc, bank=bank, dmv=dmv):
                        b.wait(vector, "pe", dmv)
                        ins = vector.tensor_sub(dnf[:, bank * 1024:bank * 1024 + 512],
                                                dn_ps[:, bank * 512:(bank + 1) * 512],
                                                csh_bc[:])
                        b.inc(ins, "dv")
                        b.wait(vector, "dv", b.val("dv"))
                        ins = vector.tensor_scalar_mul(
                            dnf[:, bank * 1024 + 512:(bank + 1) * 1024],
                            dnf[:, bank * 1024:bank * 1024 + 512],
                            sd_sb[:, dimc:dimc + 1])
                        dn_free.append(b.inc(ins, "dv"))
                    dfv = b.val("dv")

                    for j in range(4):
                        tb = th * 4 + j
                        tpar = tstate["n"] % 2
                        tstate["n"] += 1

                        @blk.tensor
                        def _(tensor, j=j, tpar=tpar, dfv=dfv, bank=bank):
                            if len(tstate["hist"]) >= 2:
                                b.wait(tensor, "dv", tstate["hist"][-2])
                            if j == 0:
                                b.wait(tensor, "dv", dfv)
                            ins = tensor.matmul(
                                tp2[:, tpar * 512:tpar * 512 + 128],
                                dnf[:, bank * 1024 + 512 + j * 128:
                                    bank * 1024 + 512 + (j + 1) * 128],
                                ident_f_sb[:], start=True, stop=True)
                            b.inc(ins, "pe")
                        tpv = b.val("pe")

                        @blk.vector
                        def _(vector, tb=tb, dimc=dimc, tpar=tpar, tpv=tpv):
                            b.wait(vector, "pe", tpv)
                            ins = vector.tensor_add(
                                x_res[:, tb * D + dimc * 128:tb * D + (dimc + 1) * 128],
                                x_res[:, tb * D + dimc * 128:tb * D + (dimc + 1) * 128],
                                tp2[:, tpar * 512:tpar * 512 + 128])
                            tstate["hist"].append(b.inc(ins, "dv"))
            ps.close()

        def emit_attn_b():
            psc = contextlib.ExitStack()
            sb = lambda n, s, d=BF16: sball(n, s, d, psc)
            tpB = psall("tpB", [128, 1024], psc)
            pjB = psall("pjB", [128, 2048], psc)
            smB = psall("smB", [128, 1024], psc)

            qp_fm = sball("qp_fm", [128, NDC * 1024], BF16, psc)
            at_fm = sball("at_fm", [128, NDC * 512], BF16, psc)
            ps1b = contextlib.ExitStack()
            kp_fm = sball("kp_fm", [128, NDC * 1024], BF16, ps1b)
            v_fm = sball("bv_fm", [128, NDC * 1024], BF16, ps1b)
            ps1 = contextlib.ExitStack()
            lnt = [sball("blnt0", [128, D], BF16, ps1)]
            lnf = sball("blnf", [128, NDC * 512], BF16, ps1)
            t_el = [sball("t_el0", [128, 512], BF16, ps1)]
            wringB = [sball(f"wringB{i}", [128, NDC * 128], BF16, ps1) for i in range(2)]

            lnt_free = [0]
            wring_free = []
            pj_free2 = []
            el_ac = []
            b_slab_done = [0]
            for sl in range(2):
                for tt in range(4):
                    tb = sl * 4 + tt

                    @blk.vector
                    def _(vector):
                        b.wait(vector, "pe", lnt_free[0])
                    lv = ln_tile(x_res[:, tb * D:(tb + 1) * D], lnt[0], tb % 2,
                                 lnt[0], scalar_gate=lnt_free[0])
                    transpose_blocks(
                        tpB,
                        [(lnt[0], dc * 128) for dc in range(NDC)],
                        [(lnf, dc * 512 + tt * 128) for dc in range(NDC)],
                        pe_waits=[("dv", lv), ("pe", b_slab_done[0])],
                    )
                    lnt_free[0] = b.val("pe")
                lnf_v = b.val("dv")

                for iw, (key, bias_t) in enumerate([("qwb", qb_sb), ("kwb", kb_sb),
                                                    ("vwb", vb_sb)]):
                    dst = [qp_fm, kp_fm, v_fm][iw]
                    for oc in range(NDC):
                        li = len(wring_free)
                        slot = li % 2

                        @blk.sync
                        def _(sync, key=key, oc=oc, li=li, slot=slot):
                            b.wait(sync, "wr_" + key, prep_done[key])
                            if li >= 2:
                                b.wait(sync, "pe", wring_free[li - 2])
                            ins = sync.dma_start(
                                out=wringB[slot][:].rearrange("p (a c) -> p a c", c=128),
                                in_=wT[key].ap()[0:D, oc * 128:(oc + 1) * 128]
                                    .rearrange("(a p) c -> p a c", p=128))
                            b.inc(ins, "bw%d" % slot, 16)
                        wv = b.val("bw%d" % slot)
                        gi = len(pj_free2)
                        bank = gi % 4

                        @blk.tensor
                        def _(tensor, slot=slot, gi=gi, bank=bank, wv=wv,
                              lnf_v=lnf_v):
                            b.wait(tensor, "bw%d" % slot, wv)
                            b.wait(tensor, "dv", lnf_v)
                            if gi >= 4:
                                b.wait(tensor, "dv", pj_free2[gi - 4])
                            for dc in range(NDC):
                                ins = tensor.matmul(
                                    pjB[:, bank * 512:(bank + 1) * 512],
                                    wringB[slot][:, dc * 128:(dc + 1) * 128],
                                    lnf[:, dc * 512:(dc + 1) * 512],
                                    start=(dc == 0), stop=(dc == NDC - 1))
                            b.inc(ins, "pe")
                        wring_free.append(b.val("pe"))
                        mv = b.val("pe")

                        if iw < 2:
                            dslice = dst[:, oc * 1024 + sl * 512:oc * 1024 + (sl + 1) * 512]

                            @blk.vector
                            def _(vector, oc=oc, bank=bank, mv=mv, bias_t=bias_t):
                                b.wait(vector, "pe", mv)
                                if len(el_ac) >= 1:
                                    b.wait(vector, "ac", el_ac[-1])
                                if pj_free2:
                                    b.wait(vector, "dv", pj_free2[-1])
                                ins = vector.tensor_scalar(
                                    t_el[0][:], pjB[:, bank * 512:(bank + 1) * 512],
                                    bias_t[:, oc:oc + 1], 0.0, ALU.add, ALU.min)
                                b.inc(ins, "dv")
                            elv = b.val("dv")

                            @blk.scalar
                            def _(scalar, dslice=dslice, elv=elv):
                                b.wait(scalar, "dv", elv)
                                ins = scalar.activation(dslice, t_el[0][:], AF.Exp)
                                el_ac.append(b.inc(ins, "ac"))
                            eav = b.val("ac")

                            @blk.vector
                            def _(vector, dslice=dslice, oc=oc, bank=bank,
                                  eav=eav, bias_t=bias_t):
                                b.wait(vector, "ac", eav)
                                ins = vector.tensor_scalar(
                                    t_el[0][:], pjB[:, bank * 512:(bank + 1) * 512],
                                    bias_t[:, oc:oc + 1], 0.0, ALU.add, ALU.max)
                                b.inc(ins, "dv")
                                b.wait(vector, "dv", b.val("dv"))
                                ins = vector.tensor_add(dslice, dslice, t_el[0][:])
                                pj_free2.append(b.inc(ins, "dv"))
                        else:
                            @blk.vector
                            def _(vector, dst=dst, oc=oc, sl=sl, bank=bank, mv=mv,
                                  bias_t=bias_t):
                                b.wait(vector, "pe", mv)
                                ins = vector.tensor_scalar(
                                    dst[:, oc * 1024 + sl * 512:oc * 1024 + (sl + 1) * 512],
                                    pjB[:, bank * 512:(bank + 1) * 512],
                                    bias_t[:, oc:oc + 1], None, ALU.add)
                                pj_free2.append(b.inc(ins, "dv"))

                b_slab_done[0] = b.val("pe")
            proj_v = b.val("dv")

            ps1.close()
            km_tm = sball("km_tm", [128, 8 * 128], BF16, ps1b)
            vm_tm = sball("vm_tm", [128, 8 * 128], BF16, ps1b)
            kvt = sball("kvt", [128, 128], F32, ps1b)
            ksrow = sball("ksrow", [1, 128], F32, ps1b)
            kv_wr = []
            for h in range(NH):
                tv = transpose_blocks(
                    tpB,
                    [(kp_fm, h * 1024 + j * 128) for j in range(8)]
                    + [(v_fm, h * 1024 + j * 128) for j in range(8)],
                    [(km_tm, j * 128) for j in range(8)]
                    + [(vm_tm, j * 128) for j in range(8)],
                    pe_waits=[("dv", proj_v)] + ([("kvw", kv_wr[-1])] if h else []),
                )

                @blk.tensor
                def _(tensor, h=h, tv=tv):
                    b.wait(tensor, "dv", tv)
                    if h >= 1:
                        b.wait(tensor, "dv", kv_ev[-1])
                    for j in range(8):
                        tensor.matmul(smB[:, 0:128], km_tm[:, j * 128:(j + 1) * 128],
                                      vm_tm[:, j * 128:(j + 1) * 128],
                                      start=(j == 0), stop=(j == 7))
                    for j in range(8):
                        ins = tensor.matmul(smB[0:1, 512:640], ones_col_sb[:],
                                            km_tm[:, j * 128:(j + 1) * 128],
                                            start=(j == 0), stop=(j == 7))
                    b.inc(ins, "pe")
                kvv = b.val("pe")

                @blk.vector
                def _(vector, kvv=kvv):
                    b.wait(vector, "pe", kvv)
                    vector.tensor_copy(kvt[:], smB[:, 0:128])
                    ins = vector.tensor_copy(ksrow[:], smB[0:1, 512:640])
                    kv_ev.append(b.inc(ins, "dv"))
                kev = kv_ev[-1]

                @blk.gpsimd
                def _(gpsimd, h=h, kev=kev):
                    b.wait(gpsimd, "dv", kev)
                    ins = gpsimd.dma_start(out=ar_in.ap()[h * 128:(h + 1) * 128, :],
                                           in_=kvt[:])
                    b.inc(ins, "kvw", 16)
                    ins = gpsimd.dma_start(
                        out=ar_in.ap()[NH * DH + h:NH * DH + h + 1, :], in_=ksrow[:])
                    kv_wr.append(b.inc(ins, "kvw", 16))
            ps1b.close()

            @blk.gpsimd
            def _(gpsimd):
                b.wait(gpsimd, "kvw", kv_wr[-1])
                ins = gpsimd.collective_compute(
                    "AllReduce", ALU.add, RG, [ar_in.ap().opt()], [ar_out.ap().opt()])
                b.inc(ins, "coll")
            coll_ar = b.val("coll")

            ps2 = contextlib.ExitStack()
            kvb = sball("kvb", [128, NH * 128], BF16, ps2)
            zrow_sb = sball("zrow_sb", [1, 512], F32, ps2)
            zb_sb = sball("zb_sb", [128, 512], F32, ps2)
            ksf = sball("ksf", [128, NH], F32, ps2)
            ksb = sball("ksb", [128, NH], BF16, ps2)

            @blk.gpsimd
            def _(gpsimd):
                b.wait(gpsimd, "coll", coll_ar)
                ins = gpsimd.dma_start(
                    out=kvb[:].rearrange("p (a c) -> p a c", c=128),
                    in_=ar_out.ap()[0:NH * DH, :].rearrange("(a p) c -> p a c", p=128))
                b.inc(ins, "kvl", 16)
                ins = gpsimd.dma_start(
                    out=ksf[:],
                    in_=bass.AP(ar_out.ap().tensor, (NH * DH) * DH, [[1, 128], [128, NH]]),
                    allow_slow_non_contiguous=True)
                b.inc(ins, "kvl", 16)
            kv_ld = b.val("kvl")

            @blk.vector
            def _(vector, kv_ld=kv_ld):
                b.wait(vector, "kvl", kv_ld)
                ins = vector.tensor_copy(ksb[:], ksf[:])
                b.inc(ins, "dv")
            ksb_v = b.val("dv")

            ps3 = contextlib.ExitStack()
            owT_b = sball("owT_b", [128, NDC * D], BF16, ps3)
            ob_row = sball("ob_row", [1, D], F32, ps3)
            ob_bc = sball("ob_bc", [128, D], F32, ps3)
            t_ep = sball("t_ep", [128, 512], F32, ps3)

            @blk.sync
            def _(sync):
                b.wait(sync, "wr_owb", prep_done["owb"])
                b.wait(sync, "kvw", kv_wr[-1])
                for ec in range(NDC):
                    ins = sync.dma_start(out=owT_b[:, ec * D:(ec + 1) * D],
                                         in_=wT["owb"].ap()[ec * 128:(ec + 1) * 128, 0:D])
                    b.inc(ins, "oww", 16)
                ins = sync.dma_start(out=ob_row[:],
                                     in_=bass.AP(EXT["ob_b"].tensor, 0, [[D, 1], [1, D]]))
                b.inc(ins, "oww", 16)
            ow_v = b.val("oww")

            @blk.tensor
            def _(tensor, ow_v=ow_v):
                b.wait(tensor, "oww", ow_v)
                for j in range(4):
                    ins = tensor.matmul(pjB[:, j * 512:(j + 1) * 512], ones_row_sb[:],
                                        ob_row[0:1, j * 512:(j + 1) * 512],
                                        start=True, stop=True)
                b.inc(ins, "pe")
            obv = b.val("pe")

            @blk.vector
            def _(vector, obv=obv):
                b.wait(vector, "pe", obv)
                ins = vector.tensor_copy(ob_bc[:], pjB[:, 0:2048])
                b.inc(ins, "dv")
            ob_v = b.val("dv")

            op_free2 = []
            at_ev = []
            for th in range(2):
                for h in range(NH):
                    @blk.tensor
                    def _(tensor, h=h, th=th, ksb_v=ksb_v):
                        if th == 0 and h == 0:
                            b.wait(tensor, "dv", ksb_v)
                        if at_ev:
                            b.wait(tensor, "dv", at_ev[-1])
                        ins = tensor.matmul(
                            smB[0:1, 0:512], ksb[:, h:h + 1],
                            qp_fm[:, h * 1024 + th * 512:h * 1024 + (th + 1) * 512],
                            start=True, stop=True)
                        b.inc(ins, "pe")
                    zv = b.val("pe")

                    @blk.vector
                    def _(vector, zv=zv):
                        b.wait(vector, "pe", zv)
                        ins = vector.tensor_scalar_add(zrow_sb[:], smB[0:1, 0:512], Z_EPS)
                        b.inc(ins, "dv")
                        b.wait(vector, "dv", b.val("dv"))
                        ins = vector.reciprocal(zrow_sb[:], zrow_sb[:])
                        b.inc(ins, "dv")
                    zrv = b.val("dv")

                    @blk.tensor
                    def _(tensor, zrv=zrv):
                        b.wait(tensor, "dv", zrv)
                        ins = tensor.matmul(smB[:, 512:1024], ones_row_sb[:], zrow_sb[:],
                                            start=True, stop=True)
                        b.inc(ins, "pe")
                    zbv = b.val("pe")

                    @blk.vector
                    def _(vector, h=h, th=th, zbv=zbv):
                        b.wait(vector, "pe", zbv)
                        ins = vector.tensor_copy(zb_sb[:], smB[:, 512:1024])
                        b.inc(ins, "dv")
                        b.wait(vector, "dv", b.val("dv"))
                        ins = vector.tensor_mul(
                            qp_fm[:, h * 1024 + th * 512:h * 1024 + (th + 1) * 512],
                            qp_fm[:, h * 1024 + th * 512:h * 1024 + (th + 1) * 512],
                            zb_sb[:])
                        b.inc(ins, "dv")
                    qsc = b.val("dv")

                    @blk.tensor
                    def _(tensor, h=h, th=th, qsc=qsc):
                        b.wait(tensor, "dv", qsc)
                        ins = tensor.matmul(
                            smB[0:128, 0:512], kvb[:, h * 128:(h + 1) * 128],
                            qp_fm[:, h * 1024 + th * 512:h * 1024 + (th + 1) * 512],
                            start=True, stop=True)
                        b.inc(ins, "pe")
                    nv = b.val("pe")

                    @blk.vector
                    def _(vector, h=h, nv=nv, th=th):
                        b.wait(vector, "pe", nv)
                        if th > 0 and len(op_free2) < 4 * th and h == 0:
                            pass
                        ins = vector.tensor_copy(at_fm[:, h * 512:(h + 1) * 512],
                                                 smB[0:128, 0:512])
                        at_ev.append(b.inc(ins, "dv"))
                at_v = at_ev[-1]

                for j in range(4):
                    tb = th * 4 + j

                    @blk.tensor
                    def _(tensor, tb=tb, j=j, th=th, at_v=at_v, ob_v=ob_v):
                        if j == 0:
                            b.wait(tensor, "dv", at_v)
                            b.wait(tensor, "dv", ob_v)
                        if len(op_free2) >= 1:
                            b.wait(tensor, "dv", op_free2[-1])
                        for ec in range(NDC):
                            for ods in range(4):
                                ins = tensor.matmul(
                                    pjB[:, ods * 512:(ods + 1) * 512],
                                    at_fm[:, ec * 512 + j * 128:ec * 512 + (j + 1) * 128],
                                    owT_b[:, ec * D + ods * 512:ec * D + (ods + 1) * 512],
                                    start=(ec == 0), stop=(ec == NDC - 1))
                        b.inc(ins, "pe")
                    ov = b.val("pe")

                    @blk.vector
                    def _(vector, tb=tb, ov=ov):
                        b.wait(vector, "pe", ov)
                        for ods in range(4):
                            if ods > 0:
                                b.wait(vector, "dv", b.val("dv"))
                            ins = vector.tensor_add(t_ep[:],
                                                    pjB[:, ods * 512:(ods + 1) * 512],
                                                    ob_bc[:, ods * 512:(ods + 1) * 512])
                            b.inc(ins, "dv")
                            b.wait(vector, "dv", b.val("dv"))
                            ins = vector.tensor_add(
                                x_res[:, tb * D + ods * 512:tb * D + (ods + 1) * 512],
                                x_res[:, tb * D + ods * 512:tb * D + (ods + 1) * 512],
                                t_ep[:])
                            b.inc(ins, "dv")
                        op_free2.append(b.val("dv"))
            ps3.close()
            ps2.close()
            psc.close()

        kv_ev = []

        if STAGE >= 2:
            barrier(dma_waits=preps_all)
            emit_ffn("F1")
            emit_dbg("x2")
        if STAGE >= 3:
            barrier()
            emit_attn_b()
            emit_dbg("x3")
        if STAGE >= 4:
            barrier()
            emit_ffn("F2")

        fin = b.val("dv")

        @blk.gpsimd
        def _(gpsimd):
            b.wait(gpsimd, "dv", fin)
            for tb in range(NTB):
                ins = gpsimd.dma_start(out=Y[tb * 128:(tb + 1) * 128, :],
                                       in_=x_res[:, tb * D:(tb + 1) * D])
                b.inc(ins, "st", 16)
            b.wait(gpsimd, "st", b.val("st"))

    stack.close()
    return nc, dbg_outs


def _in_maps(inputs):
    x = np.ascontiguousarray(inputs["L0_attn_a_x"])
    maps = []
    for c in range(8):
        bb, gg = c // 4, c % 4
        h0 = 4 * gg
        maps.append({
            "x0": np.ascontiguousarray(x[bb]),
            "x0m": np.ascontiguousarray(x[bb, gg * TOK:(gg + 1) * TOK]),
            "wq_a": np.ascontiguousarray(inputs["L0_attn_a_Wq"][h0 * DH:(h0 + 4) * DH]),
            "wk_a": np.ascontiguousarray(inputs["L0_attn_a_Wk"][h0 * DH:(h0 + 4) * DH]),
            "wv_a": np.ascontiguousarray(inputs["L0_attn_a_Wv"][h0 * DH:(h0 + 4) * DH]),
            "wo_a": np.ascontiguousarray(inputs["L0_attn_a_Wo"]),
            "qw_b": np.ascontiguousarray(inputs["L0_attn_b_q_w"]),
            "kw_b": np.ascontiguousarray(inputs["L0_attn_b_k_w"]),
            "vw_b": np.ascontiguousarray(inputs["L0_attn_b_v_w"]),
            "ow_b": np.ascontiguousarray(inputs["L0_attn_b_o_w"]),
            "qb_b": np.ascontiguousarray(inputs["L0_attn_b_q_b"]),
            "kb_b": np.ascontiguousarray(inputs["L0_attn_b_k_b"]),
            "vb_b": np.ascontiguousarray(inputs["L0_attn_b_v_b"]),
            "ob_b": np.ascontiguousarray(inputs["L0_attn_b_o_b"]),
            "wg_i": np.ascontiguousarray(inputs["L0_ffn_wq_gate"]),
            "wu_i": np.ascontiguousarray(inputs["L0_ffn_wq_up"]),
            "wd_i": np.ascontiguousarray(inputs["L0_ffn_wq_down"]),
            "sg": np.ascontiguousarray(inputs["L0_ffn_s_gate"]),
            "su": np.ascontiguousarray(inputs["L0_ffn_s_up"]),
            "sd": np.ascontiguousarray(inputs["L0_ffn_s_down"]),
        })
    return maps


def kernel(**inputs):
    nc, _ = build_core_program(0)
    res = run_bass_kernel_spmd(nc, _in_maps(inputs), core_ids=list(range(8)))
    out = np.empty((B_, T, D), dtype=np.float32)
    for c in range(8):
        bb, gg = c // 4, c % 4
        out[bb, gg * TOK:(gg + 1) * TOK, :] = res.results[c]["y"]
    return out


# version 02
# speedup vs baseline: 1.0523x; 1.0523x over previous
import contextlib
import numpy as np
import ml_dtypes
import concourse.bass as bass
import concourse.mybir as mybir
from concourse.bass_utils import run_bass_kernel_spmd

F32 = mybir.dt.float32
BF16 = mybir.dt.bfloat16
I32 = mybir.dt.int32
AF = mybir.ActivationFunctionType
ALU = mybir.AluOpType
AX = mybir.AxisListType

B_, T, D = 2, 4096, 2048
HID = 5632
NH, DH = 16, 128
TOK = 1024
NTB = TOK // 128
NDC = D // 128
NHC = HID // 128
NP = 4
LN_EPS = 1e-5
Z_EPS = 1e-6

STAGE = 4
DBG = ()


class Bld:
    def __init__(self, nc, stack):
        self.nc = nc
        self.stack = stack
        self.cnt = {}
        self.sems = {}

    def sem(self, name):
        if name not in self.sems:
            self.sems[name] = self.stack.enter_context(self.nc.semaphore(name))
            self.cnt[name] = 0
        return self.sems[name]

    def inc(self, ins, name, n=1):
        ins.then_inc(self.sem(name), n)
        self.cnt[name] += n
        return self.cnt[name]

    def val(self, name):
        self.sem(name)
        return self.cnt[name]

    def wait(self, eng, name, v):
        assert 0 <= v <= self.cnt.get(name, 0), ("wait-ahead", name, v, self.cnt.get(name))
        if v > 0:
            eng.wait_ge(self.sem(name), v)


def build_core_program(core):
    nc = bass.Bass("TRN2", target_bir_lowering=False)
    g = core % 4
    to0 = g * TOK
    RG = [[0, 1, 2, 3], [4, 5, 6, 7]]

    X0 = nc.dram_tensor("x0", [T, D], F32, kind="ExternalInput").ap()
    X0M = nc.dram_tensor("x0m", [TOK, D], F32, kind="ExternalInput").ap()
    EXT = {}
    for nm, sh, dt in [("wq_a", [512, D], F32), ("wk_a", [512, D], F32),
                       ("wv_a", [512, D], F32), ("wo_a", [D, D], F32),
                       ("qw_b", [D, D], F32), ("kw_b", [D, D], F32),
                       ("vw_b", [D, D], F32), ("ow_b", [D, D], F32),
                       ("qb_b", [D], F32), ("kb_b", [D], F32), ("vb_b", [D], F32),
                       ("ob_b", [D], F32), ("wg_i", [HID, D], I32),
                       ("wu_i", [HID, D], I32), ("wd_i", [D, HID], I32),
                       ("sg", [HID], F32), ("su", [HID], F32), ("sd", [D], F32)]:
        EXT[nm] = nc.dram_tensor(nm, sh, dt, kind="ExternalInput").ap()
    Y = nc.dram_tensor("y", [TOK, D], F32, kind="ExternalOutput").ap()

    def dram(name, shape, dt=BF16):
        return nc.dram_tensor(name, shape, dt, kind="Internal")

    CAST_SRC = {"wqa": "wq_a", "wka": "wk_a", "wva": "wv_a", "woa": "wo_a",
                "g": "wg_i", "u": "wu_i", "d": "wd_i",
                "qwb": "qw_b", "kwb": "kw_b", "vwb": "vw_b", "owb": "ow_b"}
    WSHAPE = {"wqa": [512, D], "wka": [512, D], "wva": [512, D], "woa": [D, D],
              "g": [HID, D], "u": [HID, D], "d": [D, HID],
              "qwb": [D, D], "kwb": [D, D], "vwb": [D, D], "owb": [D, D]}
    wbf = {k: dram(k + "_bf", s) for k, s in WSHAPE.items()}
    wT = {k: dram(k + "_T", [s[1], s[0]]) for k, s in WSHAPE.items() if k != "woa"}
    ag_in = dram("ag_in", [NP * DH, T])
    ag_out = dram("ag_out", [NH * DH, T])
    ar_in = dram("ar_in", [NH * DH + NH, DH], F32)
    ar_out = dram("ar_out", [NH * DH + NH, DH], F32)

    ident_bf = nc.inline_tensor(np.eye(128, dtype=ml_dtypes.bfloat16), "c_ident_bf")
    ident_f = nc.inline_tensor(np.eye(128, dtype=np.float32), "c_ident_f")
    ones_row_f = nc.inline_tensor(np.ones((1, 128), dtype=np.float32), "c_ones_row")
    row128_f = nc.inline_tensor(np.full((1, 128), 128.0, dtype=np.float32), "c_row128")
    one11_f = nc.inline_tensor(np.ones((1, 1), dtype=np.float32), "c_one11")
    ones_col_bf = nc.inline_tensor(np.ones((128, 1), dtype=ml_dtypes.bfloat16), "c_ones_col")
    eps_col_f = nc.inline_tensor(np.full((128, 1), LN_EPS, dtype=np.float32), "c_eps_col")

    stack = contextlib.ExitStack()
    b = Bld(nc, stack)
    bko_ev = []

    def sball(name, shape, dtype=BF16, st=None):
        return (st or stack).enter_context(nc.sbuf_tensor(name, shape, dtype))

    def psall(name, shape, st=None, dtype=F32):
        return (st or stack).enter_context(nc.psum_tensor(name, shape, dtype))

    ident_bf_sb = sball("ident_bf_sb", [128, 128])
    ident_f_sb = sball("ident_f_sb", [128, 128], F32)
    ones_row_sb = sball("ones_row_sb", [1, 128], F32)
    row128_sb = sball("row128_sb", [1, 128], F32)
    one11_sb = sball("one11_sb", [1, 1], F32)
    ones_col_sb = sball("ones_col_sb", [128, 1])
    eps_sb = sball("eps_sb", [128, 1], F32)
    qb_sb = sball("qb_sb", [128, 16], F32)
    kb_sb = sball("kb_sb", [128, 16], F32)
    vb_sb = sball("vb_sb", [128, 16], F32)
    sg_sb = sball("sg_sb", [128, NHC], F32)
    su_sb = sball("su_sb", [128, NHC], F32)
    sd_sb = sball("sd_sb", [128, 16], F32)
    st_a = sball("st_a", [128, 8], F32)
    WSEG = HID // 2
    wstg_st = contextlib.ExitStack()
    wstg = [sball(f"wstg{i}", [128, WSEG], BF16, wstg_st) for i in range(2)]

    dbg_outs = {}

    with nc.Block() as blk:
        @blk.sync
        def _(sync):
            for dst, src in [(ident_bf_sb, ident_bf), (ident_f_sb, ident_f),
                             (ones_row_sb, ones_row_f), (row128_sb, row128_f),
                             (one11_sb, one11_f), (ones_col_sb, ones_col_bf),
                             (eps_sb, eps_col_f)]:
                ins = sync.dma_start(out=dst[:], in_=src.ap())
                b.inc(ins, "ko", 16)
        konst_v = b.val("ko")
        bs_ps = contextlib.ExitStack()
        btmp = sball("btmp", [NHC, 128], F32, bs_ps)
        b_ps = psall("b_ps", [128, 64], bs_ps)
        for i, (dst, srcnm, n) in enumerate(
                [(qb_sb, "qb_b", 16), (kb_sb, "kb_b", 16), (vb_sb, "vb_b", 16),
                 (sg_sb, "sg", NHC), (su_sb, "su", NHC), (sd_sb, "sd", 16)]):
            @blk.scalar
            def _(scalar, srcnm=srcnm, n=n, i=i):
                if i >= 1:
                    b.wait(scalar, "dv", bko_ev[-1])
                ins = scalar.dma_start(
                    out=btmp[0:n, :],
                    in_=EXT[srcnm].rearrange("(a c) -> a c", c=128))
                b.inc(ins, "bko", 16)
            bv = b.val("bko")

            @blk.tensor
            def _(tensor, n=n, bv=bv, i=i):
                b.wait(tensor, "ko", konst_v)
                b.wait(tensor, "bko", bv)
                if i >= 1:
                    b.wait(tensor, "dv", bko_ev[-1])
                ins = tensor.transpose(b_ps[:, 0:n], btmp[0:n, :], ident_f_sb[0:n, 0:n])
                b.inc(ins, "pe")
            tv = b.val("pe")

            @blk.vector
            def _(vector, dst=dst, n=n, tv=tv):
                b.wait(vector, "pe", tv)
                ins = vector.tensor_copy(dst[:, 0:n], b_ps[:, 0:n])
                bko_ev.append(b.inc(ins, "dv"))
        bs_ps.close()

        cast_done = {}

        @blk.gpsimd
        def _(gpsimd):
            for key in ["wqa", "wka", "wva", "g", "woa", "u", "d",
                        "qwb", "kwb", "vwb", "owb"]:
                ins = gpsimd.dma_start(out=wbf[key].ap(), in_=EXT[CAST_SRC[key]])
                cast_done[key] = b.inc(ins, "ca_" + key, 16)

        for engname in ("tensor", "vector", "scalar"):
            @getattr(blk, engname)
            def _(eng):
                b.wait(eng, "ko", konst_v)

        prep_done = {}
        wr_hist = []

        def prep_wT(key, prev_key):
            rows, cols = WSHAPE[key]
            n = cols // 128
            segs = [(s, min(WSEG, rows - s)) for s in range(0, rows, WSEG)]
            ci = [0]
            for i in range(n):
                for r0, rn in segs:
                    j = ci[0]
                    ci[0] += 1

                    wsem = "wr%d_" % (j % 2) + key

                    @blk.sync
                    def _(sync, key=key, i=i, j=j, r0=r0, rn=rn, prev_key=prev_key,
                          wsem=wsem):
                        if j == 0:
                            b.wait(sync, "ca_" + key, 16)
                        if j <= 1 and prev_key is not None:
                            for par in range(2):
                                nm = "wr%d_" % par + prev_key
                                b.wait(sync, nm, b.val(nm))
                        if j >= 2:
                            b.wait(sync, wsem, b.val(wsem))
                        ins = sync.dma_start(
                            out=wstg[j % 2][:, 0:rn],
                            in_=wbf[key].ap()[r0:r0 + rn, i * 128:(i + 1) * 128],
                            transpose=True)
                        b.inc(ins, "wp%d" % (j % 2), 16)
                    sv = b.val("wp%d" % (j % 2))

                    @blk.gpsimd
                    def _(gpsimd, key=key, i=i, j=j, r0=r0, rn=rn, sv=sv, wsem=wsem):
                        b.wait(gpsimd, "wp%d" % (j % 2), sv)
                        ins = gpsimd.dma_start(
                            out=wT[key].ap()[i * 128:(i + 1) * 128, r0:r0 + rn],
                            in_=wstg[j % 2][:, 0:rn])
                        b.inc(ins, wsem, 16)
            prep_done[key] = [("wr%d_" % par + key, b.val("wr%d_" % par + key))
                              for par in range(2)]

        for key, pk in zip(["wqa", "wka", "wva"], [None, "wqa", "wka"]):
            prep_wT(key, pk)

        def ln_tile(x_ap, out_tile, par, sqt, colsum_to=None, scalar_gate=0,
                    x_gate=None):
            c = par * 4

            @blk.vector
            def _(vector):
                ins = vector.reduce_sum(st_a[:, c:c + 1], x_ap, axis=AX.X)
                b.inc(ins, "dv")

            @blk.scalar
            def _(scalar):
                b.wait(scalar, "pe", scalar_gate)
                if x_gate is not None:
                    b.wait(scalar, x_gate[0], x_gate[1])
                ins = scalar.activation(sqt[:], x_ap, AF.Square,
                                        accum_out=st_a[:, c + 1:c + 2])
                b.inc(ins, "ac")
            v0, a0 = b.val("dv"), b.val("ac")

            @blk.vector
            def _(vector):
                b.wait(vector, "dv", v0)
                b.wait(vector, "ac", a0)
                ins = vector.tensor_scalar_mul(st_a[:, c:c + 1], st_a[:, c:c + 1], 1.0 / D)
                b.inc(ins, "dv")
                b.wait(vector, "dv", b.val("dv"))
                ins = vector.tensor_mul(st_a[:, c + 2:c + 3], st_a[:, c:c + 1],
                                        st_a[:, c:c + 1])
                b.inc(ins, "dv")
                b.wait(vector, "dv", b.val("dv"))
                ins = vector.tensor_scalar(st_a[:, c + 1:c + 2], st_a[:, c + 1:c + 2],
                                           1.0 / D, None, ALU.mult)
                b.inc(ins, "dv")
                b.wait(vector, "dv", b.val("dv"))
                ins = vector.tensor_sub(st_a[:, c + 1:c + 2], st_a[:, c + 1:c + 2],
                                        st_a[:, c + 2:c + 3])
                b.inc(ins, "dv")
            v1 = b.val("dv")

            @blk.scalar
            def _(scalar):
                b.wait(scalar, "dv", v1)
                ins = scalar.activation(st_a[:, c + 2:c + 3], st_a[:, c + 1:c + 2],
                                        AF.Ln, bias=eps_sb[:, 0:1])
                b.inc(ins, "ac")
                b.wait(scalar, "ac", b.val("ac"))
                ins = scalar.activation(st_a[:, c + 3:c + 4], st_a[:, c + 2:c + 3],
                                        AF.Exp, scale=-0.5)
                b.inc(ins, "ac")
            a1 = b.val("ac")

            @blk.vector
            def _(vector):
                b.wait(vector, "ac", a1)
                ins = vector.tensor_scalar(out_tile[:], x_ap, st_a[:, c:c + 1],
                                           st_a[:, c + 3:c + 4], ALU.subtract, ALU.mult)
                b.inc(ins, "dv")
                if colsum_to is not None:
                    t, cc = colsum_to
                    b.wait(vector, "dv", b.val("dv"))
                    ins = vector.reduce_sum(t[:, cc:cc + 1], out_tile[:], axis=AX.X)
                    b.inc(ins, "dv")
            return b.val("dv")

        tstate = {"n": 0, "hist": []}

        def transpose_blocks(tp_ps, srcs, dsts, pe_waits=(), f32=False):
            ident = ident_f_sb if f32 else ident_bf_sb
            for i, ((s_t, s_c), (d_t, d_c)) in enumerate(zip(srcs, dsts)):
                par = tstate["n"] % 2
                tstate["n"] += 1

                @blk.tensor
                def _(tensor, s_t=s_t, s_c=s_c, par=par, i=i):
                    for nm, v in (pe_waits if i == 0 else ()):
                        b.wait(tensor, nm, v)
                    if len(tstate["hist"]) >= 2:
                        b.wait(tensor, "dv", tstate["hist"][-2])
                    ins = tensor.matmul(tp_ps[:, par * 512:par * 512 + 128],
                                        s_t[:, s_c:s_c + 128], ident[:],
                                        start=True, stop=True)
                    b.inc(ins, "pe")
                pv = b.val("pe")

                @blk.vector
                def _(vector, d_t=d_t, d_c=d_c, par=par, pv=pv):
                    b.wait(vector, "pe", pv)
                    ins = vector.tensor_copy(d_t[:, d_c:d_c + 128],
                                             tp_ps[:, par * 512:par * 512 + 128])
                    tstate["hist"].append(b.inc(ins, "dv"))
            return b.val("dv")

        def barrier(dma_waits=()):
            for engname in ("tensor", "vector", "scalar"):
                @getattr(blk, engname)
                def _(eng):
                    ins = eng.nop()
                    b.inc(ins, "bar")
            t = b.val("bar")
            for engname in ("tensor", "vector", "scalar", "sync", "gpsimd"):
                @getattr(blk, engname)
                def _(eng):
                    b.wait(eng, "bar", t)
                    for nm, v in dma_waits:
                        b.wait(eng, nm, v)

        def emit_dbg(name):
            if name not in DBG:
                return
            out = nc.dram_tensor("dbg_" + name, [TOK, D], F32, kind="ExternalOutput").ap()
            dbg_outs[name] = out
            wv = b.val("dv")

            @blk.gpsimd
            def _(gpsimd):
                b.wait(gpsimd, "dv", wv)
                for tb in range(NTB):
                    ins = gpsimd.dma_start(out=out[tb * 128:(tb + 1) * 128, :],
                                           in_=x_res[:, tb * D:(tb + 1) * D])
                    b.inc(ins, "dbg", 16)

        pa = contextlib.ExitStack()
        q_fm = sball("q_fm", [128, NP * T], BF16, pa)
        k_fm = sball("k_fm", [128, NP * T], BF16, pa)
        v_tm = sball("v_tm", [128, NP * 32 * 128], BF16, pa)
        tp_ps = psall("tp_psA", [128, 1024], pa)

        pa1 = contextlib.ExitStack()
        lnx_fm = sball("lnx_fm", [128, NDC * 512], BF16, pa1)
        v_fms = sball("v_fms", [128, NP * 512], BF16, pa1)
        x0t = [sball(f"x0t{i}", [128, D], F32, pa1) for i in range(2)]
        sqt_a = sball("sqt_a", [128, D], BF16, pa1)
        lnq_tm = [sball(f"lnq{i}", [128, D], BF16, pa1) for i in range(2)]
        aring = [sball(f"aring{i}", [128, NDC * 128], BF16, pa1) for i in range(2)]
        pj_ps = psall("pj_psA", [128, 2048], pa1)

        x0_free = [0, 0]
        lnq_free = [0, 0]
        aring_free = []
        pj_free = []
        a2_done = [0]

        for sl in range(8):
            for tt in range(4):
                tile = sl * 4 + tt
                par = tile % 2

                @blk.scalar
                def _(scalar, tile=tile, par=par):
                    b.wait(scalar, "dv", x0_free[par])
                    ins = scalar.dma_start(out=x0t[par][:],
                                           in_=X0[tile * 128:(tile + 1) * 128, :])
                    b.inc(ins, "x0w%d" % par, 16)
                xv = b.val("x0w%d" % par)

                @blk.vector
                def _(vector, par=par, xv=xv):
                    b.wait(vector, "x0w%d" % par, xv)
                    b.wait(vector, "pe", lnq_free[par])
                lv = ln_tile(x0t[par][:], lnq_tm[par], par, sqt_a,
                             x_gate=("x0w%d" % par, xv))
                x0_free[par] = lv
                transpose_blocks(
                    tp_ps,
                    [(lnq_tm[par], dc * 128) for dc in range(NDC)],
                    [(lnx_fm, dc * 512 + tt * 128) for dc in range(NDC)],
                    pe_waits=[("dv", lv), ("pe", a2_done[0])],
                )

                lnq_free[par] = b.val("pe")
            slab_tp = b.val("dv")

            for iw, key in enumerate(["wqa", "wka", "wva"]):
                for oc in range(NP):
                    li = len(aring_free)
                    slot = li % 2

                    @blk.scalar
                    def _(scalar, key=key, oc=oc, li=li, slot=slot):
                        for nm, v in prep_done[key]:
                            b.wait(scalar, nm, v)
                        if li >= 2:
                            b.wait(scalar, "pe", aring_free[li - 2])
                        ins = scalar.dma_start(
                            out=aring[slot][:].rearrange("p (a c) -> p a c", c=128),
                            in_=wT[key].ap()[0:D, oc * 128:(oc + 1) * 128]
                                .rearrange("(a p) c -> p a c", p=128))
                        b.inc(ins, "aw%d" % slot, 16)
                    wv = b.val("aw%d" % slot)
                    gi = sl * 12 + iw * NP + oc
                    bank = gi % 4

                    @blk.tensor
                    def _(tensor, slot=slot, gi=gi, bank=bank, wv=wv, slab_tp=slab_tp):
                        b.wait(tensor, "aw%d" % slot, wv)
                        b.wait(tensor, "dv", slab_tp)
                        if gi >= 4:
                            b.wait(tensor, "dv", pj_free[gi - 4])
                        for dc in range(NDC):
                            ins = tensor.matmul(
                                pj_ps[:, bank * 512:(bank + 1) * 512],
                                aring[slot][:, dc * 128:(dc + 1) * 128],
                                lnx_fm[:, dc * 512:(dc + 1) * 512],
                                start=(dc == 0), stop=(dc == NDC - 1))
                        b.inc(ins, "pe")
                    aring_free.append(b.val("pe"))
                    mv = b.val("pe")

                    @blk.vector
                    def _(vector, iw=iw, oc=oc, sl=sl, bank=bank, mv=mv):
                        b.wait(vector, "pe", mv)
                        dst = [q_fm, k_fm, v_fms][iw]
                        col = (oc * T + sl * 512) if iw < 2 else (oc * 512)
                        if iw == 0:
                            ins = vector.tensor_scalar_mul(
                                dst[:, col:col + 512],
                                pj_ps[:, bank * 512:(bank + 1) * 512],
                                1.0 / np.sqrt(float(DH)))
                        else:
                            ins = vector.tensor_copy(dst[:, col:col + 512],
                                                     pj_ps[:, bank * 512:(bank + 1) * 512])
                        pj_free.append(b.inc(ins, "dv"))
            vv = b.val("dv")
            transpose_blocks(
                tp_ps,
                [(v_fms, p * 512 + kk * 128) for p in range(NP) for kk in range(4)],
                [(v_tm, (p * 32 + (sl * 4 + kk)) * 128) for p in range(NP)
                 for kk in range(4)],
                pe_waits=[("dv", vv)],
            )

            a2_done[0] = b.val("pe")

        pa1.close()
        import os as _os
        if _os.environ.get("KMEM"):
            print("after A2 close:", nc.sbuf_bytes_remaining)
        flash_dv0 = b.val("dv")

        s_ps = [psall("s_ps0", [128, 1024], pa), psall("s_ps1", [128, 1024], pa)]
        pv_ps = psall("pv_ps", [128, 512], pa)
        rb_ps = psall("rb_ps", [128, 512], pa)
        p_sb = [sball(f"p_sb{i}", [128, T], BF16, pa) for i in range(4)]
        mx4_sb = sball("mx4_sb", [128, 8], F32, pa)
        sacc_sb = sball("sacc_sb", [128, 8], F32, pa)
        rsum_sb = sball("rsum_sb", [128, 1], F32, pa)
        pt_sb = sball("pt_sb", [128, 32 * 512], BF16, pa)
        ao_sb = [sball(f"ao_sb{i}", [128, 512], BF16, pa) for i in range(2)]
        sums_sb = sball("sums_sb", [128, 32], F32, pa)
        nmax_sb = sball("nmax_sb", [128, 4], F32, pa)
        rrow_sb = sball("rrow_sb", [1, 512], F32, pa)
        bcast_sb = sball("bcast_sb", [128, 512], F32, pa)

        s_use = []
        exp_v = []
        sum_dv = []
        ptdone_v = []
        ptev_v = []
        pv_done = []
        pv_ev = []
        ag_wr = []
        rr_seq = []
        rr_pe = []

        for p in range(NP):
            qc = p * T
            for qs in range(8):
                gqs = p * 8 + qs
                for qb in range(4):
                    qi = qs * 4 + qb
                    m = p * 32 + qi
                    par = qi % 2
                    for j4 in range(4):
                        n = m * 8 + j4
                        grp = n % 2

                        @blk.tensor
                        def _(tensor, qi=qi, j4=j4, grp=grp, n=n, qc=qc):
                            if n >= 2:
                                nm, v = s_use[n - 2]
                                b.wait(tensor, nm, v)
                            else:
                                b.wait(tensor, "dv", flash_dv0)
                            for jj in range(2):
                                kchunk = j4 * 2 + jj
                                ins = tensor.matmul(
                                    s_ps[grp][:, jj * 512:(jj + 1) * 512],
                                    q_fm[:, qc + qi * 128:qc + (qi + 1) * 128],
                                    k_fm[:, qc + kchunk * 512:qc + (kchunk + 1) * 512],
                                    start=True, stop=True)
                            b.inc(ins, "pe")
                        sv = b.val("pe")

                        @blk.vector
                        def _(vector, sv=sv, grp=grp, par=par, j4=j4, m=m):
                            b.wait(vector, "pe", sv)
                            if m >= 2 and j4 == 0:
                                b.wait(vector, "ac", exp_v[m - 2])
                            ins = vector.reduce_max(mx4_sb[:, par * 4 + j4:par * 4 + j4 + 1],
                                                    s_ps[grp][:], axis=AX.X)
                            s_use.append(("dv", b.inc(ins, "dv")))
                    mxs = b.val("dv")

                    @blk.vector
                    def _(vector, par=par, qb=qb, m=m, mxs=mxs):
                        b.wait(vector, "dv", mxs)
                        if m >= 4:
                            b.wait(vector, "ac", exp_v[m - 4])
                        ins = vector.reduce_max(nmax_sb[:, qb:qb + 1],
                                                mx4_sb[:, par * 4:par * 4 + 4],
                                                axis=AX.X, negate=True)
                        b.inc(ins, "dv")
                    mxv = b.val("dv")

                    for j4 in range(4):
                        n = m * 8 + 4 + j4
                        grp = n % 2

                        @blk.tensor
                        def _(tensor, qi=qi, j4=j4, grp=grp, n=n, qc=qc):
                            nm, v = s_use[n - 2]
                            b.wait(tensor, nm, v)
                            for jj in range(2):
                                kchunk = j4 * 2 + jj
                                ins = tensor.matmul(
                                    s_ps[grp][:, jj * 512:(jj + 1) * 512],
                                    q_fm[:, qc + qi * 128:qc + (qi + 1) * 128],
                                    k_fm[:, qc + kchunk * 512:qc + (kchunk + 1) * 512],
                                    start=True, stop=True)
                            b.inc(ins, "pe")
                        sv = b.val("pe")

                        @blk.scalar
                        def _(scalar, sv=sv, grp=grp, par=par, qb=qb, qi=qi, j4=j4,
                              m=m, gqs=gqs, mxv=mxv):
                            b.wait(scalar, "pe", sv)
                            if j4 == 0:
                                b.wait(scalar, "dv", mxv)
                                if gqs >= 1:
                                    b.wait(scalar, "pe", ptdone_v[gqs - 1])
                                if m >= 2:
                                    b.wait(scalar, "dv", sum_dv[m - 2])
                            ins = scalar.activation(
                                p_sb[qb][:, j4 * 1024:(j4 + 1) * 1024], s_ps[grp][:],
                                AF.Exp, bias=nmax_sb[:, qb:qb + 1],
                                accum_out=sacc_sb[:, par * 4 + j4:par * 4 + j4 + 1])
                            s_use.append(("ac", b.inc(ins, "ac")))
                    exp_v.append(b.val("ac"))
                    ev = exp_v[-1]

                    @blk.vector
                    def _(vector, par=par, qi=qi, p=p, ev=ev):
                        b.wait(vector, "ac", ev)
                        if p >= 1:
                            b.wait(vector, "pe", rr_pe[(p - 1) * 8 + qi // 4])
                        ins = vector.reduce_sum(rsum_sb[:], sacc_sb[:, par * 4:par * 4 + 4],
                                                axis=AX.X)
                        b.inc(ins, "dv")
                        b.wait(vector, "dv", b.val("dv"))
                        ins = vector.reciprocal(sums_sb[:, qi:qi + 1], rsum_sb[:])
                        sum_dv.append(b.inc(ins, "dv"))

                ev = exp_v[-1]
                for kc in range(32):
                    t = gqs * 32 + kc
                    par2 = kc % 2

                    @blk.tensor
                    def _(tensor, kc=kc, par2=par2, ev=ev, t=t):
                        if kc == 0:
                            b.wait(tensor, "ac", ev)
                        if t >= 2:
                            b.wait(tensor, "dv", ptev_v[t - 2])
                        for qb in range(4):
                            ins = tensor.matmul(
                                tp_ps[:, par2 * 512 + qb * 128:par2 * 512 + (qb + 1) * 128],
                                p_sb[qb][:, kc * 128:(kc + 1) * 128], ident_bf_sb[:],
                                start=True, stop=True)
                        b.inc(ins, "pe")
                    tv = b.val("pe")

                    @blk.vector
                    def _(vector, kc=kc, par2=par2, tv=tv, gqs=gqs):
                        b.wait(vector, "pe", tv)
                        if gqs >= 1 and kc == 0:
                            b.wait(vector, "pe", pv_done[gqs - 1])
                        ins = vector.tensor_copy(pt_sb[:, kc * 512:(kc + 1) * 512],
                                                 tp_ps[:, par2 * 512:(par2 + 1) * 512])
                        ptev_v.append(b.inc(ins, "dv"))

                ptdone_v.append(b.val("pe"))
                ptv = ptev_v[-1]

                @blk.tensor
                def _(tensor, p=p, gqs=gqs, ptv=ptv):
                    b.wait(tensor, "dv", ptv)
                    if gqs >= 1:
                        b.wait(tensor, "dv", pv_ev[gqs - 1])
                    for kc in range(32):
                        ins = tensor.matmul(
                            pv_ps[:],
                            v_tm[:, (p * 32 + kc) * 128:(p * 32 + kc + 1) * 128],
                            pt_sb[:, kc * 512:(kc + 1) * 512],
                            start=(kc == 0), stop=(kc == 31))
                    b.inc(ins, "pe")
                pv_done.append(b.val("pe"))
                sdv = sum_dv[-1]

                @blk.tensor
                def _(tensor, qs=qs, gqs=gqs, sdv=sdv):
                    b.wait(tensor, "dv", sdv)
                    if gqs >= 1:
                        b.wait(tensor, "dv", rr_seq[gqs - 1])
                    for qb in range(4):
                        qi = qs * 4 + qb
                        ins = tensor.matmul(rb_ps[0:1, qb * 128:(qb + 1) * 128],
                                            sums_sb[:, qi:qi + 1], ident_f_sb[:],
                                            start=True, stop=True)
                    b.inc(ins, "pe")
                rr_pe.append(b.val("pe"))
                rv = rr_pe[-1]

                @blk.vector
                def _(vector, rv=rv):
                    b.wait(vector, "pe", rv)
                    ins = vector.tensor_copy(rrow_sb[:], rb_ps[0:1, 0:512])
                    b.inc(ins, "dv")
                rev = b.val("dv")

                @blk.tensor
                def _(tensor, rev=rev):
                    b.wait(tensor, "dv", rev)
                    ins = tensor.matmul(rb_ps[:, 0:512], ones_row_sb[:], rrow_sb[:],
                                        start=True, stop=True)
                    b.inc(ins, "pe")
                bv = b.val("pe")
                pvv = pv_done[-1]

                @blk.vector
                def _(vector, bv=bv, pvv=pvv, gqs=gqs):
                    b.wait(vector, "pe", bv)
                    ins = vector.tensor_copy(bcast_sb[:], rb_ps[:, 0:512])
                    rr_seq.append(b.inc(ins, "dv"))
                    b.wait(vector, "pe", pvv)
                    b.wait(vector, "dv", b.val("dv"))
                    if gqs >= 2:
                        b.wait(vector, "agw", b.val("agw"))
                    ins = vector.tensor_mul(ao_sb[gqs % 2][:], pv_ps[:], bcast_sb[:])
                    pv_ev.append(b.inc(ins, "dv"))
                aov = pv_ev[-1]

                @blk.gpsimd
                def _(gpsimd, p=p, qs=qs, gqs=gqs, aov=aov):
                    b.wait(gpsimd, "dv", aov)
                    ins = gpsimd.dma_start(
                        out=ag_in.ap()[p * 128:(p + 1) * 128, qs * 512:(qs + 1) * 512],
                        in_=ao_sb[gqs % 2][:])
                    ag_wr.append(b.inc(ins, "agw", 16))

        flash_done = pv_ev[-1]
        ag_all = ag_wr[-1]
        pa.close()
        for key, pk in zip(["g", "u", "d", "qwb", "kwb", "vwb", "owb"],
                           ["wva", "g", "u", "d", "qwb", "kwb", "vwb"]):
            prep_wT(key, pk)
        preps_all = [it for k in ["wqa", "wka", "wva", "g", "u", "d", "qwb",
                                  "kwb", "vwb", "owb"] for it in prep_done[k]]
        wstg_st.close()
        import os as _os2
        if _os2.environ.get("KMEM"):
            print("after flash close:", nc.sbuf_bytes_remaining)

        @blk.gpsimd
        def _(gpsimd):
            b.wait(gpsimd, "agw", ag_all)
            ins = gpsimd.collective_compute(
                "AllGather", ALU.bypass, RG, [ag_in.ap().opt()], [ag_out.ap().opt()])
            b.inc(ins, "coll")
        coll_ag = b.val("coll")

        x_res = sball("x_res", [128, NTB * D], F32)
        pa6 = contextlib.ExitStack()
        og_sb = sball("og_sb", [128, NH * 1024], BF16, pa6)
        woT_sb = sball("woT_sb", [128, NH * D], BF16, pa6)
        x0s = [sball(f"x0s{i}", [128, D], F32, pa6) for i in range(2)]
        op_ps = psall("op_psA", [128, 2048], pa6)

        @blk.sync
        def _(sync):
            for nm, v in preps_all:
                b.wait(sync, nm, v)
            b.wait(sync, "ca_woa", 16)
            b.wait(sync, "dv", flash_done)
            b.wait(sync, "agw", ag_all)
            for h in range(NH):
                ins = sync.dma_start(out=woT_sb[:, h * D:(h + 1) * D],
                                     in_=wbf["woa"].ap()[0:D, h * 128:(h + 1) * 128],
                                     transpose=True)
                b.inc(ins, "wow", 16)
        woT_v = b.val("wow")

        @blk.gpsimd
        def _(gpsimd):
            b.wait(gpsimd, "coll", coll_ag)
            pid = gpsimd.partition_id()
            to0v = gpsimd.compute_val((pid % 4) * TOK)
            for h in range(NH):
                ins = gpsimd.dma_start(
                    out=og_sb[:, h * 1024:(h + 1) * 1024],
                    in_=ag_out.ap()[h * 128:(h + 1) * 128, bass.ds(to0v, TOK)])
                b.inc(ins, "ogw", 16)
        og_v = b.val("ogw")

        op_free = []
        for tb in range(NTB):
            par = tb % 2

            @blk.scalar
            def _(scalar, tb=tb, par=par):
                b.wait(scalar, "dv", flash_done)
                b.wait(scalar, "agw", ag_all)
                if tb >= 2:
                    b.wait(scalar, "dv", op_free[tb - 2])
                ins = scalar.dma_start(out=x0s[par][:],
                                       in_=X0M[tb * 128:(tb + 1) * 128, :])
                b.inc(ins, "xsw%d" % par, 16)
            xv = b.val("xsw%d" % par)

            @blk.tensor
            def _(tensor, tb=tb, xv=xv):
                if tb == 0:
                    b.wait(tensor, "ogw", og_v)
                    b.wait(tensor, "wow", woT_v)
                if tb >= 1:
                    b.wait(tensor, "dv", op_free[tb - 1])
                for h in range(NH):
                    for ods in range(4):
                        ins = tensor.matmul(
                            op_ps[:, ods * 512:(ods + 1) * 512],
                            og_sb[:, h * 1024 + tb * 128:h * 1024 + (tb + 1) * 128],
                            woT_sb[:, h * D + ods * 512:h * D + (ods + 1) * 512],
                            start=(h == 0), stop=(h == NH - 1))
                b.inc(ins, "pe")
            ov = b.val("pe")

            @blk.vector
            def _(vector, tb=tb, par=par, ov=ov, xv=xv):
                b.wait(vector, "pe", ov)
                b.wait(vector, "xsw%d" % par, xv)
                for ods in range(4):
                    ins = vector.tensor_add(
                        x_res[:, tb * D + ods * 512:tb * D + (ods + 1) * 512],
                        op_ps[:, ods * 512:(ods + 1) * 512],
                        x0s[par][:, ods * 512:(ods + 1) * 512])
                op_free.append(b.inc(ins, "dv"))
        pa6.close()
        import os as _os3
        if _os3.environ.get("KMEM"):
            print("after A6 close:", nc.sbuf_bytes_remaining)
        emit_dbg("x1")

        def emit_ffn(tag):
            ps = contextlib.ExitStack()
            sb = lambda n, s, d=BF16: sball(n + tag, s, d, ps)
            lnt = [sb(f"lnt{i}", [128, D]) for i in range(2)]
            sqt = sb("sqt", [128, D])
            lnf = sb("lnf", [128, NDC * 1024])
            cs_cols = sb("cs_cols", [128, NTB], F32)
            csr = sb("csr", [1, 1024], F32)
            cs_bc = sb("cs_bc", [128, 1024], F32)
            cshr = sb("cshr", [1, 512], F32)
            csh_bc = sb("csh_bc", [128, 512], F32)
            h_fm = sb("h_fm", [128, NHC * 512])
            tg_sc = [sb(f"tg{i}", [128, 512], F32) for i in range(2)]
            tu_sc = [sb(f"tu{i}", [128, 512], F32) for i in range(2)]
            gsig = [sb(f"gsig{i}", [128, 512]) for i in range(2)]
            dnf = sb("dnf", [128, 2048], F32)
            wring = [sb(f"wring{i}", [128, NDC * 128]) for i in range(3)]
            dring = [sb(f"dring{i}", [128, (NHC // 2) * 128]) for i in range(2)]
            tp2 = psall("tp2" + tag, [128, 1024], ps)
            gu_ps = psall("gu" + tag, [128, 2048], ps)
            dn_ps = psall("dn" + tag, [128, 1024], ps)

            lnt_free = [0, 0]
            for tb in range(NTB):
                par = tb % 2

                @blk.vector
                def _(vector, par=par):
                    b.wait(vector, "pe", lnt_free[par])
                lv = ln_tile(x_res[:, tb * D:(tb + 1) * D], lnt[par], par, sqt,
                             colsum_to=(cs_cols, tb))
                transpose_blocks(
                    tp2,
                    [(lnt[par], dc * 128) for dc in range(NDC)],
                    [(lnf, dc * 1024 + tb * 128) for dc in range(NDC)],
                    pe_waits=[("dv", lv)],
                )
                lnt_free[par] = b.val("pe")
            lnf_v = b.val("dv")

            @blk.tensor
            def _(tensor, lnf_v=lnf_v):
                b.wait(tensor, "dv", lnf_v)
                for tb in range(NTB):
                    ins = tensor.matmul(gu_ps[0:1, tb * 128:(tb + 1) * 128],
                                        cs_cols[:, tb:tb + 1], ident_f_sb[:],
                                        start=True, stop=True)
                b.inc(ins, "pe")
            csv = b.val("pe")

            @blk.vector
            def _(vector, csv=csv):
                b.wait(vector, "pe", csv)
                ins = vector.tensor_copy(csr[:], gu_ps[0:1, 0:1024])
                b.inc(ins, "dv")
            csrv = b.val("dv")

            @blk.tensor
            def _(tensor, csrv=csrv):
                b.wait(tensor, "dv", csrv)
                tensor.matmul(gu_ps[:, 0:512], row128_sb[:], csr[0:1, 0:512],
                              start=True, stop=True)
                ins = tensor.matmul(gu_ps[:, 512:1024], row128_sb[:], csr[0:1, 512:1024],
                                    start=True, stop=True)
                b.inc(ins, "pe")
            csbv = b.val("pe")

            @blk.vector
            def _(vector, csbv=csbv):
                b.wait(vector, "pe", csbv)
                ins = vector.tensor_copy(cs_bc[:], gu_ps[:, 0:1024])
                b.inc(ins, "dv")
            csbc_v = b.val("dv")

            wring_free = []
            dring_free = []
            gu_free = []
            dn_free = []
            th_gate_wait = csbc_v
            for th in range(2):
                for oc in range(NHC):
                    idx = th * NHC + oc
                    for iw, key in enumerate(["g", "u"]):
                        li = idx * 2 + iw
                        slot = li % 3

                        @blk.sync
                        def _(sync, key=key, oc=oc, li=li, slot=slot):
                            for nm, v in prep_done[key]:
                                b.wait(sync, nm, v)
                            if li >= 3:
                                b.wait(sync, "pe", wring_free[li - 3])
                            ins = sync.dma_start(
                                out=wring[slot][:].rearrange("p (a c) -> p a c", c=128),
                                in_=wT[key].ap()[0:D, oc * 128:(oc + 1) * 128]
                                    .rearrange("(a p) c -> p a c", p=128))
                            b.inc(ins, "fw%d" % slot, 16)
                    wv = [b.val("fw%d" % ((idx * 2 + iw) % 3)) for iw in range(2)]
                    par = idx % 2

                    @blk.tensor
                    def _(tensor, th=th, idx=idx, par=par, wv=wv,
                          th_gate_wait=th_gate_wait):
                        for iw in range(2):
                            b.wait(tensor, "fw%d" % ((idx * 2 + iw) % 3), wv[iw])
                        if idx % NHC == 0:
                            b.wait(tensor, "dv", th_gate_wait)
                        if idx >= 2:
                            b.wait(tensor, "dv", gu_free[idx - 2])
                        for iw in range(2):
                            slot = (idx * 2 + iw) % 3
                            bank = iw * 2 + par
                            for dc in range(NDC):
                                ins = tensor.matmul(
                                    gu_ps[:, bank * 512:(bank + 1) * 512],
                                    wring[slot][:, dc * 128:(dc + 1) * 128],
                                    lnf[:, dc * 1024 + th * 512:dc * 1024 + (th + 1) * 512],
                                    start=(dc == 0), stop=(dc == NDC - 1))
                            b.inc(ins, "pe")
                        wring_free.append(b.val("pe"))
                        wring_free.append(b.val("pe"))
                    gv = b.val("pe")

                    @blk.vector
                    def _(vector, oc=oc, th=th, par=par, gv=gv):
                        b.wait(vector, "pe", gv)
                        ins = vector.scalar_tensor_tensor(
                            tg_sc[par][:], gu_ps[:, par * 512:(par + 1) * 512],
                            1.0, cs_bc[:, th * 512:(th + 1) * 512],
                            ALU.mult, ALU.subtract)
                        b.inc(ins, "dv")
                        b.wait(vector, "dv", b.val("dv"))
                        ins = vector.tensor_scalar_mul(tg_sc[par][:], tg_sc[par][:],
                                                       sg_sb[:, oc:oc + 1])
                        b.inc(ins, "dv")
                        ins = vector.tensor_sub(tu_sc[par][:],
                                                gu_ps[:, (2 + par) * 512:(3 + par) * 512],
                                                cs_bc[:, th * 512:(th + 1) * 512])
                        b.inc(ins, "dv")
                    sv = b.val("dv")

                    @blk.scalar
                    def _(scalar, par=par, sv=sv):
                        b.wait(scalar, "dv", sv)
                        ins = scalar.activation(gsig[par][:], tg_sc[par][:], AF.Silu)
                        b.inc(ins, "ac")
                    av = b.val("ac")

                    @blk.vector
                    def _(vector, oc=oc, par=par, av=av):
                        b.wait(vector, "ac", av)
                        ins = vector.scalar_tensor_tensor(
                            h_fm[:, oc * 512:(oc + 1) * 512], tu_sc[par][:],
                            su_sb[:, oc:oc + 1], gsig[par][:], ALU.mult, ALU.mult)
                        gu_free.append(b.inc(ins, "dv"))
                h_v = gu_free[-1]

                @blk.tensor
                def _(tensor, h_v=h_v):
                    b.wait(tensor, "dv", h_v)
                    for hc in range(NHC):
                        ins = tensor.matmul(gu_ps[0:1, 0:512], ones_col_sb[:],
                                            h_fm[:, hc * 512:(hc + 1) * 512],
                                            start=(hc == 0), stop=(hc == NHC - 1))
                    b.inc(ins, "pe")
                chv = b.val("pe")

                @blk.vector
                def _(vector, chv=chv):
                    b.wait(vector, "pe", chv)
                    ins = vector.tensor_scalar_mul(cshr[:], gu_ps[0:1, 0:512], 128.0)
                    b.inc(ins, "dv")
                chrv = b.val("dv")

                @blk.tensor
                def _(tensor, chrv=chrv):
                    b.wait(tensor, "dv", chrv)
                    ins = tensor.matmul(gu_ps[:, 512:1024], ones_row_sb[:], cshr[:],
                                        start=True, stop=True)
                    b.inc(ins, "pe")
                chbv = b.val("pe")

                @blk.vector
                def _(vector, chbv=chbv):
                    b.wait(vector, "pe", chbv)
                    ins = vector.tensor_copy(csh_bc[:], gu_ps[:, 512:1024])
                    b.inc(ins, "dv")
                cshbc_v = b.val("dv")
                th_gate_wait = cshbc_v

                HH = NHC // 2
                for dimc in range(NDC):
                    di = th * NDC + dimc
                    bank = di % 2
                    dwvs = []
                    for hf in range(2):
                        hi = di * 2 + hf

                        @blk.sync
                        def _(sync, dimc=dimc, hf=hf, hi=hi):
                            for nm, v in prep_done["d"]:
                                b.wait(sync, nm, v)
                            if hi >= 2:
                                b.wait(sync, "pe", dring_free[hi - 2])
                            ins = sync.dma_start(
                                out=dring[hf][:].rearrange("p (a c) -> p a c", c=128),
                                in_=wT["d"].ap()[hf * HH * 128:(hf + 1) * HH * 128,
                                                 dimc * 128:(dimc + 1) * 128]
                                    .rearrange("(a p) c -> p a c", p=128))
                            b.inc(ins, "dw%d" % hf, 16)
                        dwvs.append(b.val("dw%d" % hf))

                    for hf in range(2):
                        @blk.tensor
                        def _(tensor, di=di, hf=hf, bank=bank, dwv=dwvs[hf],
                              cshbc_v=cshbc_v):
                            b.wait(tensor, "dw%d" % hf, dwv)
                            if di % NDC == 0 and hf == 0:
                                b.wait(tensor, "dv", cshbc_v)
                            if di >= 2 and hf == 0:
                                b.wait(tensor, "dv", dn_free[di - 2])
                            for hc2 in range(HH):
                                hc = hf * HH + hc2
                                ins = tensor.matmul(
                                    dn_ps[:, bank * 512:(bank + 1) * 512],
                                    dring[hf][:, hc2 * 128:(hc2 + 1) * 128],
                                    h_fm[:, hc * 512:(hc + 1) * 512],
                                    start=(hc == 0), stop=(hc == NHC - 1))
                            b.inc(ins, "pe")
                            dring_free.append(b.val("pe"))
                    dmv = b.val("pe")

                    @blk.vector
                    def _(vector, dimc=dimc, bank=bank, dmv=dmv):
                        b.wait(vector, "pe", dmv)
                        ins = vector.tensor_sub(dnf[:, bank * 1024:bank * 1024 + 512],
                                                dn_ps[:, bank * 512:(bank + 1) * 512],
                                                csh_bc[:])
                        b.inc(ins, "dv")
                        b.wait(vector, "dv", b.val("dv"))
                        ins = vector.tensor_scalar_mul(
                            dnf[:, bank * 1024 + 512:(bank + 1) * 1024],
                            dnf[:, bank * 1024:bank * 1024 + 512],
                            sd_sb[:, dimc:dimc + 1])
                        dn_free.append(b.inc(ins, "dv"))
                    dfv = b.val("dv")

                    for j in range(4):
                        tb = th * 4 + j
                        tpar = tstate["n"] % 2
                        tstate["n"] += 1

                        @blk.tensor
                        def _(tensor, j=j, tpar=tpar, dfv=dfv, bank=bank):
                            if len(tstate["hist"]) >= 2:
                                b.wait(tensor, "dv", tstate["hist"][-2])
                            if j == 0:
                                b.wait(tensor, "dv", dfv)
                            ins = tensor.matmul(
                                tp2[:, tpar * 512:tpar * 512 + 128],
                                dnf[:, bank * 1024 + 512 + j * 128:
                                    bank * 1024 + 512 + (j + 1) * 128],
                                ident_f_sb[:], start=True, stop=True)
                            b.inc(ins, "pe")
                        tpv = b.val("pe")

                        @blk.vector
                        def _(vector, tb=tb, dimc=dimc, tpar=tpar, tpv=tpv):
                            b.wait(vector, "pe", tpv)
                            ins = vector.tensor_add(
                                x_res[:, tb * D + dimc * 128:tb * D + (dimc + 1) * 128],
                                x_res[:, tb * D + dimc * 128:tb * D + (dimc + 1) * 128],
                                tp2[:, tpar * 512:tpar * 512 + 128])
                            tstate["hist"].append(b.inc(ins, "dv"))
            ps.close()

        def emit_attn_b():
            psc = contextlib.ExitStack()
            sb = lambda n, s, d=BF16: sball(n, s, d, psc)
            tpB = psall("tpB", [128, 1024], psc)
            pjB = psall("pjB", [128, 2048], psc)
            smB = psall("smB", [128, 1024], psc)

            qp_fm = sball("qp_fm", [128, NDC * 1024], BF16, psc)
            at_fm = sball("at_fm", [128, NDC * 512], BF16, psc)
            ps1b = contextlib.ExitStack()
            kp_fm = sball("kp_fm", [128, NDC * 1024], BF16, ps1b)
            v_fm = sball("bv_fm", [128, NDC * 1024], BF16, ps1b)
            ps1 = contextlib.ExitStack()
            lnt = [sball("blnt0", [128, D], BF16, ps1)]
            lnf = sball("blnf", [128, NDC * 512], BF16, ps1)
            t_el = [sball("t_el0", [128, 512], BF16, ps1)]
            wringB = [sball(f"wringB{i}", [128, NDC * 128], BF16, ps1) for i in range(2)]

            lnt_free = [0]
            wring_free = []
            pj_free2 = []
            el_ac = []
            b_slab_done = [0]
            for sl in range(2):
                for tt in range(4):
                    tb = sl * 4 + tt

                    @blk.vector
                    def _(vector):
                        b.wait(vector, "pe", lnt_free[0])
                    lv = ln_tile(x_res[:, tb * D:(tb + 1) * D], lnt[0], tb % 2,
                                 lnt[0], scalar_gate=lnt_free[0])
                    transpose_blocks(
                        tpB,
                        [(lnt[0], dc * 128) for dc in range(NDC)],
                        [(lnf, dc * 512 + tt * 128) for dc in range(NDC)],
                        pe_waits=[("dv", lv), ("pe", b_slab_done[0])],
                    )
                    lnt_free[0] = b.val("pe")
                lnf_v = b.val("dv")

                for iw, (key, bias_t) in enumerate([("qwb", qb_sb), ("kwb", kb_sb),
                                                    ("vwb", vb_sb)]):
                    dst = [qp_fm, kp_fm, v_fm][iw]
                    for oc in range(NDC):
                        li = len(wring_free)
                        slot = li % 2

                        @blk.sync
                        def _(sync, key=key, oc=oc, li=li, slot=slot):
                            for nm, v in prep_done[key]:
                                b.wait(sync, nm, v)
                            if li >= 2:
                                b.wait(sync, "pe", wring_free[li - 2])
                            ins = sync.dma_start(
                                out=wringB[slot][:].rearrange("p (a c) -> p a c", c=128),
                                in_=wT[key].ap()[0:D, oc * 128:(oc + 1) * 128]
                                    .rearrange("(a p) c -> p a c", p=128))
                            b.inc(ins, "bw%d" % slot, 16)
                        wv = b.val("bw%d" % slot)
                        gi = len(pj_free2)
                        bank = gi % 4

                        @blk.tensor
                        def _(tensor, slot=slot, gi=gi, bank=bank, wv=wv,
                              lnf_v=lnf_v):
                            b.wait(tensor, "bw%d" % slot, wv)
                            b.wait(tensor, "dv", lnf_v)
                            if gi >= 4:
                                b.wait(tensor, "dv", pj_free2[gi - 4])
                            for dc in range(NDC):
                                ins = tensor.matmul(
                                    pjB[:, bank * 512:(bank + 1) * 512],
                                    wringB[slot][:, dc * 128:(dc + 1) * 128],
                                    lnf[:, dc * 512:(dc + 1) * 512],
                                    start=(dc == 0), stop=(dc == NDC - 1))
                            b.inc(ins, "pe")
                        wring_free.append(b.val("pe"))
                        mv = b.val("pe")

                        if iw < 2:
                            dslice = dst[:, oc * 1024 + sl * 512:oc * 1024 + (sl + 1) * 512]

                            @blk.vector
                            def _(vector, oc=oc, bank=bank, mv=mv, bias_t=bias_t):
                                b.wait(vector, "pe", mv)
                                if len(el_ac) >= 1:
                                    b.wait(vector, "ac", el_ac[-1])
                                if pj_free2:
                                    b.wait(vector, "dv", pj_free2[-1])
                                ins = vector.tensor_scalar(
                                    t_el[0][:], pjB[:, bank * 512:(bank + 1) * 512],
                                    bias_t[:, oc:oc + 1], 0.0, ALU.add, ALU.min)
                                b.inc(ins, "dv")
                            elv = b.val("dv")

                            @blk.scalar
                            def _(scalar, dslice=dslice, elv=elv):
                                b.wait(scalar, "dv", elv)
                                ins = scalar.activation(dslice, t_el[0][:], AF.Exp)
                                el_ac.append(b.inc(ins, "ac"))
                            eav = b.val("ac")

                            @blk.vector
                            def _(vector, dslice=dslice, oc=oc, bank=bank,
                                  eav=eav, bias_t=bias_t):
                                b.wait(vector, "ac", eav)
                                ins = vector.tensor_scalar(
                                    t_el[0][:], pjB[:, bank * 512:(bank + 1) * 512],
                                    bias_t[:, oc:oc + 1], 0.0, ALU.add, ALU.max)
                                b.inc(ins, "dv")
                                b.wait(vector, "dv", b.val("dv"))
                                ins = vector.tensor_add(dslice, dslice, t_el[0][:])
                                pj_free2.append(b.inc(ins, "dv"))
                        else:
                            @blk.vector
                            def _(vector, dst=dst, oc=oc, sl=sl, bank=bank, mv=mv,
                                  bias_t=bias_t):
                                b.wait(vector, "pe", mv)
                                ins = vector.tensor_scalar(
                                    dst[:, oc * 1024 + sl * 512:oc * 1024 + (sl + 1) * 512],
                                    pjB[:, bank * 512:(bank + 1) * 512],
                                    bias_t[:, oc:oc + 1], None, ALU.add)
                                pj_free2.append(b.inc(ins, "dv"))

                b_slab_done[0] = b.val("pe")
            proj_v = b.val("dv")

            ps1.close()
            km_tm = sball("km_tm", [128, 8 * 128], BF16, ps1b)
            vm_tm = sball("vm_tm", [128, 8 * 128], BF16, ps1b)
            kvt = sball("kvt", [128, 128], F32, ps1b)
            ksrow = sball("ksrow", [1, 128], F32, ps1b)
            kv_wr = []
            for h in range(NH):
                tv = transpose_blocks(
                    tpB,
                    [(kp_fm, h * 1024 + j * 128) for j in range(8)]
                    + [(v_fm, h * 1024 + j * 128) for j in range(8)],
                    [(km_tm, j * 128) for j in range(8)]
                    + [(vm_tm, j * 128) for j in range(8)],
                    pe_waits=[("dv", proj_v)] + ([("kvw", kv_wr[-1])] if h else []),
                )

                @blk.tensor
                def _(tensor, h=h, tv=tv):
                    b.wait(tensor, "dv", tv)
                    if h >= 1:
                        b.wait(tensor, "dv", kv_ev[-1])
                    for j in range(8):
                        tensor.matmul(smB[:, 0:128], km_tm[:, j * 128:(j + 1) * 128],
                                      vm_tm[:, j * 128:(j + 1) * 128],
                                      start=(j == 0), stop=(j == 7))
                    for j in range(8):
                        ins = tensor.matmul(smB[0:1, 512:640], ones_col_sb[:],
                                            km_tm[:, j * 128:(j + 1) * 128],
                                            start=(j == 0), stop=(j == 7))
                    b.inc(ins, "pe")
                kvv = b.val("pe")

                @blk.vector
                def _(vector, kvv=kvv):
                    b.wait(vector, "pe", kvv)
                    vector.tensor_copy(kvt[:], smB[:, 0:128])
                    ins = vector.tensor_copy(ksrow[:], smB[0:1, 512:640])
                    kv_ev.append(b.inc(ins, "dv"))
                kev = kv_ev[-1]

                @blk.gpsimd
                def _(gpsimd, h=h, kev=kev):
                    b.wait(gpsimd, "dv", kev)
                    ins = gpsimd.dma_start(out=ar_in.ap()[h * 128:(h + 1) * 128, :],
                                           in_=kvt[:])
                    b.inc(ins, "kvw", 16)
                    ins = gpsimd.dma_start(
                        out=ar_in.ap()[NH * DH + h:NH * DH + h + 1, :], in_=ksrow[:])
                    kv_wr.append(b.inc(ins, "kvw", 16))
            ps1b.close()

            @blk.gpsimd
            def _(gpsimd):
                b.wait(gpsimd, "kvw", kv_wr[-1])
                ins = gpsimd.collective_compute(
                    "AllReduce", ALU.add, RG, [ar_in.ap().opt()], [ar_out.ap().opt()])
                b.inc(ins, "coll")
            coll_ar = b.val("coll")

            ps2 = contextlib.ExitStack()
            kvb = sball("kvb", [128, NH * 128], BF16, ps2)
            zrow_sb = sball("zrow_sb", [1, 512], F32, ps2)
            zb_sb = sball("zb_sb", [128, 512], F32, ps2)
            ksf = sball("ksf", [128, NH], F32, ps2)
            ksb = sball("ksb", [128, NH], BF16, ps2)

            @blk.gpsimd
            def _(gpsimd):
                b.wait(gpsimd, "coll", coll_ar)
                ins = gpsimd.dma_start(
                    out=kvb[:].rearrange("p (a c) -> p a c", c=128),
                    in_=ar_out.ap()[0:NH * DH, :].rearrange("(a p) c -> p a c", p=128))
                b.inc(ins, "kvl", 16)
                ins = gpsimd.dma_start(
                    out=ksf[:],
                    in_=bass.AP(ar_out.ap().tensor, (NH * DH) * DH, [[1, 128], [128, NH]]),
                    allow_slow_non_contiguous=True)
                b.inc(ins, "kvl", 16)
            kv_ld = b.val("kvl")

            @blk.vector
            def _(vector, kv_ld=kv_ld):
                b.wait(vector, "kvl", kv_ld)
                ins = vector.tensor_copy(ksb[:], ksf[:])
                b.inc(ins, "dv")
            ksb_v = b.val("dv")

            ps3 = contextlib.ExitStack()
            owT_b = sball("owT_b", [128, NDC * D], BF16, ps3)
            ob_row = sball("ob_row", [1, D], F32, ps3)
            ob_bc = sball("ob_bc", [128, D], F32, ps3)
            t_ep = sball("t_ep", [128, 512], F32, ps3)

            @blk.sync
            def _(sync):
                for nm, v in prep_done["owb"]:
                    b.wait(sync, nm, v)
                b.wait(sync, "kvw", kv_wr[-1])
                for ec in range(NDC):
                    ins = sync.dma_start(out=owT_b[:, ec * D:(ec + 1) * D],
                                         in_=wT["owb"].ap()[ec * 128:(ec + 1) * 128, 0:D])
                    b.inc(ins, "oww", 16)
                ins = sync.dma_start(out=ob_row[:],
                                     in_=bass.AP(EXT["ob_b"].tensor, 0, [[D, 1], [1, D]]))
                b.inc(ins, "oww", 16)
            ow_v = b.val("oww")

            @blk.tensor
            def _(tensor, ow_v=ow_v):
                b.wait(tensor, "oww", ow_v)
                for j in range(4):
                    ins = tensor.matmul(pjB[:, j * 512:(j + 1) * 512], ones_row_sb[:],
                                        ob_row[0:1, j * 512:(j + 1) * 512],
                                        start=True, stop=True)
                b.inc(ins, "pe")
            obv = b.val("pe")

            @blk.vector
            def _(vector, obv=obv):
                b.wait(vector, "pe", obv)
                ins = vector.tensor_copy(ob_bc[:], pjB[:, 0:2048])
                b.inc(ins, "dv")
            ob_v = b.val("dv")

            op_free2 = []
            at_ev = []
            for th in range(2):
                for h in range(NH):
                    @blk.tensor
                    def _(tensor, h=h, th=th, ksb_v=ksb_v):
                        if th == 0 and h == 0:
                            b.wait(tensor, "dv", ksb_v)
                        if at_ev:
                            b.wait(tensor, "dv", at_ev[-1])
                        ins = tensor.matmul(
                            smB[0:1, 0:512], ksb[:, h:h + 1],
                            qp_fm[:, h * 1024 + th * 512:h * 1024 + (th + 1) * 512],
                            start=True, stop=True)
                        b.inc(ins, "pe")
                    zv = b.val("pe")

                    @blk.vector
                    def _(vector, zv=zv):
                        b.wait(vector, "pe", zv)
                        ins = vector.tensor_scalar_add(zrow_sb[:], smB[0:1, 0:512], Z_EPS)
                        b.inc(ins, "dv")
                        b.wait(vector, "dv", b.val("dv"))
                        ins = vector.reciprocal(zrow_sb[:], zrow_sb[:])
                        b.inc(ins, "dv")
                    zrv = b.val("dv")

                    @blk.tensor
                    def _(tensor, zrv=zrv):
                        b.wait(tensor, "dv", zrv)
                        ins = tensor.matmul(smB[:, 512:1024], ones_row_sb[:], zrow_sb[:],
                                            start=True, stop=True)
                        b.inc(ins, "pe")
                    zbv = b.val("pe")

                    @blk.vector
                    def _(vector, h=h, th=th, zbv=zbv):
                        b.wait(vector, "pe", zbv)
                        ins = vector.tensor_copy(zb_sb[:], smB[:, 512:1024])
                        b.inc(ins, "dv")
                        b.wait(vector, "dv", b.val("dv"))
                        ins = vector.tensor_mul(
                            qp_fm[:, h * 1024 + th * 512:h * 1024 + (th + 1) * 512],
                            qp_fm[:, h * 1024 + th * 512:h * 1024 + (th + 1) * 512],
                            zb_sb[:])
                        b.inc(ins, "dv")
                    qsc = b.val("dv")

                    @blk.tensor
                    def _(tensor, h=h, th=th, qsc=qsc):
                        b.wait(tensor, "dv", qsc)
                        ins = tensor.matmul(
                            smB[0:128, 0:512], kvb[:, h * 128:(h + 1) * 128],
                            qp_fm[:, h * 1024 + th * 512:h * 1024 + (th + 1) * 512],
                            start=True, stop=True)
                        b.inc(ins, "pe")
                    nv = b.val("pe")

                    @blk.vector
                    def _(vector, h=h, nv=nv, th=th):
                        b.wait(vector, "pe", nv)
                        if th > 0 and len(op_free2) < 4 * th and h == 0:
                            pass
                        ins = vector.tensor_copy(at_fm[:, h * 512:(h + 1) * 512],
                                                 smB[0:128, 0:512])
                        at_ev.append(b.inc(ins, "dv"))
                at_v = at_ev[-1]

                for j in range(4):
                    tb = th * 4 + j

                    @blk.tensor
                    def _(tensor, tb=tb, j=j, th=th, at_v=at_v, ob_v=ob_v):
                        if j == 0:
                            b.wait(tensor, "dv", at_v)
                            b.wait(tensor, "dv", ob_v)
                        if len(op_free2) >= 1:
                            b.wait(tensor, "dv", op_free2[-1])
                        for ec in range(NDC):
                            for ods in range(4):
                                ins = tensor.matmul(
                                    pjB[:, ods * 512:(ods + 1) * 512],
                                    at_fm[:, ec * 512 + j * 128:ec * 512 + (j + 1) * 128],
                                    owT_b[:, ec * D + ods * 512:ec * D + (ods + 1) * 512],
                                    start=(ec == 0), stop=(ec == NDC - 1))
                        b.inc(ins, "pe")
                    ov = b.val("pe")

                    @blk.vector
                    def _(vector, tb=tb, ov=ov):
                        b.wait(vector, "pe", ov)
                        for ods in range(4):
                            if ods > 0:
                                b.wait(vector, "dv", b.val("dv"))
                            ins = vector.tensor_add(t_ep[:],
                                                    pjB[:, ods * 512:(ods + 1) * 512],
                                                    ob_bc[:, ods * 512:(ods + 1) * 512])
                            b.inc(ins, "dv")
                            b.wait(vector, "dv", b.val("dv"))
                            ins = vector.tensor_add(
                                x_res[:, tb * D + ods * 512:tb * D + (ods + 1) * 512],
                                x_res[:, tb * D + ods * 512:tb * D + (ods + 1) * 512],
                                t_ep[:])
                            b.inc(ins, "dv")
                        op_free2.append(b.val("dv"))
            ps3.close()
            ps2.close()
            psc.close()

        kv_ev = []

        if STAGE >= 2:
            barrier(dma_waits=preps_all)
            emit_ffn("F1")
            emit_dbg("x2")
        if STAGE >= 3:
            barrier()
            emit_attn_b()
            emit_dbg("x3")
        if STAGE >= 4:
            barrier()
            emit_ffn("F2")

        fin = b.val("dv")

        @blk.gpsimd
        def _(gpsimd):
            b.wait(gpsimd, "dv", fin)
            for tb in range(NTB):
                ins = gpsimd.dma_start(out=Y[tb * 128:(tb + 1) * 128, :],
                                       in_=x_res[:, tb * D:(tb + 1) * D])
                b.inc(ins, "st", 16)
            b.wait(gpsimd, "st", b.val("st"))

    stack.close()
    return nc, dbg_outs


def _in_maps(inputs):
    x = np.ascontiguousarray(inputs["L0_attn_a_x"])
    maps = []
    for c in range(8):
        bb, gg = c // 4, c % 4
        h0 = 4 * gg
        maps.append({
            "x0": np.ascontiguousarray(x[bb]),
            "x0m": np.ascontiguousarray(x[bb, gg * TOK:(gg + 1) * TOK]),
            "wq_a": np.ascontiguousarray(inputs["L0_attn_a_Wq"][h0 * DH:(h0 + 4) * DH]),
            "wk_a": np.ascontiguousarray(inputs["L0_attn_a_Wk"][h0 * DH:(h0 + 4) * DH]),
            "wv_a": np.ascontiguousarray(inputs["L0_attn_a_Wv"][h0 * DH:(h0 + 4) * DH]),
            "wo_a": np.ascontiguousarray(inputs["L0_attn_a_Wo"]),
            "qw_b": np.ascontiguousarray(inputs["L0_attn_b_q_w"]),
            "kw_b": np.ascontiguousarray(inputs["L0_attn_b_k_w"]),
            "vw_b": np.ascontiguousarray(inputs["L0_attn_b_v_w"]),
            "ow_b": np.ascontiguousarray(inputs["L0_attn_b_o_w"]),
            "qb_b": np.ascontiguousarray(inputs["L0_attn_b_q_b"]),
            "kb_b": np.ascontiguousarray(inputs["L0_attn_b_k_b"]),
            "vb_b": np.ascontiguousarray(inputs["L0_attn_b_v_b"]),
            "ob_b": np.ascontiguousarray(inputs["L0_attn_b_o_b"]),
            "wg_i": np.ascontiguousarray(inputs["L0_ffn_wq_gate"]),
            "wu_i": np.ascontiguousarray(inputs["L0_ffn_wq_up"]),
            "wd_i": np.ascontiguousarray(inputs["L0_ffn_wq_down"]),
            "sg": np.ascontiguousarray(inputs["L0_ffn_s_gate"]),
            "su": np.ascontiguousarray(inputs["L0_ffn_s_up"]),
            "sd": np.ascontiguousarray(inputs["L0_ffn_s_down"]),
        })
    return maps


def kernel(**inputs):
    nc, _ = build_core_program(0)
    res = run_bass_kernel_spmd(nc, _in_maps(inputs), core_ids=list(range(8)))
    out = np.empty((B_, T, D), dtype=np.float32)
    for c in range(8):
        bb, gg = c // 4, c % 4
        out[bb, gg * TOK:(gg + 1) * TOK, :] = res.results[c]["y"]
    return out


# version 03
# speedup vs baseline: 1.1241x; 1.0683x over previous
import contextlib
import numpy as np
import ml_dtypes
import concourse.bass as bass
import concourse.mybir as mybir
from concourse.bass_utils import run_bass_kernel_spmd

F32 = mybir.dt.float32
BF16 = mybir.dt.bfloat16
I32 = mybir.dt.int32
AF = mybir.ActivationFunctionType
ALU = mybir.AluOpType
AX = mybir.AxisListType

B_, T, D = 2, 4096, 2048
HID = 5632
NH, DH = 16, 128
TOK = 1024
NTB = TOK // 128
NDC = D // 128
NHC = HID // 128
NP = 4
LN_EPS = 1e-5
Z_EPS = 1e-6

STAGE = 4
DBG = ()


class Bld:
    def __init__(self, nc, stack):
        self.nc = nc
        self.stack = stack
        self.cnt = {}
        self.sems = {}

    def sem(self, name):
        if name not in self.sems:
            self.sems[name] = self.stack.enter_context(self.nc.semaphore(name))
            self.cnt[name] = 0
        return self.sems[name]

    def inc(self, ins, name, n=1):
        ins.then_inc(self.sem(name), n)
        self.cnt[name] += n
        return self.cnt[name]

    def val(self, name):
        self.sem(name)
        return self.cnt[name]

    def wait(self, eng, name, v):
        assert 0 <= v <= self.cnt.get(name, 0), ("wait-ahead", name, v, self.cnt.get(name))
        if v > 0:
            eng.wait_ge(self.sem(name), v)


def build_core_program(core):
    nc = bass.Bass("TRN2", target_bir_lowering=False)
    g = core % 4
    to0 = g * TOK
    RG = [[0, 1, 2, 3], [4, 5, 6, 7]]

    X0 = nc.dram_tensor("x0", [T, D], F32, kind="ExternalInput").ap()
    X0M = nc.dram_tensor("x0m", [TOK, D], F32, kind="ExternalInput").ap()
    EXT = {}
    for nm, sh, dt in [("wq_a", [512, D], F32), ("wk_a", [512, D], F32),
                       ("wv_a", [512, D], F32), ("wo_a", [D, D], F32),
                       ("qw_b", [D, D], F32), ("kw_b", [D, D], F32),
                       ("vw_b", [D, D], F32), ("ow_b", [D, D], F32),
                       ("qb_b", [D], F32), ("kb_b", [D], F32), ("vb_b", [D], F32),
                       ("ob_b", [D], F32), ("wg_i", [HID, D], I32),
                       ("wu_i", [HID, D], I32), ("wd_i", [D, HID], I32),
                       ("sg", [HID], F32), ("su", [HID], F32), ("sd", [D], F32)]:
        EXT[nm] = nc.dram_tensor(nm, sh, dt, kind="ExternalInput").ap()
    Y = nc.dram_tensor("y", [TOK, D], F32, kind="ExternalOutput").ap()

    def dram(name, shape, dt=BF16):
        return nc.dram_tensor(name, shape, dt, kind="Internal")

    CAST_SRC = {"wqa": "wq_a", "wka": "wk_a", "wva": "wv_a", "woa": "wo_a",
                "g": "wg_i", "u": "wu_i", "d": "wd_i",
                "qwb": "qw_b", "kwb": "kw_b", "vwb": "vw_b", "owb": "ow_b"}
    WSHAPE = {"wqa": [512, D], "wka": [512, D], "wva": [512, D], "woa": [D, D],
              "g": [HID, D], "u": [HID, D], "d": [D, HID],
              "qwb": [D, D], "kwb": [D, D], "vwb": [D, D], "owb": [D, D]}
    wbf = {k: dram(k + "_bf", s) for k, s in WSHAPE.items()}
    wT = {k: dram(k + "_T", [s[1], s[0]]) for k, s in WSHAPE.items() if k != "woa"}
    ag_in = dram("ag_in", [NP * DH, T])
    ag_out = dram("ag_out", [NH * DH, T])
    ar_in = dram("ar_in", [NH * DH + NH, DH], F32)
    ar_out = dram("ar_out", [NH * DH + NH, DH], F32)

    ident_bf = nc.inline_tensor(np.eye(128, dtype=ml_dtypes.bfloat16), "c_ident_bf")
    ident_f = nc.inline_tensor(np.eye(128, dtype=np.float32), "c_ident_f")
    ones_row_f = nc.inline_tensor(np.ones((1, 128), dtype=np.float32), "c_ones_row")
    row128_f = nc.inline_tensor(np.full((1, 128), 128.0, dtype=np.float32), "c_row128")
    one11_f = nc.inline_tensor(np.ones((1, 1), dtype=np.float32), "c_one11")
    ones_col_bf = nc.inline_tensor(np.ones((128, 1), dtype=ml_dtypes.bfloat16), "c_ones_col")
    eps_col_f = nc.inline_tensor(np.full((128, 1), LN_EPS, dtype=np.float32), "c_eps_col")

    stack = contextlib.ExitStack()
    b = Bld(nc, stack)
    bko_ev = []

    def sball(name, shape, dtype=BF16, st=None):
        return (st or stack).enter_context(nc.sbuf_tensor(name, shape, dtype))

    def psall(name, shape, st=None, dtype=F32):
        return (st or stack).enter_context(nc.psum_tensor(name, shape, dtype))

    ident_bf_sb = sball("ident_bf_sb", [128, 128])
    ident_f_sb = sball("ident_f_sb", [128, 128], F32)
    ones_row_sb = sball("ones_row_sb", [1, 128], F32)
    row128_sb = sball("row128_sb", [1, 128], F32)
    one11_sb = sball("one11_sb", [1, 1], F32)
    ones_col_sb = sball("ones_col_sb", [128, 1])
    eps_sb = sball("eps_sb", [128, 1], F32)
    qb_sb = sball("qb_sb", [128, 16], F32)
    kb_sb = sball("kb_sb", [128, 16], F32)
    vb_sb = sball("vb_sb", [128, 16], F32)
    sg_sb = sball("sg_sb", [128, NHC], F32)
    su_sb = sball("su_sb", [128, NHC], F32)
    sd_sb = sball("sd_sb", [128, 16], F32)
    st_a = sball("st_a", [128, 8], F32)
    WSEG = HID // 2
    wstg_st = contextlib.ExitStack()
    wstg = [sball(f"wstg{i}", [128, WSEG], BF16, wstg_st) for i in range(2)]

    dbg_outs = {}

    with nc.Block() as blk:
        @blk.sync
        def _(sync):
            for dst, src in [(ident_bf_sb, ident_bf), (ident_f_sb, ident_f),
                             (ones_row_sb, ones_row_f), (row128_sb, row128_f),
                             (one11_sb, one11_f), (ones_col_sb, ones_col_bf),
                             (eps_sb, eps_col_f)]:
                ins = sync.dma_start(out=dst[:], in_=src.ap())
                b.inc(ins, "ko", 16)
        konst_v = b.val("ko")
        bs_ps = contextlib.ExitStack()
        btmp = sball("btmp", [NHC, 128], F32, bs_ps)
        b_ps = psall("b_ps", [128, 64], bs_ps)
        for i, (dst, srcnm, n) in enumerate(
                [(qb_sb, "qb_b", 16), (kb_sb, "kb_b", 16), (vb_sb, "vb_b", 16),
                 (sg_sb, "sg", NHC), (su_sb, "su", NHC), (sd_sb, "sd", 16)]):
            @blk.scalar
            def _(scalar, srcnm=srcnm, n=n, i=i):
                if i >= 1:
                    b.wait(scalar, "dv", bko_ev[-1])
                ins = scalar.dma_start(
                    out=btmp[0:n, :],
                    in_=EXT[srcnm].rearrange("(a c) -> a c", c=128))
                b.inc(ins, "bko", 16)
            bv = b.val("bko")

            @blk.tensor
            def _(tensor, n=n, bv=bv, i=i):
                b.wait(tensor, "ko", konst_v)
                b.wait(tensor, "bko", bv)
                if i >= 1:
                    b.wait(tensor, "dv", bko_ev[-1])
                ins = tensor.transpose(b_ps[:, 0:n], btmp[0:n, :], ident_f_sb[0:n, 0:n])
                b.inc(ins, "pe")
            tv = b.val("pe")

            @blk.vector
            def _(vector, dst=dst, n=n, tv=tv):
                b.wait(vector, "pe", tv)
                ins = vector.tensor_copy(dst[:, 0:n], b_ps[:, 0:n])
                bko_ev.append(b.inc(ins, "dv"))
        bs_ps.close()

        cast_done = {}

        @blk.gpsimd
        def _(gpsimd):
            for key in ["wqa", "wka", "wva", "g", "woa", "u", "d",
                        "qwb", "kwb", "vwb", "owb"]:
                ins = gpsimd.dma_start(out=wbf[key].ap(), in_=EXT[CAST_SRC[key]])
                cast_done[key] = b.inc(ins, "ca_" + key, 16)

        for engname in ("tensor", "vector", "scalar"):
            @getattr(blk, engname)
            def _(eng):
                b.wait(eng, "ko", konst_v)

        prep_done = {}
        wr_hist = []

        def prep_wT(key, prev_key):
            rows, cols = WSHAPE[key]
            n = cols // 128
            segs = [(s, min(WSEG, rows - s)) for s in range(0, rows, WSEG)]
            ci = [0]
            for i in range(n):
                for r0, rn in segs:
                    j = ci[0]
                    ci[0] += 1

                    wsem = "wr%d_" % (j % 2) + key

                    @blk.sync
                    def _(sync, key=key, i=i, j=j, r0=r0, rn=rn, prev_key=prev_key,
                          wsem=wsem):
                        if j == 0:
                            b.wait(sync, "ca_" + key, 16)
                        if j <= 1 and prev_key is not None:
                            for par in range(2):
                                nm = "wr%d_" % par + prev_key
                                b.wait(sync, nm, b.val(nm))
                        if j >= 2:
                            b.wait(sync, wsem, b.val(wsem))
                        ins = sync.dma_start(
                            out=wstg[j % 2][:, 0:rn],
                            in_=wbf[key].ap()[r0:r0 + rn, i * 128:(i + 1) * 128],
                            transpose=True)
                        b.inc(ins, "wp%d" % (j % 2), 16)
                    sv = b.val("wp%d" % (j % 2))

                    @blk.gpsimd
                    def _(gpsimd, key=key, i=i, j=j, r0=r0, rn=rn, sv=sv, wsem=wsem):
                        b.wait(gpsimd, "wp%d" % (j % 2), sv)
                        ins = gpsimd.dma_start(
                            out=wT[key].ap()[i * 128:(i + 1) * 128, r0:r0 + rn],
                            in_=wstg[j % 2][:, 0:rn])
                        b.inc(ins, wsem, 16)
            prep_done[key] = [("wr%d_" % par + key, b.val("wr%d_" % par + key))
                              for par in range(2)]

        for key, pk in zip(["wqa", "wka", "wva"], [None, "wqa", "wka"]):
            prep_wT(key, pk)

        def ln_tile(x_ap, out_tile, par, sqt, colsum_to=None, scalar_gate=0,
                    x_gate=None):
            c = par * 4

            @blk.vector
            def _(vector):
                ins = vector.reduce_sum(st_a[:, c:c + 1], x_ap, axis=AX.X)
                b.inc(ins, "dv")

            @blk.scalar
            def _(scalar):
                b.wait(scalar, "pe", scalar_gate)
                if x_gate is not None:
                    b.wait(scalar, x_gate[0], x_gate[1])
                ins = scalar.activation(sqt[:], x_ap, AF.Square,
                                        accum_out=st_a[:, c + 1:c + 2])
                b.inc(ins, "ac")
            v0, a0 = b.val("dv"), b.val("ac")

            @blk.vector
            def _(vector):
                b.wait(vector, "dv", v0)
                b.wait(vector, "ac", a0)
                ins = vector.tensor_scalar_mul(st_a[:, c:c + 1], st_a[:, c:c + 1], 1.0 / D)
                b.inc(ins, "dv")
                b.wait(vector, "dv", b.val("dv"))
                ins = vector.tensor_mul(st_a[:, c + 2:c + 3], st_a[:, c:c + 1],
                                        st_a[:, c:c + 1])
                b.inc(ins, "dv")
                b.wait(vector, "dv", b.val("dv"))
                ins = vector.tensor_scalar(st_a[:, c + 1:c + 2], st_a[:, c + 1:c + 2],
                                           1.0 / D, None, ALU.mult)
                b.inc(ins, "dv")
                b.wait(vector, "dv", b.val("dv"))
                ins = vector.tensor_sub(st_a[:, c + 1:c + 2], st_a[:, c + 1:c + 2],
                                        st_a[:, c + 2:c + 3])
                b.inc(ins, "dv")
            v1 = b.val("dv")

            @blk.scalar
            def _(scalar):
                b.wait(scalar, "dv", v1)
                ins = scalar.activation(st_a[:, c + 2:c + 3], st_a[:, c + 1:c + 2],
                                        AF.Ln, bias=eps_sb[:, 0:1])
                b.inc(ins, "ac")
                b.wait(scalar, "ac", b.val("ac"))
                ins = scalar.activation(st_a[:, c + 3:c + 4], st_a[:, c + 2:c + 3],
                                        AF.Exp, scale=-0.5)
                b.inc(ins, "ac")
            a1 = b.val("ac")

            @blk.vector
            def _(vector):
                b.wait(vector, "ac", a1)
                ins = vector.tensor_scalar(out_tile[:], x_ap, st_a[:, c:c + 1],
                                           st_a[:, c + 3:c + 4], ALU.subtract, ALU.mult)
                b.inc(ins, "dv")
                if colsum_to is not None:
                    t, cc = colsum_to
                    b.wait(vector, "dv", b.val("dv"))
                    ins = vector.reduce_sum(t[:, cc:cc + 1], out_tile[:], axis=AX.X)
                    b.inc(ins, "dv")
            return b.val("dv")

        tstate = {"n": 0, "hist": []}

        def transpose_blocks(tp_ps, srcs, dsts, pe_waits=(), f32=False):
            ident = ident_f_sb if f32 else ident_bf_sb
            for i, ((s_t, s_c), (d_t, d_c)) in enumerate(zip(srcs, dsts)):
                par = tstate["n"] % 2
                tstate["n"] += 1

                @blk.tensor
                def _(tensor, s_t=s_t, s_c=s_c, par=par, i=i):
                    for nm, v in (pe_waits if i == 0 else ()):
                        b.wait(tensor, nm, v)
                    if len(tstate["hist"]) >= 2:
                        b.wait(tensor, "dv", tstate["hist"][-2])
                    ins = tensor.matmul(tp_ps[:, par * 512:par * 512 + 128],
                                        s_t[:, s_c:s_c + 128], ident[:],
                                        start=True, stop=True)
                    b.inc(ins, "pe")
                pv = b.val("pe")

                @blk.vector
                def _(vector, d_t=d_t, d_c=d_c, par=par, pv=pv):
                    b.wait(vector, "pe", pv)
                    ins = vector.tensor_copy(d_t[:, d_c:d_c + 128],
                                             tp_ps[:, par * 512:par * 512 + 128])
                    tstate["hist"].append(b.inc(ins, "dv"))
            return b.val("dv")

        def barrier(dma_waits=()):
            for engname in ("tensor", "vector", "scalar"):
                @getattr(blk, engname)
                def _(eng):
                    ins = eng.nop()
                    b.inc(ins, "bar")
            t = b.val("bar")
            for engname in ("tensor", "vector", "scalar", "sync", "gpsimd"):
                @getattr(blk, engname)
                def _(eng):
                    b.wait(eng, "bar", t)
                    for nm, v in dma_waits:
                        b.wait(eng, nm, v)

        def emit_dbg(name):
            if name not in DBG:
                return
            out = nc.dram_tensor("dbg_" + name, [TOK, D], F32, kind="ExternalOutput").ap()
            dbg_outs[name] = out
            wv = b.val("dv")

            @blk.gpsimd
            def _(gpsimd):
                b.wait(gpsimd, "dv", wv)
                for tb in range(NTB):
                    ins = gpsimd.dma_start(out=out[tb * 128:(tb + 1) * 128, :],
                                           in_=x_res[:, tb * D:(tb + 1) * D])
                    b.inc(ins, "dbg", 16)

        pa = contextlib.ExitStack()
        q_fm = sball("q_fm", [128, NP * T], BF16, pa)
        k_fm = sball("k_fm", [128, NP * T], BF16, pa)
        v_tm = sball("v_tm", [128, NP * 32 * 128], BF16, pa)
        tp_ps = psall("tp_psA", [128, 1024], pa)

        pa1 = contextlib.ExitStack()
        lnx_fm = sball("lnx_fm", [128, NDC * 512], BF16, pa1)
        v_fms = sball("v_fms", [128, NP * 512], BF16, pa1)
        x0t = [sball(f"x0t{i}", [128, D], F32, pa1) for i in range(2)]
        sqt_a = sball("sqt_a", [128, D], BF16, pa1)
        lnq_tm = [sball(f"lnq{i}", [128, D], BF16, pa1) for i in range(2)]
        aring = [sball(f"aring{i}", [128, NDC * 128], BF16, pa1) for i in range(2)]
        pj_ps = psall("pj_psA", [128, 2048], pa1)

        x0_free = [0, 0]
        lnq_free = [0, 0]
        aring_free = []
        pj_free = []
        a2_done = [0]

        for sl in range(8):
            for tt in range(4):
                tile = sl * 4 + tt
                par = tile % 2

                @blk.scalar
                def _(scalar, tile=tile, par=par):
                    b.wait(scalar, "dv", x0_free[par])
                    ins = scalar.dma_start(out=x0t[par][:],
                                           in_=X0[tile * 128:(tile + 1) * 128, :])
                    b.inc(ins, "x0w%d" % par, 16)
                xv = b.val("x0w%d" % par)

                @blk.vector
                def _(vector, par=par, xv=xv):
                    b.wait(vector, "x0w%d" % par, xv)
                    b.wait(vector, "pe", lnq_free[par])
                lv = ln_tile(x0t[par][:], lnq_tm[par], par, sqt_a,
                             x_gate=("x0w%d" % par, xv))
                x0_free[par] = lv
                transpose_blocks(
                    tp_ps,
                    [(lnq_tm[par], dc * 128) for dc in range(NDC)],
                    [(lnx_fm, dc * 512 + tt * 128) for dc in range(NDC)],
                    pe_waits=[("dv", lv), ("pe", a2_done[0])],
                )

                lnq_free[par] = b.val("pe")
            slab_tp = b.val("dv")

            for iw, key in enumerate(["wqa", "wka", "wva"]):
                for oc in range(NP):
                    li = len(aring_free)
                    slot = li % 2

                    @blk.scalar
                    def _(scalar, key=key, oc=oc, li=li, slot=slot):
                        for nm, v in prep_done[key]:
                            b.wait(scalar, nm, v)
                        if li >= 2:
                            b.wait(scalar, "pe", aring_free[li - 2])
                        ins = scalar.dma_start(
                            out=aring[slot][:].rearrange("p (a c) -> p a c", c=128),
                            in_=wT[key].ap()[0:D, oc * 128:(oc + 1) * 128]
                                .rearrange("(a p) c -> p a c", p=128))
                        b.inc(ins, "aw%d" % slot, 16)
                    wv = b.val("aw%d" % slot)
                    gi = sl * 12 + iw * NP + oc
                    bank = gi % 4

                    @blk.tensor
                    def _(tensor, slot=slot, gi=gi, bank=bank, wv=wv, slab_tp=slab_tp):
                        b.wait(tensor, "aw%d" % slot, wv)
                        b.wait(tensor, "dv", slab_tp)
                        if gi >= 4:
                            b.wait(tensor, "dv", pj_free[gi - 4])
                        for dc in range(NDC):
                            ins = tensor.matmul(
                                pj_ps[:, bank * 512:(bank + 1) * 512],
                                aring[slot][:, dc * 128:(dc + 1) * 128],
                                lnx_fm[:, dc * 512:(dc + 1) * 512],
                                start=(dc == 0), stop=(dc == NDC - 1))
                        b.inc(ins, "pe")
                    aring_free.append(b.val("pe"))
                    mv = b.val("pe")

                    @blk.vector
                    def _(vector, iw=iw, oc=oc, sl=sl, bank=bank, mv=mv):
                        b.wait(vector, "pe", mv)
                        dst = [q_fm, k_fm, v_fms][iw]
                        col = (oc * T + sl * 512) if iw < 2 else (oc * 512)
                        if iw == 0:
                            ins = vector.tensor_scalar_mul(
                                dst[:, col:col + 512],
                                pj_ps[:, bank * 512:(bank + 1) * 512],
                                1.0 / np.sqrt(float(DH)))
                        else:
                            ins = vector.tensor_copy(dst[:, col:col + 512],
                                                     pj_ps[:, bank * 512:(bank + 1) * 512])
                        pj_free.append(b.inc(ins, "dv"))
            vv = b.val("dv")
            transpose_blocks(
                tp_ps,
                [(v_fms, p * 512 + kk * 128) for p in range(NP) for kk in range(4)],
                [(v_tm, (p * 32 + (sl * 4 + kk)) * 128) for p in range(NP)
                 for kk in range(4)],
                pe_waits=[("dv", vv)],
            )

            a2_done[0] = b.val("pe")

        pa1.close()
        import os as _os
        if _os.environ.get("KMEM"):
            print("after A2 close:", nc.sbuf_bytes_remaining)
        flash_dv0 = b.val("dv")

        s_ps = [psall("s_ps0", [128, 1024], pa), psall("s_ps1", [128, 1024], pa)]
        pv_ps = psall("pv_ps", [128, 512], pa)
        rb_ps = psall("rb_ps", [128, 512], pa)
        p_sb = [sball(f"p_sb{i}", [128, T], BF16, pa) for i in range(4)]
        mx4_sb = sball("mx4_sb", [128, 8], F32, pa)
        sacc_sb = sball("sacc_sb", [128, 8], F32, pa)
        rsum_sb = sball("rsum_sb", [128, 1], F32, pa)
        pt_sb = sball("pt_sb", [128, 32 * 512], BF16, pa)
        ao_sb = [sball(f"ao_sb{i}", [128, 512], BF16, pa) for i in range(2)]
        sums_sb = sball("sums_sb", [128, 32], F32, pa)
        nmax_sb = sball("nmax_sb", [128, 4], F32, pa)
        rrow_sb = sball("rrow_sb", [1, 512], F32, pa)
        bcast_sb = sball("bcast_sb", [128, 512], F32, pa)

        s_use = []
        exp_v = []
        sum_dv = []
        ptdone_v = []
        ptev_v = []
        pv_done = []
        pv_ev = []
        ag_wr = []
        rr_seq = []
        rr_pe = []

        for p in range(NP):
            qc = p * T
            for qs in range(8):
                gqs = p * 8 + qs
                for qb in range(4):
                    qi = qs * 4 + qb
                    m = p * 32 + qi
                    par = qi % 2
                    for j4 in range(4):
                        n = m * 8 + j4
                        grp = n % 2

                        @blk.tensor
                        def _(tensor, qi=qi, j4=j4, grp=grp, n=n, qc=qc):
                            if n >= 2:
                                nm, v = s_use[n - 2]
                                b.wait(tensor, nm, v)
                            else:
                                b.wait(tensor, "dv", flash_dv0)
                            for jj in range(2):
                                kchunk = j4 * 2 + jj
                                ins = tensor.matmul(
                                    s_ps[grp][:, jj * 512:(jj + 1) * 512],
                                    q_fm[:, qc + qi * 128:qc + (qi + 1) * 128],
                                    k_fm[:, qc + kchunk * 512:qc + (kchunk + 1) * 512],
                                    start=True, stop=True)
                            b.inc(ins, "pe")
                        sv = b.val("pe")

                        @blk.vector
                        def _(vector, sv=sv, grp=grp, par=par, j4=j4, m=m):
                            b.wait(vector, "pe", sv)
                            if m >= 2 and j4 == 0:
                                b.wait(vector, "ac", exp_v[m - 2])
                            ins = vector.reduce_max(mx4_sb[:, par * 4 + j4:par * 4 + j4 + 1],
                                                    s_ps[grp][:], axis=AX.X)
                            s_use.append(("dv", b.inc(ins, "dv")))
                    mxs = b.val("dv")

                    @blk.vector
                    def _(vector, par=par, qb=qb, m=m, mxs=mxs):
                        b.wait(vector, "dv", mxs)
                        if m >= 4:
                            b.wait(vector, "ac", exp_v[m - 4])
                        ins = vector.reduce_max(nmax_sb[:, qb:qb + 1],
                                                mx4_sb[:, par * 4:par * 4 + 4],
                                                axis=AX.X, negate=True)
                        b.inc(ins, "dv")
                    mxv = b.val("dv")

                    for j4 in range(4):
                        n = m * 8 + 4 + j4
                        grp = n % 2

                        @blk.tensor
                        def _(tensor, qi=qi, j4=j4, grp=grp, n=n, qc=qc):
                            nm, v = s_use[n - 2]
                            b.wait(tensor, nm, v)
                            for jj in range(2):
                                kchunk = j4 * 2 + jj
                                ins = tensor.matmul(
                                    s_ps[grp][:, jj * 512:(jj + 1) * 512],
                                    q_fm[:, qc + qi * 128:qc + (qi + 1) * 128],
                                    k_fm[:, qc + kchunk * 512:qc + (kchunk + 1) * 512],
                                    start=True, stop=True)
                            b.inc(ins, "pe")
                        sv = b.val("pe")

                        @blk.scalar
                        def _(scalar, sv=sv, grp=grp, par=par, qb=qb, qi=qi, j4=j4,
                              m=m, gqs=gqs, mxv=mxv):
                            b.wait(scalar, "pe", sv)
                            if j4 == 0:
                                b.wait(scalar, "dv", mxv)
                                if gqs >= 1:
                                    b.wait(scalar, "pe", ptdone_v[gqs - 1])
                                if m >= 2:
                                    b.wait(scalar, "dv", sum_dv[m - 2])
                            ins = scalar.activation(
                                p_sb[qb][:, j4 * 1024:(j4 + 1) * 1024], s_ps[grp][:],
                                AF.Exp, bias=nmax_sb[:, qb:qb + 1],
                                accum_out=sacc_sb[:, par * 4 + j4:par * 4 + j4 + 1])
                            s_use.append(("ac", b.inc(ins, "ac")))
                    exp_v.append(b.val("ac"))
                    ev = exp_v[-1]

                    @blk.vector
                    def _(vector, par=par, qi=qi, p=p, ev=ev):
                        b.wait(vector, "ac", ev)
                        if p >= 1:
                            b.wait(vector, "pe", rr_pe[(p - 1) * 8 + qi // 4])
                        ins = vector.reduce_sum(rsum_sb[:], sacc_sb[:, par * 4:par * 4 + 4],
                                                axis=AX.X)
                        b.inc(ins, "dv")
                        b.wait(vector, "dv", b.val("dv"))
                        ins = vector.reciprocal(sums_sb[:, qi:qi + 1], rsum_sb[:])
                        sum_dv.append(b.inc(ins, "dv"))

                ev = exp_v[-1]
                for kc in range(32):
                    t = gqs * 32 + kc
                    par2 = kc % 2

                    @blk.tensor
                    def _(tensor, kc=kc, par2=par2, ev=ev, t=t):
                        if kc == 0:
                            b.wait(tensor, "ac", ev)
                        if t >= 2:
                            b.wait(tensor, "dv", ptev_v[t - 2])
                        for qb in range(4):
                            ins = tensor.matmul(
                                tp_ps[:, par2 * 512 + qb * 128:par2 * 512 + (qb + 1) * 128],
                                p_sb[qb][:, kc * 128:(kc + 1) * 128], ident_bf_sb[:],
                                start=True, stop=True)
                        b.inc(ins, "pe")
                    tv = b.val("pe")

                    @blk.vector
                    def _(vector, kc=kc, par2=par2, tv=tv, gqs=gqs):
                        b.wait(vector, "pe", tv)
                        if gqs >= 1 and kc == 0:
                            b.wait(vector, "pe", pv_done[gqs - 1])
                        ins = vector.tensor_copy(pt_sb[:, kc * 512:(kc + 1) * 512],
                                                 tp_ps[:, par2 * 512:(par2 + 1) * 512])
                        ptev_v.append(b.inc(ins, "dv"))

                ptdone_v.append(b.val("pe"))
                ptv = ptev_v[-1]

                @blk.tensor
                def _(tensor, p=p, gqs=gqs, ptv=ptv):
                    b.wait(tensor, "dv", ptv)
                    if gqs >= 1:
                        b.wait(tensor, "dv", pv_ev[gqs - 1])
                    for kc in range(32):
                        ins = tensor.matmul(
                            pv_ps[:],
                            v_tm[:, (p * 32 + kc) * 128:(p * 32 + kc + 1) * 128],
                            pt_sb[:, kc * 512:(kc + 1) * 512],
                            start=(kc == 0), stop=(kc == 31))
                    b.inc(ins, "pe")
                pv_done.append(b.val("pe"))
                sdv = sum_dv[-1]

                @blk.tensor
                def _(tensor, qs=qs, gqs=gqs, sdv=sdv):
                    b.wait(tensor, "dv", sdv)
                    if gqs >= 1:
                        b.wait(tensor, "dv", rr_seq[gqs - 1])
                    for qb in range(4):
                        qi = qs * 4 + qb
                        ins = tensor.matmul(rb_ps[0:1, qb * 128:(qb + 1) * 128],
                                            sums_sb[:, qi:qi + 1], ident_f_sb[:],
                                            start=True, stop=True)
                    b.inc(ins, "pe")
                rr_pe.append(b.val("pe"))
                rv = rr_pe[-1]

                @blk.vector
                def _(vector, rv=rv):
                    b.wait(vector, "pe", rv)
                    ins = vector.tensor_copy(rrow_sb[:], rb_ps[0:1, 0:512])
                    b.inc(ins, "dv")
                rev = b.val("dv")

                @blk.tensor
                def _(tensor, rev=rev):
                    b.wait(tensor, "dv", rev)
                    ins = tensor.matmul(rb_ps[:, 0:512], ones_row_sb[:], rrow_sb[:],
                                        start=True, stop=True)
                    b.inc(ins, "pe")
                bv = b.val("pe")
                pvv = pv_done[-1]

                @blk.vector
                def _(vector, bv=bv, pvv=pvv, gqs=gqs):
                    b.wait(vector, "pe", bv)
                    ins = vector.tensor_copy(bcast_sb[:], rb_ps[:, 0:512])
                    rr_seq.append(b.inc(ins, "dv"))
                    b.wait(vector, "pe", pvv)
                    b.wait(vector, "dv", b.val("dv"))
                    if gqs >= 2:
                        b.wait(vector, "agw", b.val("agw"))
                    ins = vector.tensor_mul(ao_sb[gqs % 2][:], pv_ps[:], bcast_sb[:])
                    pv_ev.append(b.inc(ins, "dv"))
                aov = pv_ev[-1]

                @blk.scalar
                def _(scalar, p=p, qs=qs, gqs=gqs, aov=aov):
                    b.wait(scalar, "dv", aov)
                    ins = scalar.dma_start(
                        out=ag_in.ap()[p * 128:(p + 1) * 128, qs * 512:(qs + 1) * 512],
                        in_=ao_sb[gqs % 2][:])
                    ag_wr.append(b.inc(ins, "agw", 16))

        flash_done = pv_ev[-1]
        ag_all = ag_wr[-1]
        pa.close()
        for key, pk in zip(["g", "u", "d", "qwb", "kwb", "vwb", "owb"],
                           ["wva", "g", "u", "d", "qwb", "kwb", "vwb"]):
            prep_wT(key, pk)
        preps_all = [it for k in ["wqa", "wka", "wva", "g", "u", "d", "qwb",
                                  "kwb", "vwb", "owb"] for it in prep_done[k]]
        wstg_st.close()
        import os as _os2
        if _os2.environ.get("KMEM"):
            print("after flash close:", nc.sbuf_bytes_remaining)

        @blk.gpsimd
        def _(gpsimd):
            b.wait(gpsimd, "agw", ag_all)
            ins = gpsimd.collective_compute(
                "AllGather", ALU.bypass, RG, [ag_in.ap().opt()], [ag_out.ap().opt()])
            b.inc(ins, "coll")
        coll_ag = b.val("coll")

        x_res = sball("x_res", [128, NTB * D], F32)
        pa6 = contextlib.ExitStack()
        og_sb = sball("og_sb", [128, NH * 1024], BF16, pa6)
        woT_sb = sball("woT_sb", [128, NH * D], BF16, pa6)
        x0s = [sball(f"x0s{i}", [128, D], F32, pa6) for i in range(2)]
        op_ps = psall("op_psA", [128, 2048], pa6)

        @blk.sync
        def _(sync):
            for nm, v in preps_all:
                b.wait(sync, nm, v)
            b.wait(sync, "ca_woa", 16)
            b.wait(sync, "dv", flash_done)
            b.wait(sync, "agw", ag_all)
            for h in range(NH):
                ins = sync.dma_start(out=woT_sb[:, h * D:(h + 1) * D],
                                     in_=wbf["woa"].ap()[0:D, h * 128:(h + 1) * 128],
                                     transpose=True)
                b.inc(ins, "wow", 16)
        woT_v = b.val("wow")

        @blk.gpsimd
        def _(gpsimd):
            b.wait(gpsimd, "coll", coll_ag)
            pid = gpsimd.partition_id()
            to0v = gpsimd.compute_val((pid % 4) * TOK)
            for h in range(NH):
                ins = gpsimd.dma_start(
                    out=og_sb[:, h * 1024:(h + 1) * 1024],
                    in_=ag_out.ap()[h * 128:(h + 1) * 128, bass.ds(to0v, TOK)])
                b.inc(ins, "ogw", 16)
        og_v = b.val("ogw")

        op_free = []
        for tb in range(NTB):
            par = tb % 2

            @blk.scalar
            def _(scalar, tb=tb, par=par):
                b.wait(scalar, "dv", flash_done)
                b.wait(scalar, "agw", ag_all)
                if tb >= 2:
                    b.wait(scalar, "dv", op_free[tb - 2])
                ins = scalar.dma_start(out=x0s[par][:],
                                       in_=X0M[tb * 128:(tb + 1) * 128, :])
                b.inc(ins, "xsw%d" % par, 16)
            xv = b.val("xsw%d" % par)

            @blk.tensor
            def _(tensor, tb=tb, xv=xv):
                if tb == 0:
                    b.wait(tensor, "ogw", og_v)
                    b.wait(tensor, "wow", woT_v)
                if tb >= 1:
                    b.wait(tensor, "dv", op_free[tb - 1])
                for h in range(NH):
                    for ods in range(4):
                        ins = tensor.matmul(
                            op_ps[:, ods * 512:(ods + 1) * 512],
                            og_sb[:, h * 1024 + tb * 128:h * 1024 + (tb + 1) * 128],
                            woT_sb[:, h * D + ods * 512:h * D + (ods + 1) * 512],
                            start=(h == 0), stop=(h == NH - 1))
                b.inc(ins, "pe")
            ov = b.val("pe")

            @blk.vector
            def _(vector, tb=tb, par=par, ov=ov, xv=xv):
                b.wait(vector, "pe", ov)
                b.wait(vector, "xsw%d" % par, xv)
                for ods in range(4):
                    ins = vector.tensor_add(
                        x_res[:, tb * D + ods * 512:tb * D + (ods + 1) * 512],
                        op_ps[:, ods * 512:(ods + 1) * 512],
                        x0s[par][:, ods * 512:(ods + 1) * 512])
                op_free.append(b.inc(ins, "dv"))
        pa6.close()
        import os as _os3
        if _os3.environ.get("KMEM"):
            print("after A6 close:", nc.sbuf_bytes_remaining)
        emit_dbg("x1")

        def emit_ffn(tag):
            ps = contextlib.ExitStack()
            sb = lambda n, s, d=BF16: sball(n + tag, s, d, ps)
            lnt = [sb(f"lnt{i}", [128, D]) for i in range(2)]
            sqt = sb("sqt", [128, D])
            lnf = sb("lnf", [128, NDC * 1024])
            cs_cols = sb("cs_cols", [128, NTB], F32)
            csr = sb("csr", [1, 1024], F32)
            cs_bc = sb("cs_bc", [128, 1024], F32)
            cshr = sb("cshr", [1, 512], F32)
            csh_bc = sb("csh_bc", [128, 512], F32)
            h_fm = sb("h_fm", [128, NHC * 512])
            tg_sc = [sb(f"tg{i}", [128, 512], F32) for i in range(2)]
            tu_sc = [sb(f"tu{i}", [128, 512], F32) for i in range(2)]
            gsig = [sb(f"gsig{i}", [128, 512]) for i in range(2)]
            dnf = sb("dnf", [128, 2048], F32)
            wring = [sb(f"wring{i}", [128, NDC * 128]) for i in range(3)]
            dring = [sb(f"dring{i}", [128, (NHC // 2) * 128]) for i in range(2)]
            tp2 = psall("tp2" + tag, [128, 1024], ps)
            gu_ps = psall("gu" + tag, [128, 2048], ps)
            dn_ps = psall("dn" + tag, [128, 1024], ps)

            lnt_free = [0, 0]
            for tb in range(NTB):
                par = tb % 2

                @blk.vector
                def _(vector, par=par):
                    b.wait(vector, "pe", lnt_free[par])
                lv = ln_tile(x_res[:, tb * D:(tb + 1) * D], lnt[par], par, sqt,
                             colsum_to=(cs_cols, tb))
                transpose_blocks(
                    tp2,
                    [(lnt[par], dc * 128) for dc in range(NDC)],
                    [(lnf, dc * 1024 + tb * 128) for dc in range(NDC)],
                    pe_waits=[("dv", lv)],
                )
                lnt_free[par] = b.val("pe")
            lnf_v = b.val("dv")

            @blk.tensor
            def _(tensor, lnf_v=lnf_v):
                b.wait(tensor, "dv", lnf_v)
                for tb in range(NTB):
                    ins = tensor.matmul(gu_ps[0:1, tb * 128:(tb + 1) * 128],
                                        cs_cols[:, tb:tb + 1], ident_f_sb[:],
                                        start=True, stop=True)
                b.inc(ins, "pe")
            csv = b.val("pe")

            @blk.vector
            def _(vector, csv=csv):
                b.wait(vector, "pe", csv)
                ins = vector.tensor_copy(csr[:], gu_ps[0:1, 0:1024])
                b.inc(ins, "dv")
            csrv = b.val("dv")

            @blk.tensor
            def _(tensor, csrv=csrv):
                b.wait(tensor, "dv", csrv)
                tensor.matmul(gu_ps[:, 0:512], row128_sb[:], csr[0:1, 0:512],
                              start=True, stop=True)
                ins = tensor.matmul(gu_ps[:, 512:1024], row128_sb[:], csr[0:1, 512:1024],
                                    start=True, stop=True)
                b.inc(ins, "pe")
            csbv = b.val("pe")

            @blk.vector
            def _(vector, csbv=csbv):
                b.wait(vector, "pe", csbv)
                ins = vector.tensor_copy(cs_bc[:], gu_ps[:, 0:1024])
                b.inc(ins, "dv")
            csbc_v = b.val("dv")

            wring_free = []
            dring_free = []
            gu_free = []
            dn_free = []
            th_gate_wait = csbc_v
            for th in range(2):
                for oc in range(NHC):
                    idx = th * NHC + oc
                    for iw, key in enumerate(["g", "u"]):
                        li = idx * 2 + iw
                        slot = li % 3

                        @blk.sync
                        def _(sync, key=key, oc=oc, li=li, slot=slot):
                            for nm, v in prep_done[key]:
                                b.wait(sync, nm, v)
                            if li >= 3:
                                b.wait(sync, "pe", wring_free[li - 3])
                            ins = sync.dma_start(
                                out=wring[slot][:].rearrange("p (a c) -> p a c", c=128),
                                in_=wT[key].ap()[0:D, oc * 128:(oc + 1) * 128]
                                    .rearrange("(a p) c -> p a c", p=128))
                            b.inc(ins, "fw%d" % slot, 16)
                    wv = [b.val("fw%d" % ((idx * 2 + iw) % 3)) for iw in range(2)]
                    par = idx % 2

                    @blk.tensor
                    def _(tensor, th=th, idx=idx, par=par, wv=wv,
                          th_gate_wait=th_gate_wait):
                        for iw in range(2):
                            b.wait(tensor, "fw%d" % ((idx * 2 + iw) % 3), wv[iw])
                        if idx % NHC == 0:
                            b.wait(tensor, "dv", th_gate_wait)
                        if idx >= 2:
                            b.wait(tensor, "dv", gu_free[idx - 2])
                        for iw in range(2):
                            slot = (idx * 2 + iw) % 3
                            bank = iw * 2 + par
                            for dc in range(NDC):
                                ins = tensor.matmul(
                                    gu_ps[:, bank * 512:(bank + 1) * 512],
                                    wring[slot][:, dc * 128:(dc + 1) * 128],
                                    lnf[:, dc * 1024 + th * 512:dc * 1024 + (th + 1) * 512],
                                    start=(dc == 0), stop=(dc == NDC - 1))
                            b.inc(ins, "pe")
                        wring_free.append(b.val("pe"))
                        wring_free.append(b.val("pe"))
                    gv = b.val("pe")

                    @blk.vector
                    def _(vector, oc=oc, th=th, par=par, gv=gv):
                        b.wait(vector, "pe", gv)
                        ins = vector.scalar_tensor_tensor(
                            tg_sc[par][:], gu_ps[:, par * 512:(par + 1) * 512],
                            1.0, cs_bc[:, th * 512:(th + 1) * 512],
                            ALU.mult, ALU.subtract)
                        b.inc(ins, "dv")
                        b.wait(vector, "dv", b.val("dv"))
                        ins = vector.tensor_scalar_mul(tg_sc[par][:], tg_sc[par][:],
                                                       sg_sb[:, oc:oc + 1])
                        b.inc(ins, "dv")
                        ins = vector.tensor_sub(tu_sc[par][:],
                                                gu_ps[:, (2 + par) * 512:(3 + par) * 512],
                                                cs_bc[:, th * 512:(th + 1) * 512])
                        b.inc(ins, "dv")
                    sv = b.val("dv")

                    @blk.scalar
                    def _(scalar, par=par, sv=sv):
                        b.wait(scalar, "dv", sv)
                        ins = scalar.activation(gsig[par][:], tg_sc[par][:], AF.Silu)
                        b.inc(ins, "ac")
                    av = b.val("ac")

                    @blk.vector
                    def _(vector, oc=oc, par=par, av=av):
                        b.wait(vector, "ac", av)
                        ins = vector.scalar_tensor_tensor(
                            h_fm[:, oc * 512:(oc + 1) * 512], tu_sc[par][:],
                            su_sb[:, oc:oc + 1], gsig[par][:], ALU.mult, ALU.mult)
                        gu_free.append(b.inc(ins, "dv"))
                h_v = gu_free[-1]

                @blk.tensor
                def _(tensor, h_v=h_v):
                    b.wait(tensor, "dv", h_v)
                    for hc in range(NHC):
                        ins = tensor.matmul(gu_ps[0:1, 0:512], ones_col_sb[:],
                                            h_fm[:, hc * 512:(hc + 1) * 512],
                                            start=(hc == 0), stop=(hc == NHC - 1))
                    b.inc(ins, "pe")
                chv = b.val("pe")

                @blk.vector
                def _(vector, chv=chv):
                    b.wait(vector, "pe", chv)
                    ins = vector.tensor_scalar_mul(cshr[:], gu_ps[0:1, 0:512], 128.0)
                    b.inc(ins, "dv")
                chrv = b.val("dv")

                @blk.tensor
                def _(tensor, chrv=chrv):
                    b.wait(tensor, "dv", chrv)
                    ins = tensor.matmul(gu_ps[:, 512:1024], ones_row_sb[:], cshr[:],
                                        start=True, stop=True)
                    b.inc(ins, "pe")
                chbv = b.val("pe")

                @blk.vector
                def _(vector, chbv=chbv):
                    b.wait(vector, "pe", chbv)
                    ins = vector.tensor_copy(csh_bc[:], gu_ps[:, 512:1024])
                    b.inc(ins, "dv")
                cshbc_v = b.val("dv")
                th_gate_wait = cshbc_v

                HH = NHC // 2
                for dimc in range(NDC):
                    di = th * NDC + dimc
                    bank = di % 2
                    dwvs = []
                    for hf in range(2):
                        hi = di * 2 + hf

                        @blk.sync
                        def _(sync, dimc=dimc, hf=hf, hi=hi):
                            for nm, v in prep_done["d"]:
                                b.wait(sync, nm, v)
                            if hi >= 2:
                                b.wait(sync, "pe", dring_free[hi - 2])
                            ins = sync.dma_start(
                                out=dring[hf][:].rearrange("p (a c) -> p a c", c=128),
                                in_=wT["d"].ap()[hf * HH * 128:(hf + 1) * HH * 128,
                                                 dimc * 128:(dimc + 1) * 128]
                                    .rearrange("(a p) c -> p a c", p=128))
                            b.inc(ins, "dw%d" % hf, 16)
                        dwvs.append(b.val("dw%d" % hf))

                    for hf in range(2):
                        @blk.tensor
                        def _(tensor, di=di, hf=hf, bank=bank, dwv=dwvs[hf],
                              cshbc_v=cshbc_v):
                            b.wait(tensor, "dw%d" % hf, dwv)
                            if di % NDC == 0 and hf == 0:
                                b.wait(tensor, "dv", cshbc_v)
                            if di >= 2 and hf == 0:
                                b.wait(tensor, "dv", dn_free[di - 2])
                            for hc2 in range(HH):
                                hc = hf * HH + hc2
                                ins = tensor.matmul(
                                    dn_ps[:, bank * 512:(bank + 1) * 512],
                                    dring[hf][:, hc2 * 128:(hc2 + 1) * 128],
                                    h_fm[:, hc * 512:(hc + 1) * 512],
                                    start=(hc == 0), stop=(hc == NHC - 1))
                            b.inc(ins, "pe")
                            dring_free.append(b.val("pe"))
                    dmv = b.val("pe")

                    @blk.vector
                    def _(vector, dimc=dimc, bank=bank, dmv=dmv):
                        b.wait(vector, "pe", dmv)
                        ins = vector.tensor_sub(dnf[:, bank * 1024:bank * 1024 + 512],
                                                dn_ps[:, bank * 512:(bank + 1) * 512],
                                                csh_bc[:])
                        b.inc(ins, "dv")
                        b.wait(vector, "dv", b.val("dv"))
                        ins = vector.tensor_scalar_mul(
                            dnf[:, bank * 1024 + 512:(bank + 1) * 1024],
                            dnf[:, bank * 1024:bank * 1024 + 512],
                            sd_sb[:, dimc:dimc + 1])
                        dn_free.append(b.inc(ins, "dv"))
                    dfv = b.val("dv")

                    for j in range(4):
                        tb = th * 4 + j
                        tpar = tstate["n"] % 2
                        tstate["n"] += 1

                        @blk.tensor
                        def _(tensor, j=j, tpar=tpar, dfv=dfv, bank=bank):
                            if len(tstate["hist"]) >= 2:
                                b.wait(tensor, "dv", tstate["hist"][-2])
                            if j == 0:
                                b.wait(tensor, "dv", dfv)
                            ins = tensor.matmul(
                                tp2[:, tpar * 512:tpar * 512 + 128],
                                dnf[:, bank * 1024 + 512 + j * 128:
                                    bank * 1024 + 512 + (j + 1) * 128],
                                ident_f_sb[:], start=True, stop=True)
                            b.inc(ins, "pe")
                        tpv = b.val("pe")

                        @blk.vector
                        def _(vector, tb=tb, dimc=dimc, tpar=tpar, tpv=tpv):
                            b.wait(vector, "pe", tpv)
                            ins = vector.tensor_add(
                                x_res[:, tb * D + dimc * 128:tb * D + (dimc + 1) * 128],
                                x_res[:, tb * D + dimc * 128:tb * D + (dimc + 1) * 128],
                                tp2[:, tpar * 512:tpar * 512 + 128])
                            tstate["hist"].append(b.inc(ins, "dv"))
            ps.close()

        def emit_attn_b():
            psc = contextlib.ExitStack()
            sb = lambda n, s, d=BF16: sball(n, s, d, psc)
            tpB = psall("tpB", [128, 1024], psc)
            pjB = psall("pjB", [128, 2048], psc)
            smB = psall("smB", [128, 1024], psc)

            qp_fm = sball("qp_fm", [128, NDC * 1024], BF16, psc)
            at_fm = sball("at_fm", [128, NDC * 512], BF16, psc)
            ps1b = contextlib.ExitStack()
            kp_fm = sball("kp_fm", [128, NDC * 1024], BF16, ps1b)
            v_fm = sball("bv_fm", [128, NDC * 1024], BF16, ps1b)
            ps1 = contextlib.ExitStack()
            lnt = [sball("blnt0", [128, D], BF16, ps1)]
            lnf = sball("blnf", [128, NDC * 512], BF16, ps1)
            t_el = [sball("t_el0", [128, 512], BF16, ps1)]
            wringB = [sball(f"wringB{i}", [128, NDC * 128], BF16, ps1) for i in range(2)]

            lnt_free = [0]
            wring_free = []
            pj_free2 = []
            el_ac = []
            b_slab_done = [0]
            for sl in range(2):
                for tt in range(4):
                    tb = sl * 4 + tt

                    @blk.vector
                    def _(vector):
                        b.wait(vector, "pe", lnt_free[0])
                    lv = ln_tile(x_res[:, tb * D:(tb + 1) * D], lnt[0], tb % 2,
                                 lnt[0], scalar_gate=lnt_free[0])
                    transpose_blocks(
                        tpB,
                        [(lnt[0], dc * 128) for dc in range(NDC)],
                        [(lnf, dc * 512 + tt * 128) for dc in range(NDC)],
                        pe_waits=[("dv", lv), ("pe", b_slab_done[0])],
                    )
                    lnt_free[0] = b.val("pe")
                lnf_v = b.val("dv")

                for iw, (key, bias_t) in enumerate([("qwb", qb_sb), ("kwb", kb_sb),
                                                    ("vwb", vb_sb)]):
                    dst = [qp_fm, kp_fm, v_fm][iw]
                    for oc in range(NDC):
                        li = len(wring_free)
                        slot = li % 2

                        @blk.sync
                        def _(sync, key=key, oc=oc, li=li, slot=slot):
                            for nm, v in prep_done[key]:
                                b.wait(sync, nm, v)
                            if li >= 2:
                                b.wait(sync, "pe", wring_free[li - 2])
                            ins = sync.dma_start(
                                out=wringB[slot][:].rearrange("p (a c) -> p a c", c=128),
                                in_=wT[key].ap()[0:D, oc * 128:(oc + 1) * 128]
                                    .rearrange("(a p) c -> p a c", p=128))
                            b.inc(ins, "bw%d" % slot, 16)
                        wv = b.val("bw%d" % slot)
                        gi = len(pj_free2)
                        bank = gi % 4

                        @blk.tensor
                        def _(tensor, slot=slot, gi=gi, bank=bank, wv=wv,
                              lnf_v=lnf_v):
                            b.wait(tensor, "bw%d" % slot, wv)
                            b.wait(tensor, "dv", lnf_v)
                            if gi >= 4:
                                b.wait(tensor, "dv", pj_free2[gi - 4])
                            for dc in range(NDC):
                                ins = tensor.matmul(
                                    pjB[:, bank * 512:(bank + 1) * 512],
                                    wringB[slot][:, dc * 128:(dc + 1) * 128],
                                    lnf[:, dc * 512:(dc + 1) * 512],
                                    start=(dc == 0), stop=(dc == NDC - 1))
                            b.inc(ins, "pe")
                        wring_free.append(b.val("pe"))
                        mv = b.val("pe")

                        if iw < 2:
                            dslice = dst[:, oc * 1024 + sl * 512:oc * 1024 + (sl + 1) * 512]

                            @blk.vector
                            def _(vector, oc=oc, bank=bank, mv=mv, bias_t=bias_t):
                                b.wait(vector, "pe", mv)
                                if len(el_ac) >= 1:
                                    b.wait(vector, "ac", el_ac[-1])
                                if pj_free2:
                                    b.wait(vector, "dv", pj_free2[-1])
                                ins = vector.tensor_scalar(
                                    t_el[0][:], pjB[:, bank * 512:(bank + 1) * 512],
                                    bias_t[:, oc:oc + 1], 0.0, ALU.add, ALU.min)
                                b.inc(ins, "dv")
                            elv = b.val("dv")

                            @blk.scalar
                            def _(scalar, dslice=dslice, elv=elv):
                                b.wait(scalar, "dv", elv)
                                ins = scalar.activation(dslice, t_el[0][:], AF.Exp)
                                el_ac.append(b.inc(ins, "ac"))
                            eav = b.val("ac")

                            @blk.vector
                            def _(vector, dslice=dslice, oc=oc, bank=bank,
                                  eav=eav, bias_t=bias_t):
                                b.wait(vector, "ac", eav)
                                ins = vector.tensor_scalar(
                                    t_el[0][:], pjB[:, bank * 512:(bank + 1) * 512],
                                    bias_t[:, oc:oc + 1], 0.0, ALU.add, ALU.max)
                                b.inc(ins, "dv")
                                b.wait(vector, "dv", b.val("dv"))
                                ins = vector.tensor_add(dslice, dslice, t_el[0][:])
                                pj_free2.append(b.inc(ins, "dv"))
                        else:
                            @blk.vector
                            def _(vector, dst=dst, oc=oc, sl=sl, bank=bank, mv=mv,
                                  bias_t=bias_t):
                                b.wait(vector, "pe", mv)
                                ins = vector.tensor_scalar(
                                    dst[:, oc * 1024 + sl * 512:oc * 1024 + (sl + 1) * 512],
                                    pjB[:, bank * 512:(bank + 1) * 512],
                                    bias_t[:, oc:oc + 1], None, ALU.add)
                                pj_free2.append(b.inc(ins, "dv"))

                b_slab_done[0] = b.val("pe")
            proj_v = b.val("dv")

            ps1.close()
            km_tm = sball("km_tm", [128, 8 * 128], BF16, ps1b)
            vm_tm = sball("vm_tm", [128, 8 * 128], BF16, ps1b)
            kvt = sball("kvt", [128, 128], F32, ps1b)
            ksrow = sball("ksrow", [1, 128], F32, ps1b)
            kv_wr = []
            for h in range(NH):
                tv = transpose_blocks(
                    tpB,
                    [(kp_fm, h * 1024 + j * 128) for j in range(8)]
                    + [(v_fm, h * 1024 + j * 128) for j in range(8)],
                    [(km_tm, j * 128) for j in range(8)]
                    + [(vm_tm, j * 128) for j in range(8)],
                    pe_waits=[("dv", proj_v)] + ([("kvw", kv_wr[-1])] if h else []),
                )

                @blk.tensor
                def _(tensor, h=h, tv=tv):
                    b.wait(tensor, "dv", tv)
                    if h >= 1:
                        b.wait(tensor, "dv", kv_ev[-1])
                    for j in range(8):
                        tensor.matmul(smB[:, 0:128], km_tm[:, j * 128:(j + 1) * 128],
                                      vm_tm[:, j * 128:(j + 1) * 128],
                                      start=(j == 0), stop=(j == 7))
                    for j in range(8):
                        ins = tensor.matmul(smB[0:1, 512:640], ones_col_sb[:],
                                            km_tm[:, j * 128:(j + 1) * 128],
                                            start=(j == 0), stop=(j == 7))
                    b.inc(ins, "pe")
                kvv = b.val("pe")

                @blk.vector
                def _(vector, kvv=kvv):
                    b.wait(vector, "pe", kvv)
                    vector.tensor_copy(kvt[:], smB[:, 0:128])
                    ins = vector.tensor_copy(ksrow[:], smB[0:1, 512:640])
                    kv_ev.append(b.inc(ins, "dv"))
                kev = kv_ev[-1]

                @blk.gpsimd
                def _(gpsimd, h=h, kev=kev):
                    b.wait(gpsimd, "dv", kev)
                    ins = gpsimd.dma_start(out=ar_in.ap()[h * 128:(h + 1) * 128, :],
                                           in_=kvt[:])
                    b.inc(ins, "kvw", 16)
                    ins = gpsimd.dma_start(
                        out=ar_in.ap()[NH * DH + h:NH * DH + h + 1, :], in_=ksrow[:])
                    kv_wr.append(b.inc(ins, "kvw", 16))
            ps1b.close()

            @blk.gpsimd
            def _(gpsimd):
                b.wait(gpsimd, "kvw", kv_wr[-1])
                ins = gpsimd.collective_compute(
                    "AllReduce", ALU.add, RG, [ar_in.ap().opt()], [ar_out.ap().opt()])
                b.inc(ins, "coll")
            coll_ar = b.val("coll")

            ps2 = contextlib.ExitStack()
            kvb = sball("kvb", [128, NH * 128], BF16, ps2)
            zrow_sb = sball("zrow_sb", [1, 512], F32, ps2)
            zb_sb = sball("zb_sb", [128, 512], F32, ps2)
            ksf = sball("ksf", [128, NH], F32, ps2)
            ksb = sball("ksb", [128, NH], BF16, ps2)

            @blk.gpsimd
            def _(gpsimd):
                b.wait(gpsimd, "coll", coll_ar)
                ins = gpsimd.dma_start(
                    out=kvb[:].rearrange("p (a c) -> p a c", c=128),
                    in_=ar_out.ap()[0:NH * DH, :].rearrange("(a p) c -> p a c", p=128))
                b.inc(ins, "kvl", 16)
                ins = gpsimd.dma_start(
                    out=ksf[:],
                    in_=bass.AP(ar_out.ap().tensor, (NH * DH) * DH, [[1, 128], [128, NH]]),
                    allow_slow_non_contiguous=True)
                b.inc(ins, "kvl", 16)
            kv_ld = b.val("kvl")

            @blk.vector
            def _(vector, kv_ld=kv_ld):
                b.wait(vector, "kvl", kv_ld)
                ins = vector.tensor_copy(ksb[:], ksf[:])
                b.inc(ins, "dv")
            ksb_v = b.val("dv")

            ps3 = contextlib.ExitStack()
            owT_b = sball("owT_b", [128, NDC * D], BF16, ps3)
            ob_row = sball("ob_row", [1, D], F32, ps3)
            ob_bc = sball("ob_bc", [128, D], F32, ps3)
            t_ep = sball("t_ep", [128, 512], F32, ps3)

            @blk.sync
            def _(sync):
                for nm, v in prep_done["owb"]:
                    b.wait(sync, nm, v)
                b.wait(sync, "kvw", kv_wr[-1])
                for ec in range(NDC):
                    ins = sync.dma_start(out=owT_b[:, ec * D:(ec + 1) * D],
                                         in_=wT["owb"].ap()[ec * 128:(ec + 1) * 128, 0:D])
                    b.inc(ins, "oww", 16)
                ins = sync.dma_start(out=ob_row[:],
                                     in_=bass.AP(EXT["ob_b"].tensor, 0, [[D, 1], [1, D]]))
                b.inc(ins, "oww", 16)
            ow_v = b.val("oww")

            @blk.tensor
            def _(tensor, ow_v=ow_v):
                b.wait(tensor, "oww", ow_v)
                for j in range(4):
                    ins = tensor.matmul(pjB[:, j * 512:(j + 1) * 512], ones_row_sb[:],
                                        ob_row[0:1, j * 512:(j + 1) * 512],
                                        start=True, stop=True)
                b.inc(ins, "pe")
            obv = b.val("pe")

            @blk.vector
            def _(vector, obv=obv):
                b.wait(vector, "pe", obv)
                ins = vector.tensor_copy(ob_bc[:], pjB[:, 0:2048])
                b.inc(ins, "dv")
            ob_v = b.val("dv")

            op_free2 = []
            at_ev = []
            for th in range(2):
                for h in range(NH):
                    @blk.tensor
                    def _(tensor, h=h, th=th, ksb_v=ksb_v):
                        if th == 0 and h == 0:
                            b.wait(tensor, "dv", ksb_v)
                        if at_ev:
                            b.wait(tensor, "dv", at_ev[-1])
                        ins = tensor.matmul(
                            smB[0:1, 0:512], ksb[:, h:h + 1],
                            qp_fm[:, h * 1024 + th * 512:h * 1024 + (th + 1) * 512],
                            start=True, stop=True)
                        b.inc(ins, "pe")
                    zv = b.val("pe")

                    @blk.vector
                    def _(vector, zv=zv):
                        b.wait(vector, "pe", zv)
                        ins = vector.tensor_scalar_add(zrow_sb[:], smB[0:1, 0:512], Z_EPS)
                        b.inc(ins, "dv")
                        b.wait(vector, "dv", b.val("dv"))
                        ins = vector.reciprocal(zrow_sb[:], zrow_sb[:])
                        b.inc(ins, "dv")
                    zrv = b.val("dv")

                    @blk.tensor
                    def _(tensor, zrv=zrv):
                        b.wait(tensor, "dv", zrv)
                        ins = tensor.matmul(smB[:, 512:1024], ones_row_sb[:], zrow_sb[:],
                                            start=True, stop=True)
                        b.inc(ins, "pe")
                    zbv = b.val("pe")

                    @blk.vector
                    def _(vector, h=h, th=th, zbv=zbv):
                        b.wait(vector, "pe", zbv)
                        ins = vector.tensor_copy(zb_sb[:], smB[:, 512:1024])
                        b.inc(ins, "dv")
                        b.wait(vector, "dv", b.val("dv"))
                        ins = vector.tensor_mul(
                            qp_fm[:, h * 1024 + th * 512:h * 1024 + (th + 1) * 512],
                            qp_fm[:, h * 1024 + th * 512:h * 1024 + (th + 1) * 512],
                            zb_sb[:])
                        b.inc(ins, "dv")
                    qsc = b.val("dv")

                    @blk.tensor
                    def _(tensor, h=h, th=th, qsc=qsc):
                        b.wait(tensor, "dv", qsc)
                        ins = tensor.matmul(
                            smB[0:128, 0:512], kvb[:, h * 128:(h + 1) * 128],
                            qp_fm[:, h * 1024 + th * 512:h * 1024 + (th + 1) * 512],
                            start=True, stop=True)
                        b.inc(ins, "pe")
                    nv = b.val("pe")

                    @blk.vector
                    def _(vector, h=h, nv=nv, th=th):
                        b.wait(vector, "pe", nv)
                        if th > 0 and len(op_free2) < 4 * th and h == 0:
                            pass
                        ins = vector.tensor_copy(at_fm[:, h * 512:(h + 1) * 512],
                                                 smB[0:128, 0:512])
                        at_ev.append(b.inc(ins, "dv"))
                at_v = at_ev[-1]

                for j in range(4):
                    tb = th * 4 + j

                    @blk.tensor
                    def _(tensor, tb=tb, j=j, th=th, at_v=at_v, ob_v=ob_v):
                        if j == 0:
                            b.wait(tensor, "dv", at_v)
                            b.wait(tensor, "dv", ob_v)
                        if len(op_free2) >= 1:
                            b.wait(tensor, "dv", op_free2[-1])
                        for ec in range(NDC):
                            for ods in range(4):
                                ins = tensor.matmul(
                                    pjB[:, ods * 512:(ods + 1) * 512],
                                    at_fm[:, ec * 512 + j * 128:ec * 512 + (j + 1) * 128],
                                    owT_b[:, ec * D + ods * 512:ec * D + (ods + 1) * 512],
                                    start=(ec == 0), stop=(ec == NDC - 1))
                        b.inc(ins, "pe")
                    ov = b.val("pe")

                    @blk.vector
                    def _(vector, tb=tb, ov=ov):
                        b.wait(vector, "pe", ov)
                        for ods in range(4):
                            if ods > 0:
                                b.wait(vector, "dv", b.val("dv"))
                            ins = vector.tensor_add(t_ep[:],
                                                    pjB[:, ods * 512:(ods + 1) * 512],
                                                    ob_bc[:, ods * 512:(ods + 1) * 512])
                            b.inc(ins, "dv")
                            b.wait(vector, "dv", b.val("dv"))
                            ins = vector.tensor_add(
                                x_res[:, tb * D + ods * 512:tb * D + (ods + 1) * 512],
                                x_res[:, tb * D + ods * 512:tb * D + (ods + 1) * 512],
                                t_ep[:])
                            b.inc(ins, "dv")
                        op_free2.append(b.val("dv"))
            ps3.close()
            ps2.close()
            psc.close()

        kv_ev = []

        if STAGE >= 2:
            barrier(dma_waits=preps_all)
            emit_ffn("F1")
            emit_dbg("x2")
        if STAGE >= 3:
            barrier()
            emit_attn_b()
            emit_dbg("x3")
        if STAGE >= 4:
            barrier()
            emit_ffn("F2")

        fin = b.val("dv")

        @blk.gpsimd
        def _(gpsimd):
            b.wait(gpsimd, "dv", fin)
            for tb in range(NTB):
                ins = gpsimd.dma_start(out=Y[tb * 128:(tb + 1) * 128, :],
                                       in_=x_res[:, tb * D:(tb + 1) * D])
                b.inc(ins, "st", 16)
            b.wait(gpsimd, "st", b.val("st"))

    stack.close()
    return nc, dbg_outs


def _in_maps(inputs):
    x = np.ascontiguousarray(inputs["L0_attn_a_x"])
    maps = []
    for c in range(8):
        bb, gg = c // 4, c % 4
        h0 = 4 * gg
        maps.append({
            "x0": np.ascontiguousarray(x[bb]),
            "x0m": np.ascontiguousarray(x[bb, gg * TOK:(gg + 1) * TOK]),
            "wq_a": np.ascontiguousarray(inputs["L0_attn_a_Wq"][h0 * DH:(h0 + 4) * DH]),
            "wk_a": np.ascontiguousarray(inputs["L0_attn_a_Wk"][h0 * DH:(h0 + 4) * DH]),
            "wv_a": np.ascontiguousarray(inputs["L0_attn_a_Wv"][h0 * DH:(h0 + 4) * DH]),
            "wo_a": np.ascontiguousarray(inputs["L0_attn_a_Wo"]),
            "qw_b": np.ascontiguousarray(inputs["L0_attn_b_q_w"]),
            "kw_b": np.ascontiguousarray(inputs["L0_attn_b_k_w"]),
            "vw_b": np.ascontiguousarray(inputs["L0_attn_b_v_w"]),
            "ow_b": np.ascontiguousarray(inputs["L0_attn_b_o_w"]),
            "qb_b": np.ascontiguousarray(inputs["L0_attn_b_q_b"]),
            "kb_b": np.ascontiguousarray(inputs["L0_attn_b_k_b"]),
            "vb_b": np.ascontiguousarray(inputs["L0_attn_b_v_b"]),
            "ob_b": np.ascontiguousarray(inputs["L0_attn_b_o_b"]),
            "wg_i": np.ascontiguousarray(inputs["L0_ffn_wq_gate"]),
            "wu_i": np.ascontiguousarray(inputs["L0_ffn_wq_up"]),
            "wd_i": np.ascontiguousarray(inputs["L0_ffn_wq_down"]),
            "sg": np.ascontiguousarray(inputs["L0_ffn_s_gate"]),
            "su": np.ascontiguousarray(inputs["L0_ffn_s_up"]),
            "sd": np.ascontiguousarray(inputs["L0_ffn_s_down"]),
        })
    return maps


def kernel(**inputs):
    nc, _ = build_core_program(0)
    res = run_bass_kernel_spmd(nc, _in_maps(inputs), core_ids=list(range(8)))
    out = np.empty((B_, T, D), dtype=np.float32)
    for c in range(8):
        bb, gg = c // 4, c % 4
        out[bb, gg * TOK:(gg + 1) * TOK, :] = res.results[c]["y"]
    return out


# version 04
# speedup vs baseline: 1.1398x; 1.0139x over previous
import contextlib
import numpy as np
import ml_dtypes
import concourse.bass as bass
import concourse.mybir as mybir
from concourse.bass_utils import run_bass_kernel_spmd

F32 = mybir.dt.float32
BF16 = mybir.dt.bfloat16
I32 = mybir.dt.int32
AF = mybir.ActivationFunctionType
ALU = mybir.AluOpType
AX = mybir.AxisListType

B_, T, D = 2, 4096, 2048
HID = 5632
NH, DH = 16, 128
TOK = 1024
NTB = TOK // 128
NDC = D // 128
NHC = HID // 128
NP = 4
LN_EPS = 1e-5
Z_EPS = 1e-6

STAGE = 4
DBG = ()


class Bld:
    def __init__(self, nc, stack):
        self.nc = nc
        self.stack = stack
        self.cnt = {}
        self.sems = {}

    def sem(self, name):
        if name not in self.sems:
            self.sems[name] = self.stack.enter_context(self.nc.semaphore(name))
            self.cnt[name] = 0
        return self.sems[name]

    def inc(self, ins, name, n=1):
        ins.then_inc(self.sem(name), n)
        self.cnt[name] += n
        return self.cnt[name]

    def val(self, name):
        self.sem(name)
        return self.cnt[name]

    def wait(self, eng, name, v):
        assert 0 <= v <= self.cnt.get(name, 0), ("wait-ahead", name, v, self.cnt.get(name))
        if v > 0:
            eng.wait_ge(self.sem(name), v)


def build_core_program(core):
    nc = bass.Bass("TRN2", target_bir_lowering=False)
    g = core % 4
    to0 = g * TOK
    RG = [[0, 1, 2, 3], [4, 5, 6, 7]]

    X0 = nc.dram_tensor("x0", [T, D], F32, kind="ExternalInput").ap()
    X0M = nc.dram_tensor("x0m", [TOK, D], F32, kind="ExternalInput").ap()
    EXT = {}
    for nm, sh, dt in [("wq_a", [512, D], F32), ("wk_a", [512, D], F32),
                       ("wv_a", [512, D], F32), ("wo_a", [D, D], F32),
                       ("qw_b", [D, D], F32), ("kw_b", [D, D], F32),
                       ("vw_b", [D, D], F32), ("ow_b", [D, D], F32),
                       ("qb_b", [D], F32), ("kb_b", [D], F32), ("vb_b", [D], F32),
                       ("ob_b", [D], F32), ("wg_i", [HID, D], I32),
                       ("wu_i", [HID, D], I32), ("wd_i", [D, HID], I32),
                       ("sg", [HID], F32), ("su", [HID], F32), ("sd", [D], F32)]:
        EXT[nm] = nc.dram_tensor(nm, sh, dt, kind="ExternalInput").ap()
    Y = nc.dram_tensor("y", [TOK, D], F32, kind="ExternalOutput").ap()

    def dram(name, shape, dt=BF16):
        return nc.dram_tensor(name, shape, dt, kind="Internal")

    CAST_SRC = {"wqa": "wq_a", "wka": "wk_a", "wva": "wv_a", "woa": "wo_a",
                "g": "wg_i", "u": "wu_i", "d": "wd_i",
                "qwb": "qw_b", "kwb": "kw_b", "vwb": "vw_b", "owb": "ow_b"}
    WSHAPE = {"wqa": [512, D], "wka": [512, D], "wva": [512, D], "woa": [D, D],
              "g": [HID, D], "u": [HID, D], "d": [D, HID],
              "qwb": [D, D], "kwb": [D, D], "vwb": [D, D], "owb": [D, D]}
    wbf = {k: dram(k + "_bf", s) for k, s in WSHAPE.items()}
    wT = {k: dram(k + "_T", [s[1], s[0]]) for k, s in WSHAPE.items() if k != "woa"}
    ag_in = dram("ag_in", [NP * DH, T])
    ag_out = dram("ag_out", [NH * DH, T])
    ar_in = dram("ar_in", [NH * DH + NH, DH], F32)
    ar_out = dram("ar_out", [NH * DH + NH, DH], F32)

    ident_bf = nc.inline_tensor(np.eye(128, dtype=ml_dtypes.bfloat16), "c_ident_bf")
    ident_f = nc.inline_tensor(np.eye(128, dtype=np.float32), "c_ident_f")
    ones_row_f = nc.inline_tensor(np.ones((1, 128), dtype=np.float32), "c_ones_row")
    row128_f = nc.inline_tensor(np.full((1, 128), 128.0, dtype=np.float32), "c_row128")
    one11_f = nc.inline_tensor(np.ones((1, 1), dtype=np.float32), "c_one11")
    ones_col_bf = nc.inline_tensor(np.ones((128, 1), dtype=ml_dtypes.bfloat16), "c_ones_col")
    eps_col_f = nc.inline_tensor(np.full((128, 1), LN_EPS, dtype=np.float32), "c_eps_col")

    stack = contextlib.ExitStack()
    b = Bld(nc, stack)
    bko_ev = []

    def sball(name, shape, dtype=BF16, st=None):
        return (st or stack).enter_context(nc.sbuf_tensor(name, shape, dtype))

    def psall(name, shape, st=None, dtype=F32):
        return (st or stack).enter_context(nc.psum_tensor(name, shape, dtype))

    ident_bf_sb = sball("ident_bf_sb", [128, 128])
    ident_f_sb = sball("ident_f_sb", [128, 128], F32)
    ones_row_sb = sball("ones_row_sb", [1, 128], F32)
    row128_sb = sball("row128_sb", [1, 128], F32)
    one11_sb = sball("one11_sb", [1, 1], F32)
    ones_col_sb = sball("ones_col_sb", [128, 1])
    eps_sb = sball("eps_sb", [128, 1], F32)
    qb_sb = sball("qb_sb", [128, 16], F32)
    kb_sb = sball("kb_sb", [128, 16], F32)
    vb_sb = sball("vb_sb", [128, 16], F32)
    sg_sb = sball("sg_sb", [128, NHC], F32)
    su_sb = sball("su_sb", [128, NHC], F32)
    sd_sb = sball("sd_sb", [128, 16], F32)
    st_a = sball("st_a", [128, 8], F32)
    WSEG = HID // 2
    wstg_st = contextlib.ExitStack()
    wstg = [sball(f"wstg{i}", [128, WSEG], BF16, wstg_st) for i in range(2)]

    dbg_outs = {}

    with nc.Block() as blk:
        @blk.sync
        def _(sync):
            for dst, src in [(ident_bf_sb, ident_bf), (ident_f_sb, ident_f),
                             (ones_row_sb, ones_row_f), (row128_sb, row128_f),
                             (one11_sb, one11_f), (ones_col_sb, ones_col_bf),
                             (eps_sb, eps_col_f)]:
                ins = sync.dma_start(out=dst[:], in_=src.ap())
                b.inc(ins, "ko", 16)
        konst_v = b.val("ko")
        bs_ps = contextlib.ExitStack()
        btmp = sball("btmp", [NHC, 128], F32, bs_ps)
        b_ps = psall("b_ps", [128, 64], bs_ps)
        for i, (dst, srcnm, n) in enumerate(
                [(qb_sb, "qb_b", 16), (kb_sb, "kb_b", 16), (vb_sb, "vb_b", 16),
                 (sg_sb, "sg", NHC), (su_sb, "su", NHC), (sd_sb, "sd", 16)]):
            @blk.scalar
            def _(scalar, srcnm=srcnm, n=n, i=i):
                if i >= 1:
                    b.wait(scalar, "dv", bko_ev[-1])
                ins = scalar.dma_start(
                    out=btmp[0:n, :],
                    in_=EXT[srcnm].rearrange("(a c) -> a c", c=128))
                b.inc(ins, "bko", 16)
            bv = b.val("bko")

            @blk.tensor
            def _(tensor, n=n, bv=bv, i=i):
                b.wait(tensor, "ko", konst_v)
                b.wait(tensor, "bko", bv)
                if i >= 1:
                    b.wait(tensor, "dv", bko_ev[-1])
                ins = tensor.transpose(b_ps[:, 0:n], btmp[0:n, :], ident_f_sb[0:n, 0:n])
                b.inc(ins, "pe")
            tv = b.val("pe")

            @blk.vector
            def _(vector, dst=dst, n=n, tv=tv):
                b.wait(vector, "pe", tv)
                ins = vector.tensor_copy(dst[:, 0:n], b_ps[:, 0:n])
                bko_ev.append(b.inc(ins, "dv"))
        bs_ps.close()

        cast_done = {}

        @blk.gpsimd
        def _(gpsimd):
            for key in ["wqa", "wka", "wva"]:
                ins = gpsimd.dma_start(out=wbf[key].ap(), in_=EXT[CAST_SRC[key]])
                cast_done[key] = b.inc(ins, "ca_" + key, 16)

        for engname in ("tensor", "vector", "scalar"):
            @getattr(blk, engname)
            def _(eng):
                b.wait(eng, "ko", konst_v)

        prep_done = {}
        wr_hist = []

        def prep_wT(key, prev_key):
            rows, cols = WSHAPE[key]
            n = cols // 128
            segs = [(s, min(WSEG, rows - s)) for s in range(0, rows, WSEG)]
            ci = [0]
            for i in range(n):
                for r0, rn in segs:
                    j = ci[0]
                    ci[0] += 1

                    wsem = "wr%d_" % (j % 2) + key

                    @blk.sync
                    def _(sync, key=key, i=i, j=j, r0=r0, rn=rn, prev_key=prev_key,
                          wsem=wsem):
                        if j == 0:
                            b.wait(sync, "ca_" + key, 16)
                        if j <= 1 and prev_key is not None:
                            for par in range(2):
                                nm = "wr%d_" % par + prev_key
                                b.wait(sync, nm, b.val(nm))
                        if j >= 2:
                            b.wait(sync, wsem, b.val(wsem))
                        ins = sync.dma_start(
                            out=wstg[j % 2][:, 0:rn],
                            in_=wbf[key].ap()[r0:r0 + rn, i * 128:(i + 1) * 128],
                            transpose=True)
                        b.inc(ins, "wp%d" % (j % 2), 16)
                    sv = b.val("wp%d" % (j % 2))

                    @blk.gpsimd
                    def _(gpsimd, key=key, i=i, j=j, r0=r0, rn=rn, sv=sv, wsem=wsem):
                        b.wait(gpsimd, "wp%d" % (j % 2), sv)
                        ins = gpsimd.dma_start(
                            out=wT[key].ap()[i * 128:(i + 1) * 128, r0:r0 + rn],
                            in_=wstg[j % 2][:, 0:rn])
                        b.inc(ins, wsem, 16)
            prep_done[key] = [("wr%d_" % par + key, b.val("wr%d_" % par + key))
                              for par in range(2)]

        for key, pk in zip(["wqa", "wka", "wva"], [None, "wqa", "wka"]):
            prep_wT(key, pk)

        @blk.gpsimd
        def _(gpsimd):
            for key in ["g", "woa", "u", "d", "qwb", "kwb", "vwb", "owb"]:
                ins = gpsimd.dma_start(out=wbf[key].ap(), in_=EXT[CAST_SRC[key]])
                cast_done[key] = b.inc(ins, "ca_" + key, 16)

        def ln_tile(x_ap, out_tile, par, sqt, colsum_to=None, scalar_gate=0,
                    x_gate=None):
            c = par * 4

            @blk.vector
            def _(vector):
                ins = vector.reduce_sum(st_a[:, c:c + 1], x_ap, axis=AX.X)
                b.inc(ins, "dv")

            @blk.scalar
            def _(scalar):
                b.wait(scalar, "pe", scalar_gate)
                if x_gate is not None:
                    b.wait(scalar, x_gate[0], x_gate[1])
                ins = scalar.activation(sqt[:], x_ap, AF.Square,
                                        accum_out=st_a[:, c + 1:c + 2])
                b.inc(ins, "ac")
            v0, a0 = b.val("dv"), b.val("ac")

            @blk.vector
            def _(vector):
                b.wait(vector, "dv", v0)
                b.wait(vector, "ac", a0)
                ins = vector.tensor_scalar_mul(st_a[:, c:c + 1], st_a[:, c:c + 1], 1.0 / D)
                b.inc(ins, "dv")
                b.wait(vector, "dv", b.val("dv"))
                ins = vector.tensor_mul(st_a[:, c + 2:c + 3], st_a[:, c:c + 1],
                                        st_a[:, c:c + 1])
                b.inc(ins, "dv")
                b.wait(vector, "dv", b.val("dv"))
                ins = vector.tensor_scalar(st_a[:, c + 1:c + 2], st_a[:, c + 1:c + 2],
                                           1.0 / D, None, ALU.mult)
                b.inc(ins, "dv")
                b.wait(vector, "dv", b.val("dv"))
                ins = vector.tensor_sub(st_a[:, c + 1:c + 2], st_a[:, c + 1:c + 2],
                                        st_a[:, c + 2:c + 3])
                b.inc(ins, "dv")
            v1 = b.val("dv")

            @blk.scalar
            def _(scalar):
                b.wait(scalar, "dv", v1)
                ins = scalar.activation(st_a[:, c + 2:c + 3], st_a[:, c + 1:c + 2],
                                        AF.Ln, bias=eps_sb[:, 0:1])
                b.inc(ins, "ac")
                b.wait(scalar, "ac", b.val("ac"))
                ins = scalar.activation(st_a[:, c + 3:c + 4], st_a[:, c + 2:c + 3],
                                        AF.Exp, scale=-0.5)
                b.inc(ins, "ac")
            a1 = b.val("ac")

            @blk.vector
            def _(vector):
                b.wait(vector, "ac", a1)
                ins = vector.tensor_scalar(out_tile[:], x_ap, st_a[:, c:c + 1],
                                           st_a[:, c + 3:c + 4], ALU.subtract, ALU.mult)
                b.inc(ins, "dv")
                if colsum_to is not None:
                    t, cc = colsum_to
                    b.wait(vector, "dv", b.val("dv"))
                    ins = vector.reduce_sum(t[:, cc:cc + 1], out_tile[:], axis=AX.X)
                    b.inc(ins, "dv")
            return b.val("dv")

        tstate = {"n": 0, "hist": []}

        def transpose_blocks(tp_ps, srcs, dsts, pe_waits=(), f32=False):
            ident = ident_f_sb if f32 else ident_bf_sb
            for i, ((s_t, s_c), (d_t, d_c)) in enumerate(zip(srcs, dsts)):
                par = tstate["n"] % 2
                tstate["n"] += 1

                @blk.tensor
                def _(tensor, s_t=s_t, s_c=s_c, par=par, i=i):
                    for nm, v in (pe_waits if i == 0 else ()):
                        b.wait(tensor, nm, v)
                    if len(tstate["hist"]) >= 2:
                        b.wait(tensor, "dv", tstate["hist"][-2])
                    ins = tensor.matmul(tp_ps[:, par * 512:par * 512 + 128],
                                        s_t[:, s_c:s_c + 128], ident[:],
                                        start=True, stop=True)
                    b.inc(ins, "pe")
                pv = b.val("pe")

                @blk.vector
                def _(vector, d_t=d_t, d_c=d_c, par=par, pv=pv):
                    b.wait(vector, "pe", pv)
                    ins = vector.tensor_copy(d_t[:, d_c:d_c + 128],
                                             tp_ps[:, par * 512:par * 512 + 128])
                    tstate["hist"].append(b.inc(ins, "dv"))
            return b.val("dv")

        def barrier(dma_waits=()):
            for engname in ("tensor", "vector", "scalar"):
                @getattr(blk, engname)
                def _(eng):
                    ins = eng.nop()
                    b.inc(ins, "bar")
            t = b.val("bar")
            for engname in ("tensor", "vector", "scalar", "sync", "gpsimd"):
                @getattr(blk, engname)
                def _(eng):
                    b.wait(eng, "bar", t)
                    for nm, v in dma_waits:
                        b.wait(eng, nm, v)

        def emit_dbg(name):
            if name not in DBG:
                return
            out = nc.dram_tensor("dbg_" + name, [TOK, D], F32, kind="ExternalOutput").ap()
            dbg_outs[name] = out
            wv = b.val("dv")

            @blk.gpsimd
            def _(gpsimd):
                b.wait(gpsimd, "dv", wv)
                for tb in range(NTB):
                    ins = gpsimd.dma_start(out=out[tb * 128:(tb + 1) * 128, :],
                                           in_=x_res[:, tb * D:(tb + 1) * D])
                    b.inc(ins, "dbg", 16)

        pa = contextlib.ExitStack()
        q_fm = sball("q_fm", [128, NP * T], BF16, pa)
        k_fm = sball("k_fm", [128, NP * T], BF16, pa)
        v_tm = sball("v_tm", [128, NP * 32 * 128], BF16, pa)
        tp_ps = psall("tp_psA", [128, 1024], pa)

        pa1 = contextlib.ExitStack()
        lnx_fm2 = [sball(f"lnx_fm{i}", [128, NDC * 512], BF16, pa1) for i in range(2)]
        v_fms = sball("v_fms", [128, NP * 512], BF16, pa1)
        x0t = [sball(f"x0t{i}", [128, D], F32, pa1) for i in range(2)]
        sqt_a = sball("sqt_a", [128, D], BF16, pa1)
        lnq_tm = [sball(f"lnq{i}", [128, D], BF16, pa1) for i in range(2)]
        aring = [sball(f"aring{i}", [128, NDC * 128], BF16, pa1) for i in range(2)]
        pj_ps = psall("pj_psA", [128, 2048], pa1)

        x0_free = [0, 0]
        lnq_free = [0, 0]
        aring_free = []
        pj_free = []
        a2_done = [0, 0]

        for sl in range(8):
            lnx_fm = lnx_fm2[sl % 2]
            for tt in range(4):
                tile = sl * 4 + tt
                par = tile % 2

                @blk.scalar
                def _(scalar, tile=tile, par=par):
                    b.wait(scalar, "dv", x0_free[par])
                    ins = scalar.dma_start(out=x0t[par][:],
                                           in_=X0[tile * 128:(tile + 1) * 128, :])
                    b.inc(ins, "x0w%d" % par, 16)
                xv = b.val("x0w%d" % par)

                @blk.vector
                def _(vector, par=par, xv=xv):
                    b.wait(vector, "x0w%d" % par, xv)
                    b.wait(vector, "pe", lnq_free[par])
                lv = ln_tile(x0t[par][:], lnq_tm[par], par, sqt_a,
                             x_gate=("x0w%d" % par, xv))
                x0_free[par] = lv
                transpose_blocks(
                    tp_ps,
                    [(lnq_tm[par], dc * 128) for dc in range(NDC)],
                    [(lnx_fm, dc * 512 + tt * 128) for dc in range(NDC)],
                    pe_waits=[("dv", lv), ("pe", a2_done[sl % 2])],
                )

                lnq_free[par] = b.val("pe")
            slab_tp = b.val("dv")

            for iw, key in enumerate(["wqa", "wka", "wva"]):
                for oc in range(NP):
                    li = len(aring_free)
                    slot = li % 2

                    @blk.scalar
                    def _(scalar, key=key, oc=oc, li=li, slot=slot):
                        for nm, v in prep_done[key]:
                            b.wait(scalar, nm, v)
                        if li >= 2:
                            b.wait(scalar, "pe", aring_free[li - 2])
                        ins = scalar.dma_start(
                            out=aring[slot][:].rearrange("p (a c) -> p a c", c=128),
                            in_=wT[key].ap()[0:D, oc * 128:(oc + 1) * 128]
                                .rearrange("(a p) c -> p a c", p=128))
                        b.inc(ins, "aw%d" % slot, 16)
                    wv = b.val("aw%d" % slot)
                    gi = sl * 12 + iw * NP + oc
                    bank = gi % 4

                    @blk.tensor
                    def _(tensor, slot=slot, gi=gi, bank=bank, wv=wv, slab_tp=slab_tp):
                        b.wait(tensor, "aw%d" % slot, wv)
                        b.wait(tensor, "dv", slab_tp)
                        if gi >= 4:
                            b.wait(tensor, "dv", pj_free[gi - 4])
                        for dc in range(NDC):
                            ins = tensor.matmul(
                                pj_ps[:, bank * 512:(bank + 1) * 512],
                                aring[slot][:, dc * 128:(dc + 1) * 128],
                                lnx_fm[:, dc * 512:(dc + 1) * 512],
                                start=(dc == 0), stop=(dc == NDC - 1))
                        b.inc(ins, "pe")
                    aring_free.append(b.val("pe"))
                    mv = b.val("pe")

                    @blk.vector
                    def _(vector, iw=iw, oc=oc, sl=sl, bank=bank, mv=mv):
                        b.wait(vector, "pe", mv)
                        dst = [q_fm, k_fm, v_fms][iw]
                        col = (oc * T + sl * 512) if iw < 2 else (oc * 512)
                        if iw == 0:
                            ins = vector.tensor_scalar_mul(
                                dst[:, col:col + 512],
                                pj_ps[:, bank * 512:(bank + 1) * 512],
                                1.0 / np.sqrt(float(DH)))
                        else:
                            ins = vector.tensor_copy(dst[:, col:col + 512],
                                                     pj_ps[:, bank * 512:(bank + 1) * 512])
                        pj_free.append(b.inc(ins, "dv"))
            vv = b.val("dv")
            transpose_blocks(
                tp_ps,
                [(v_fms, p * 512 + kk * 128) for p in range(NP) for kk in range(4)],
                [(v_tm, (p * 32 + (sl * 4 + kk)) * 128) for p in range(NP)
                 for kk in range(4)],
                pe_waits=[("dv", vv)],
            )

            a2_done[sl % 2] = b.val("pe")

        pa1.close()
        import os as _os
        if _os.environ.get("KMEM"):
            print("after A2 close:", nc.sbuf_bytes_remaining)
        flash_dv0 = b.val("dv")

        s_ps = [psall("s_ps0", [128, 1024], pa), psall("s_ps1", [128, 1024], pa)]
        pv_ps = psall("pv_ps", [128, 512], pa)
        rb_ps = psall("rb_ps", [128, 512], pa)
        p_sb = [sball(f"p_sb{i}", [128, T], BF16, pa) for i in range(4)]
        mx4_sb = sball("mx4_sb", [128, 8], F32, pa)
        sacc_sb = sball("sacc_sb", [128, 8], F32, pa)
        rsum_sb = sball("rsum_sb", [128, 1], F32, pa)
        pt_sb = sball("pt_sb", [128, 32 * 512], BF16, pa)
        ao_sb = [sball(f"ao_sb{i}", [128, 512], BF16, pa) for i in range(2)]
        sums_sb = sball("sums_sb", [128, 32], F32, pa)
        nmax_sb = sball("nmax_sb", [128, 4], F32, pa)
        rrow_sb = sball("rrow_sb", [1, 512], F32, pa)
        bcast_sb = sball("bcast_sb", [128, 512], F32, pa)

        s_use = []
        exp_v = []
        sum_dv = []
        ptdone_v = []
        ptev_v = []
        pv_done = []
        pv_ev = []
        ag_wr = []
        rr_seq = []
        rr_pe = []

        for p in range(NP):
            qc = p * T
            for qs in range(8):
                gqs = p * 8 + qs
                for qb in range(4):
                    qi = qs * 4 + qb
                    m = p * 32 + qi
                    par = qi % 2
                    for j4 in range(4):
                        n = m * 8 + j4
                        grp = n % 2

                        @blk.tensor
                        def _(tensor, qi=qi, j4=j4, grp=grp, n=n, qc=qc):
                            if n >= 2:
                                nm, v = s_use[n - 2]
                                b.wait(tensor, nm, v)
                            else:
                                b.wait(tensor, "dv", flash_dv0)
                            for jj in range(2):
                                kchunk = j4 * 2 + jj
                                ins = tensor.matmul(
                                    s_ps[grp][:, jj * 512:(jj + 1) * 512],
                                    q_fm[:, qc + qi * 128:qc + (qi + 1) * 128],
                                    k_fm[:, qc + kchunk * 512:qc + (kchunk + 1) * 512],
                                    start=True, stop=True)
                            b.inc(ins, "pe")
                        sv = b.val("pe")

                        @blk.vector
                        def _(vector, sv=sv, grp=grp, par=par, j4=j4, m=m):
                            b.wait(vector, "pe", sv)
                            if m >= 2 and j4 == 0:
                                b.wait(vector, "ac", exp_v[m - 2])
                            ins = vector.reduce_max(mx4_sb[:, par * 4 + j4:par * 4 + j4 + 1],
                                                    s_ps[grp][:], axis=AX.X)
                            s_use.append(("dv", b.inc(ins, "dv")))
                    mxs = b.val("dv")

                    @blk.vector
                    def _(vector, par=par, qb=qb, m=m, mxs=mxs):
                        b.wait(vector, "dv", mxs)
                        if m >= 4:
                            b.wait(vector, "ac", exp_v[m - 4])
                        ins = vector.reduce_max(nmax_sb[:, qb:qb + 1],
                                                mx4_sb[:, par * 4:par * 4 + 4],
                                                axis=AX.X, negate=True)
                        b.inc(ins, "dv")
                    mxv = b.val("dv")

                    for j4 in range(4):
                        n = m * 8 + 4 + j4
                        grp = n % 2

                        @blk.tensor
                        def _(tensor, qi=qi, j4=j4, grp=grp, n=n, qc=qc):
                            nm, v = s_use[n - 2]
                            b.wait(tensor, nm, v)
                            for jj in range(2):
                                kchunk = j4 * 2 + jj
                                ins = tensor.matmul(
                                    s_ps[grp][:, jj * 512:(jj + 1) * 512],
                                    q_fm[:, qc + qi * 128:qc + (qi + 1) * 128],
                                    k_fm[:, qc + kchunk * 512:qc + (kchunk + 1) * 512],
                                    start=True, stop=True)
                            b.inc(ins, "pe")
                        sv = b.val("pe")

                        @blk.scalar
                        def _(scalar, sv=sv, grp=grp, par=par, qb=qb, qi=qi, j4=j4,
                              m=m, gqs=gqs, mxv=mxv):
                            b.wait(scalar, "pe", sv)
                            if j4 == 0:
                                b.wait(scalar, "dv", mxv)
                                if gqs >= 1:
                                    b.wait(scalar, "pe", ptdone_v[gqs - 1])
                                if m >= 2:
                                    b.wait(scalar, "dv", sum_dv[m - 2])
                            ins = scalar.activation(
                                p_sb[qb][:, j4 * 1024:(j4 + 1) * 1024], s_ps[grp][:],
                                AF.Exp, bias=nmax_sb[:, qb:qb + 1],
                                accum_out=sacc_sb[:, par * 4 + j4:par * 4 + j4 + 1])
                            s_use.append(("ac", b.inc(ins, "ac")))
                    exp_v.append(b.val("ac"))
                    ev = exp_v[-1]

                    @blk.vector
                    def _(vector, par=par, qi=qi, p=p, ev=ev):
                        b.wait(vector, "ac", ev)
                        if p >= 1:
                            b.wait(vector, "pe", rr_pe[(p - 1) * 8 + qi // 4])
                        ins = vector.reduce_sum(rsum_sb[:], sacc_sb[:, par * 4:par * 4 + 4],
                                                axis=AX.X)
                        b.inc(ins, "dv")
                        b.wait(vector, "dv", b.val("dv"))
                        ins = vector.reciprocal(sums_sb[:, qi:qi + 1], rsum_sb[:])
                        sum_dv.append(b.inc(ins, "dv"))

                ev = exp_v[-1]
                for kc in range(32):
                    t = gqs * 32 + kc
                    par2 = kc % 2

                    @blk.tensor
                    def _(tensor, kc=kc, par2=par2, ev=ev, t=t):
                        if kc == 0:
                            b.wait(tensor, "ac", ev)
                        if t >= 2:
                            b.wait(tensor, "dv", ptev_v[t - 2])
                        for qb in range(4):
                            ins = tensor.matmul(
                                tp_ps[:, par2 * 512 + qb * 128:par2 * 512 + (qb + 1) * 128],
                                p_sb[qb][:, kc * 128:(kc + 1) * 128], ident_bf_sb[:],
                                start=True, stop=True)
                        b.inc(ins, "pe")
                    tv = b.val("pe")

                    @blk.vector
                    def _(vector, kc=kc, par2=par2, tv=tv, gqs=gqs):
                        b.wait(vector, "pe", tv)
                        if gqs >= 1 and kc == 0:
                            b.wait(vector, "pe", pv_done[gqs - 1])
                        ins = vector.tensor_copy(pt_sb[:, kc * 512:(kc + 1) * 512],
                                                 tp_ps[:, par2 * 512:(par2 + 1) * 512])
                        ptev_v.append(b.inc(ins, "dv"))

                ptdone_v.append(b.val("pe"))
                ptv = ptev_v[-1]

                @blk.tensor
                def _(tensor, p=p, gqs=gqs, ptv=ptv):
                    b.wait(tensor, "dv", ptv)
                    if gqs >= 1:
                        b.wait(tensor, "dv", pv_ev[gqs - 1])
                    for kc in range(32):
                        ins = tensor.matmul(
                            pv_ps[:],
                            v_tm[:, (p * 32 + kc) * 128:(p * 32 + kc + 1) * 128],
                            pt_sb[:, kc * 512:(kc + 1) * 512],
                            start=(kc == 0), stop=(kc == 31))
                    b.inc(ins, "pe")
                pv_done.append(b.val("pe"))
                sdv = sum_dv[-1]

                @blk.tensor
                def _(tensor, qs=qs, gqs=gqs, sdv=sdv):
                    b.wait(tensor, "dv", sdv)
                    if gqs >= 1:
                        b.wait(tensor, "dv", rr_seq[gqs - 1])
                    for qb in range(4):
                        qi = qs * 4 + qb
                        ins = tensor.matmul(rb_ps[0:1, qb * 128:(qb + 1) * 128],
                                            sums_sb[:, qi:qi + 1], ident_f_sb[:],
                                            start=True, stop=True)
                    b.inc(ins, "pe")
                rr_pe.append(b.val("pe"))
                rv = rr_pe[-1]

                @blk.vector
                def _(vector, rv=rv):
                    b.wait(vector, "pe", rv)
                    ins = vector.tensor_copy(rrow_sb[:], rb_ps[0:1, 0:512])
                    b.inc(ins, "dv")
                rev = b.val("dv")

                @blk.tensor
                def _(tensor, rev=rev):
                    b.wait(tensor, "dv", rev)
                    ins = tensor.matmul(rb_ps[:, 0:512], ones_row_sb[:], rrow_sb[:],
                                        start=True, stop=True)
                    b.inc(ins, "pe")
                bv = b.val("pe")
                pvv = pv_done[-1]

                @blk.vector
                def _(vector, bv=bv, pvv=pvv, gqs=gqs):
                    b.wait(vector, "pe", bv)
                    ins = vector.tensor_copy(bcast_sb[:], rb_ps[:, 0:512])
                    rr_seq.append(b.inc(ins, "dv"))
                    b.wait(vector, "pe", pvv)
                    b.wait(vector, "dv", b.val("dv"))
                    if gqs >= 2:
                        b.wait(vector, "agw", b.val("agw"))
                    ins = vector.tensor_mul(ao_sb[gqs % 2][:], pv_ps[:], bcast_sb[:])
                    pv_ev.append(b.inc(ins, "dv"))
                aov = pv_ev[-1]

                @blk.scalar
                def _(scalar, p=p, qs=qs, gqs=gqs, aov=aov):
                    b.wait(scalar, "dv", aov)
                    ins = scalar.dma_start(
                        out=ag_in.ap()[p * 128:(p + 1) * 128, qs * 512:(qs + 1) * 512],
                        in_=ao_sb[gqs % 2][:])
                    ag_wr.append(b.inc(ins, "agw", 16))

        flash_done = pv_ev[-1]
        ag_all = ag_wr[-1]
        pa.close()
        for key, pk in zip(["g", "u", "d", "qwb", "kwb", "vwb", "owb"],
                           ["wva", "g", "u", "d", "qwb", "kwb", "vwb"]):
            prep_wT(key, pk)
        preps_all = [it for k in ["wqa", "wka", "wva", "g", "u", "d", "qwb",
                                  "kwb", "vwb", "owb"] for it in prep_done[k]]
        wstg_st.close()
        import os as _os2
        if _os2.environ.get("KMEM"):
            print("after flash close:", nc.sbuf_bytes_remaining)

        @blk.gpsimd
        def _(gpsimd):
            b.wait(gpsimd, "agw", ag_all)
            ins = gpsimd.collective_compute(
                "AllGather", ALU.bypass, RG, [ag_in.ap().opt()], [ag_out.ap().opt()])
            b.inc(ins, "coll")
        coll_ag = b.val("coll")

        x_res = sball("x_res", [128, NTB * D], F32)
        pa6 = contextlib.ExitStack()
        og_sb = sball("og_sb", [128, NH * 1024], BF16, pa6)
        woT_sb = sball("woT_sb", [128, NH * D], BF16, pa6)
        x0s = [sball(f"x0s{i}", [128, D], F32, pa6) for i in range(2)]
        op_ps = psall("op_psA", [128, 2048], pa6)

        @blk.sync
        def _(sync):
            for nm, v in preps_all:
                b.wait(sync, nm, v)
            b.wait(sync, "ca_woa", 16)
            b.wait(sync, "dv", flash_done)
            b.wait(sync, "agw", ag_all)
            for h in range(NH):
                ins = sync.dma_start(out=woT_sb[:, h * D:(h + 1) * D],
                                     in_=wbf["woa"].ap()[0:D, h * 128:(h + 1) * 128],
                                     transpose=True)
                b.inc(ins, "wow", 16)
        woT_v = b.val("wow")

        @blk.gpsimd
        def _(gpsimd):
            b.wait(gpsimd, "coll", coll_ag)
            pid = gpsimd.partition_id()
            to0v = gpsimd.compute_val((pid % 4) * TOK)
            for h in range(NH):
                ins = gpsimd.dma_start(
                    out=og_sb[:, h * 1024:(h + 1) * 1024],
                    in_=ag_out.ap()[h * 128:(h + 1) * 128, bass.ds(to0v, TOK)])
                b.inc(ins, "ogw", 16)
        og_v = b.val("ogw")

        op_free = []
        for tb in range(NTB):
            par = tb % 2

            @blk.scalar
            def _(scalar, tb=tb, par=par):
                b.wait(scalar, "dv", flash_done)
                b.wait(scalar, "agw", ag_all)
                if tb >= 2:
                    b.wait(scalar, "dv", op_free[tb - 2])
                ins = scalar.dma_start(out=x0s[par][:],
                                       in_=X0M[tb * 128:(tb + 1) * 128, :])
                b.inc(ins, "xsw%d" % par, 16)
            xv = b.val("xsw%d" % par)

            @blk.tensor
            def _(tensor, tb=tb, xv=xv):
                if tb == 0:
                    b.wait(tensor, "ogw", og_v)
                    b.wait(tensor, "wow", woT_v)
                if tb >= 1:
                    b.wait(tensor, "dv", op_free[tb - 1])
                for h in range(NH):
                    for ods in range(4):
                        ins = tensor.matmul(
                            op_ps[:, ods * 512:(ods + 1) * 512],
                            og_sb[:, h * 1024 + tb * 128:h * 1024 + (tb + 1) * 128],
                            woT_sb[:, h * D + ods * 512:h * D + (ods + 1) * 512],
                            start=(h == 0), stop=(h == NH - 1))
                b.inc(ins, "pe")
            ov = b.val("pe")

            @blk.vector
            def _(vector, tb=tb, par=par, ov=ov, xv=xv):
                b.wait(vector, "pe", ov)
                b.wait(vector, "xsw%d" % par, xv)
                for ods in range(4):
                    ins = vector.tensor_add(
                        x_res[:, tb * D + ods * 512:tb * D + (ods + 1) * 512],
                        op_ps[:, ods * 512:(ods + 1) * 512],
                        x0s[par][:, ods * 512:(ods + 1) * 512])
                op_free.append(b.inc(ins, "dv"))
        pa6.close()
        import os as _os3
        if _os3.environ.get("KMEM"):
            print("after A6 close:", nc.sbuf_bytes_remaining)
        emit_dbg("x1")

        def emit_ffn(tag):
            ps = contextlib.ExitStack()
            sb = lambda n, s, d=BF16: sball(n + tag, s, d, ps)
            lnt = [sb(f"lnt{i}", [128, D]) for i in range(2)]
            sqt = sb("sqt", [128, D])
            lnf = sb("lnf", [128, NDC * 1024])
            cs_cols = sb("cs_cols", [128, NTB], F32)
            csr = sb("csr", [1, 1024], F32)
            cs_bc = sb("cs_bc", [128, 1024], F32)
            cshr = sb("cshr", [1, 512], F32)
            csh_bc = sb("csh_bc", [128, 512], F32)
            h_fm = sb("h_fm", [128, NHC * 512])
            tg_sc = [sb(f"tg{i}", [128, 512], F32) for i in range(2)]
            tu_sc = [sb(f"tu{i}", [128, 512], F32) for i in range(2)]
            gsig = [sb(f"gsig{i}", [128, 512]) for i in range(2)]
            dnf = sb("dnf", [128, 2048], F32)
            wring = [sb(f"wring{i}", [128, NDC * 128]) for i in range(3)]
            dring = [sb(f"dring{i}", [128, (NHC // 2) * 128]) for i in range(2)]
            tp2 = psall("tp2" + tag, [128, 1024], ps)
            gu_ps = psall("gu" + tag, [128, 2048], ps)
            dn_ps = psall("dn" + tag, [128, 1024], ps)

            lnt_free = [0, 0]
            for tb in range(NTB):
                par = tb % 2

                @blk.vector
                def _(vector, par=par):
                    b.wait(vector, "pe", lnt_free[par])
                lv = ln_tile(x_res[:, tb * D:(tb + 1) * D], lnt[par], par, sqt,
                             colsum_to=(cs_cols, tb))
                transpose_blocks(
                    tp2,
                    [(lnt[par], dc * 128) for dc in range(NDC)],
                    [(lnf, dc * 1024 + tb * 128) for dc in range(NDC)],
                    pe_waits=[("dv", lv)],
                )
                lnt_free[par] = b.val("pe")
            lnf_v = b.val("dv")

            @blk.tensor
            def _(tensor, lnf_v=lnf_v):
                b.wait(tensor, "dv", lnf_v)
                for tb in range(NTB):
                    ins = tensor.matmul(gu_ps[0:1, tb * 128:(tb + 1) * 128],
                                        cs_cols[:, tb:tb + 1], ident_f_sb[:],
                                        start=True, stop=True)
                b.inc(ins, "pe")
            csv = b.val("pe")

            @blk.vector
            def _(vector, csv=csv):
                b.wait(vector, "pe", csv)
                ins = vector.tensor_copy(csr[:], gu_ps[0:1, 0:1024])
                b.inc(ins, "dv")
            csrv = b.val("dv")

            @blk.tensor
            def _(tensor, csrv=csrv):
                b.wait(tensor, "dv", csrv)
                tensor.matmul(gu_ps[:, 0:512], row128_sb[:], csr[0:1, 0:512],
                              start=True, stop=True)
                ins = tensor.matmul(gu_ps[:, 512:1024], row128_sb[:], csr[0:1, 512:1024],
                                    start=True, stop=True)
                b.inc(ins, "pe")
            csbv = b.val("pe")

            @blk.vector
            def _(vector, csbv=csbv):
                b.wait(vector, "pe", csbv)
                ins = vector.tensor_copy(cs_bc[:], gu_ps[:, 0:1024])
                b.inc(ins, "dv")
            csbc_v = b.val("dv")

            wring_free = []
            dring_free = []
            gu_free = []
            dn_free = []
            th_gate_wait = csbc_v
            for th in range(2):
                for oc in range(NHC):
                    idx = th * NHC + oc
                    for iw, key in enumerate(["g", "u"]):
                        li = idx * 2 + iw
                        slot = li % 3

                        @blk.sync
                        def _(sync, key=key, oc=oc, li=li, slot=slot):
                            for nm, v in prep_done[key]:
                                b.wait(sync, nm, v)
                            if li >= 3:
                                b.wait(sync, "pe", wring_free[li - 3])
                            ins = sync.dma_start(
                                out=wring[slot][:].rearrange("p (a c) -> p a c", c=128),
                                in_=wT[key].ap()[0:D, oc * 128:(oc + 1) * 128]
                                    .rearrange("(a p) c -> p a c", p=128))
                            b.inc(ins, "fw%d" % slot, 16)
                    wv = [b.val("fw%d" % ((idx * 2 + iw) % 3)) for iw in range(2)]
                    par = idx % 2

                    @blk.tensor
                    def _(tensor, th=th, idx=idx, par=par, wv=wv,
                          th_gate_wait=th_gate_wait):
                        for iw in range(2):
                            b.wait(tensor, "fw%d" % ((idx * 2 + iw) % 3), wv[iw])
                        if idx % NHC == 0:
                            b.wait(tensor, "dv", th_gate_wait)
                        if idx >= 2:
                            b.wait(tensor, "dv", gu_free[idx - 2])
                        for iw in range(2):
                            slot = (idx * 2 + iw) % 3
                            bank = iw * 2 + par
                            for dc in range(NDC):
                                ins = tensor.matmul(
                                    gu_ps[:, bank * 512:(bank + 1) * 512],
                                    wring[slot][:, dc * 128:(dc + 1) * 128],
                                    lnf[:, dc * 1024 + th * 512:dc * 1024 + (th + 1) * 512],
                                    start=(dc == 0), stop=(dc == NDC - 1))
                            b.inc(ins, "pe")
                        wring_free.append(b.val("pe"))
                        wring_free.append(b.val("pe"))
                    gv = b.val("pe")

                    @blk.vector
                    def _(vector, oc=oc, th=th, par=par, gv=gv):
                        b.wait(vector, "pe", gv)
                        ins = vector.scalar_tensor_tensor(
                            tg_sc[par][:], gu_ps[:, par * 512:(par + 1) * 512],
                            1.0, cs_bc[:, th * 512:(th + 1) * 512],
                            ALU.mult, ALU.subtract)
                        b.inc(ins, "dv")
                        b.wait(vector, "dv", b.val("dv"))
                        ins = vector.tensor_scalar_mul(tg_sc[par][:], tg_sc[par][:],
                                                       sg_sb[:, oc:oc + 1])
                        b.inc(ins, "dv")
                        ins = vector.tensor_sub(tu_sc[par][:],
                                                gu_ps[:, (2 + par) * 512:(3 + par) * 512],
                                                cs_bc[:, th * 512:(th + 1) * 512])
                        b.inc(ins, "dv")
                    sv = b.val("dv")

                    @blk.scalar
                    def _(scalar, par=par, sv=sv):
                        b.wait(scalar, "dv", sv)
                        ins = scalar.activation(gsig[par][:], tg_sc[par][:], AF.Silu)
                        b.inc(ins, "ac")
                    av = b.val("ac")

                    @blk.vector
                    def _(vector, oc=oc, par=par, av=av):
                        b.wait(vector, "ac", av)
                        ins = vector.scalar_tensor_tensor(
                            h_fm[:, oc * 512:(oc + 1) * 512], tu_sc[par][:],
                            su_sb[:, oc:oc + 1], gsig[par][:], ALU.mult, ALU.mult)
                        gu_free.append(b.inc(ins, "dv"))
                h_v = gu_free[-1]

                @blk.tensor
                def _(tensor, h_v=h_v):
                    b.wait(tensor, "dv", h_v)
                    for hc in range(NHC):
                        ins = tensor.matmul(gu_ps[0:1, 0:512], ones_col_sb[:],
                                            h_fm[:, hc * 512:(hc + 1) * 512],
                                            start=(hc == 0), stop=(hc == NHC - 1))
                    b.inc(ins, "pe")
                chv = b.val("pe")

                @blk.vector
                def _(vector, chv=chv):
                    b.wait(vector, "pe", chv)
                    ins = vector.tensor_scalar_mul(cshr[:], gu_ps[0:1, 0:512], 128.0)
                    b.inc(ins, "dv")
                chrv = b.val("dv")

                @blk.tensor
                def _(tensor, chrv=chrv):
                    b.wait(tensor, "dv", chrv)
                    ins = tensor.matmul(gu_ps[:, 512:1024], ones_row_sb[:], cshr[:],
                                        start=True, stop=True)
                    b.inc(ins, "pe")
                chbv = b.val("pe")

                @blk.vector
                def _(vector, chbv=chbv):
                    b.wait(vector, "pe", chbv)
                    ins = vector.tensor_copy(csh_bc[:], gu_ps[:, 512:1024])
                    b.inc(ins, "dv")
                cshbc_v = b.val("dv")
                th_gate_wait = cshbc_v

                HH = NHC // 2
                for dimc in range(NDC):
                    di = th * NDC + dimc
                    bank = di % 2
                    dwvs = []
                    for hf in range(2):
                        hi = di * 2 + hf

                        @blk.sync
                        def _(sync, dimc=dimc, hf=hf, hi=hi):
                            for nm, v in prep_done["d"]:
                                b.wait(sync, nm, v)
                            if hi >= 2:
                                b.wait(sync, "pe", dring_free[hi - 2])
                            ins = sync.dma_start(
                                out=dring[hf][:].rearrange("p (a c) -> p a c", c=128),
                                in_=wT["d"].ap()[hf * HH * 128:(hf + 1) * HH * 128,
                                                 dimc * 128:(dimc + 1) * 128]
                                    .rearrange("(a p) c -> p a c", p=128))
                            b.inc(ins, "dw%d" % hf, 16)
                        dwvs.append(b.val("dw%d" % hf))

                    for hf in range(2):
                        @blk.tensor
                        def _(tensor, di=di, hf=hf, bank=bank, dwv=dwvs[hf],
                              cshbc_v=cshbc_v):
                            b.wait(tensor, "dw%d" % hf, dwv)
                            if di % NDC == 0 and hf == 0:
                                b.wait(tensor, "dv", cshbc_v)
                            if di >= 2 and hf == 0:
                                b.wait(tensor, "dv", dn_free[di - 2])
                            for hc2 in range(HH):
                                hc = hf * HH + hc2
                                ins = tensor.matmul(
                                    dn_ps[:, bank * 512:(bank + 1) * 512],
                                    dring[hf][:, hc2 * 128:(hc2 + 1) * 128],
                                    h_fm[:, hc * 512:(hc + 1) * 512],
                                    start=(hc == 0), stop=(hc == NHC - 1))
                            b.inc(ins, "pe")
                            dring_free.append(b.val("pe"))
                    dmv = b.val("pe")

                    @blk.vector
                    def _(vector, dimc=dimc, bank=bank, dmv=dmv):
                        b.wait(vector, "pe", dmv)
                        ins = vector.tensor_sub(dnf[:, bank * 1024:bank * 1024 + 512],
                                                dn_ps[:, bank * 512:(bank + 1) * 512],
                                                csh_bc[:])
                        b.inc(ins, "dv")
                        b.wait(vector, "dv", b.val("dv"))
                        ins = vector.tensor_scalar_mul(
                            dnf[:, bank * 1024 + 512:(bank + 1) * 1024],
                            dnf[:, bank * 1024:bank * 1024 + 512],
                            sd_sb[:, dimc:dimc + 1])
                        dn_free.append(b.inc(ins, "dv"))
                    dfv = b.val("dv")

                    for j in range(4):
                        tb = th * 4 + j
                        tpar = tstate["n"] % 2
                        tstate["n"] += 1

                        @blk.tensor
                        def _(tensor, j=j, tpar=tpar, dfv=dfv, bank=bank):
                            if len(tstate["hist"]) >= 2:
                                b.wait(tensor, "dv", tstate["hist"][-2])
                            if j == 0:
                                b.wait(tensor, "dv", dfv)
                            ins = tensor.matmul(
                                tp2[:, tpar * 512:tpar * 512 + 128],
                                dnf[:, bank * 1024 + 512 + j * 128:
                                    bank * 1024 + 512 + (j + 1) * 128],
                                ident_f_sb[:], start=True, stop=True)
                            b.inc(ins, "pe")
                        tpv = b.val("pe")

                        @blk.vector
                        def _(vector, tb=tb, dimc=dimc, tpar=tpar, tpv=tpv):
                            b.wait(vector, "pe", tpv)
                            ins = vector.tensor_add(
                                x_res[:, tb * D + dimc * 128:tb * D + (dimc + 1) * 128],
                                x_res[:, tb * D + dimc * 128:tb * D + (dimc + 1) * 128],
                                tp2[:, tpar * 512:tpar * 512 + 128])
                            tstate["hist"].append(b.inc(ins, "dv"))
            ps.close()

        def emit_attn_b():
            psc = contextlib.ExitStack()
            sb = lambda n, s, d=BF16: sball(n, s, d, psc)
            tpB = psall("tpB", [128, 1024], psc)
            pjB = psall("pjB", [128, 2048], psc)
            smB = psall("smB", [128, 1024], psc)

            qp_fm = sball("qp_fm", [128, NDC * 1024], BF16, psc)
            at_fm = sball("at_fm", [128, NDC * 512], BF16, psc)
            ps1b = contextlib.ExitStack()
            kp_fm = sball("kp_fm", [128, NDC * 1024], BF16, ps1b)
            v_fm = sball("bv_fm", [128, NDC * 1024], BF16, ps1b)
            ps1 = contextlib.ExitStack()
            lnt = [sball("blnt0", [128, D], BF16, ps1)]
            lnf = sball("blnf", [128, NDC * 512], BF16, ps1)
            t_el = [sball("t_el0", [128, 512], BF16, ps1)]
            wringB = [sball(f"wringB{i}", [128, NDC * 128], BF16, ps1) for i in range(2)]

            lnt_free = [0]
            wring_free = []
            pj_free2 = []
            el_ac = []
            b_slab_done = [0]
            for sl in range(2):
                for tt in range(4):
                    tb = sl * 4 + tt

                    @blk.vector
                    def _(vector):
                        b.wait(vector, "pe", lnt_free[0])
                    lv = ln_tile(x_res[:, tb * D:(tb + 1) * D], lnt[0], tb % 2,
                                 lnt[0], scalar_gate=lnt_free[0])
                    transpose_blocks(
                        tpB,
                        [(lnt[0], dc * 128) for dc in range(NDC)],
                        [(lnf, dc * 512 + tt * 128) for dc in range(NDC)],
                        pe_waits=[("dv", lv), ("pe", b_slab_done[0])],
                    )
                    lnt_free[0] = b.val("pe")
                lnf_v = b.val("dv")

                for iw, (key, bias_t) in enumerate([("qwb", qb_sb), ("kwb", kb_sb),
                                                    ("vwb", vb_sb)]):
                    dst = [qp_fm, kp_fm, v_fm][iw]
                    for oc in range(NDC):
                        li = len(wring_free)
                        slot = li % 2

                        @blk.sync
                        def _(sync, key=key, oc=oc, li=li, slot=slot):
                            for nm, v in prep_done[key]:
                                b.wait(sync, nm, v)
                            if li >= 2:
                                b.wait(sync, "pe", wring_free[li - 2])
                            ins = sync.dma_start(
                                out=wringB[slot][:].rearrange("p (a c) -> p a c", c=128),
                                in_=wT[key].ap()[0:D, oc * 128:(oc + 1) * 128]
                                    .rearrange("(a p) c -> p a c", p=128))
                            b.inc(ins, "bw%d" % slot, 16)
                        wv = b.val("bw%d" % slot)
                        gi = len(pj_free2)
                        bank = gi % 4

                        @blk.tensor
                        def _(tensor, slot=slot, gi=gi, bank=bank, wv=wv,
                              lnf_v=lnf_v):
                            b.wait(tensor, "bw%d" % slot, wv)
                            b.wait(tensor, "dv", lnf_v)
                            if gi >= 4:
                                b.wait(tensor, "dv", pj_free2[gi - 4])
                            for dc in range(NDC):
                                ins = tensor.matmul(
                                    pjB[:, bank * 512:(bank + 1) * 512],
                                    wringB[slot][:, dc * 128:(dc + 1) * 128],
                                    lnf[:, dc * 512:(dc + 1) * 512],
                                    start=(dc == 0), stop=(dc == NDC - 1))
                            b.inc(ins, "pe")
                        wring_free.append(b.val("pe"))
                        mv = b.val("pe")

                        if iw < 2:
                            dslice = dst[:, oc * 1024 + sl * 512:oc * 1024 + (sl + 1) * 512]

                            @blk.vector
                            def _(vector, oc=oc, bank=bank, mv=mv, bias_t=bias_t):
                                b.wait(vector, "pe", mv)
                                if len(el_ac) >= 1:
                                    b.wait(vector, "ac", el_ac[-1])
                                if pj_free2:
                                    b.wait(vector, "dv", pj_free2[-1])
                                ins = vector.tensor_scalar(
                                    t_el[0][:], pjB[:, bank * 512:(bank + 1) * 512],
                                    bias_t[:, oc:oc + 1], 0.0, ALU.add, ALU.min)
                                b.inc(ins, "dv")
                            elv = b.val("dv")

                            @blk.scalar
                            def _(scalar, dslice=dslice, elv=elv):
                                b.wait(scalar, "dv", elv)
                                ins = scalar.activation(dslice, t_el[0][:], AF.Exp)
                                el_ac.append(b.inc(ins, "ac"))
                            eav = b.val("ac")

                            @blk.vector
                            def _(vector, dslice=dslice, oc=oc, bank=bank,
                                  eav=eav, bias_t=bias_t):
                                b.wait(vector, "ac", eav)
                                ins = vector.tensor_scalar(
                                    t_el[0][:], pjB[:, bank * 512:(bank + 1) * 512],
                                    bias_t[:, oc:oc + 1], 0.0, ALU.add, ALU.max)
                                b.inc(ins, "dv")
                                b.wait(vector, "dv", b.val("dv"))
                                ins = vector.tensor_add(dslice, dslice, t_el[0][:])
                                pj_free2.append(b.inc(ins, "dv"))
                        else:
                            @blk.vector
                            def _(vector, dst=dst, oc=oc, sl=sl, bank=bank, mv=mv,
                                  bias_t=bias_t):
                                b.wait(vector, "pe", mv)
                                ins = vector.tensor_scalar(
                                    dst[:, oc * 1024 + sl * 512:oc * 1024 + (sl + 1) * 512],
                                    pjB[:, bank * 512:(bank + 1) * 512],
                                    bias_t[:, oc:oc + 1], None, ALU.add)
                                pj_free2.append(b.inc(ins, "dv"))

                b_slab_done[0] = b.val("pe")
            proj_v = b.val("dv")

            ps1.close()
            km_tm = sball("km_tm", [128, 8 * 128], BF16, ps1b)
            vm_tm = sball("vm_tm", [128, 8 * 128], BF16, ps1b)
            kvt = sball("kvt", [128, 128], F32, ps1b)
            ksrow = sball("ksrow", [1, 128], F32, ps1b)
            kv_wr = []
            for h in range(NH):
                tv = transpose_blocks(
                    tpB,
                    [(kp_fm, h * 1024 + j * 128) for j in range(8)]
                    + [(v_fm, h * 1024 + j * 128) for j in range(8)],
                    [(km_tm, j * 128) for j in range(8)]
                    + [(vm_tm, j * 128) for j in range(8)],
                    pe_waits=[("dv", proj_v)] + ([("kvw", kv_wr[-1])] if h else []),
                )

                @blk.tensor
                def _(tensor, h=h, tv=tv):
                    b.wait(tensor, "dv", tv)
                    if h >= 1:
                        b.wait(tensor, "dv", kv_ev[-1])
                    for j in range(8):
                        tensor.matmul(smB[:, 0:128], km_tm[:, j * 128:(j + 1) * 128],
                                      vm_tm[:, j * 128:(j + 1) * 128],
                                      start=(j == 0), stop=(j == 7))
                    for j in range(8):
                        ins = tensor.matmul(smB[0:1, 512:640], ones_col_sb[:],
                                            km_tm[:, j * 128:(j + 1) * 128],
                                            start=(j == 0), stop=(j == 7))
                    b.inc(ins, "pe")
                kvv = b.val("pe")

                @blk.vector
                def _(vector, kvv=kvv):
                    b.wait(vector, "pe", kvv)
                    vector.tensor_copy(kvt[:], smB[:, 0:128])
                    ins = vector.tensor_copy(ksrow[:], smB[0:1, 512:640])
                    kv_ev.append(b.inc(ins, "dv"))
                kev = kv_ev[-1]

                @blk.gpsimd
                def _(gpsimd, h=h, kev=kev):
                    b.wait(gpsimd, "dv", kev)
                    ins = gpsimd.dma_start(out=ar_in.ap()[h * 128:(h + 1) * 128, :],
                                           in_=kvt[:])
                    b.inc(ins, "kvw", 16)
                    ins = gpsimd.dma_start(
                        out=ar_in.ap()[NH * DH + h:NH * DH + h + 1, :], in_=ksrow[:])
                    kv_wr.append(b.inc(ins, "kvw", 16))
            ps1b.close()

            @blk.gpsimd
            def _(gpsimd):
                b.wait(gpsimd, "kvw", kv_wr[-1])
                ins = gpsimd.collective_compute(
                    "AllReduce", ALU.add, RG, [ar_in.ap().opt()], [ar_out.ap().opt()])
                b.inc(ins, "coll")
            coll_ar = b.val("coll")

            ps2 = contextlib.ExitStack()
            kvb = sball("kvb", [128, NH * 128], BF16, ps2)
            zrows = sball("zrows", [1, 4 * 512], F32, ps2)
            ksf = sball("ksf", [128, NH], F32, ps2)
            ksb = sball("ksb", [128, NH], BF16, ps2)

            @blk.gpsimd
            def _(gpsimd):
                b.wait(gpsimd, "coll", coll_ar)
                ins = gpsimd.dma_start(
                    out=kvb[:].rearrange("p (a c) -> p a c", c=128),
                    in_=ar_out.ap()[0:NH * DH, :].rearrange("(a p) c -> p a c", p=128))
                b.inc(ins, "kvl", 16)
                ins = gpsimd.dma_start(
                    out=ksf[:],
                    in_=bass.AP(ar_out.ap().tensor, (NH * DH) * DH, [[1, 128], [128, NH]]),
                    allow_slow_non_contiguous=True)
                b.inc(ins, "kvl", 16)
            kv_ld = b.val("kvl")

            @blk.vector
            def _(vector, kv_ld=kv_ld):
                b.wait(vector, "kvl", kv_ld)
                ins = vector.tensor_copy(ksb[:], ksf[:])
                b.inc(ins, "dv")
            ksb_v = b.val("dv")

            ps3 = contextlib.ExitStack()
            owT_b = sball("owT_b", [128, NDC * D], BF16, ps3)
            ob_row = sball("ob_row", [1, D], F32, ps3)
            ob_bc = sball("ob_bc", [128, D], F32, ps3)

            @blk.sync
            def _(sync):
                for nm, v in prep_done["owb"]:
                    b.wait(sync, nm, v)
                b.wait(sync, "kvw", kv_wr[-1])
                for ec in range(NDC):
                    ins = sync.dma_start(out=owT_b[:, ec * D:(ec + 1) * D],
                                         in_=wT["owb"].ap()[ec * 128:(ec + 1) * 128, 0:D])
                    b.inc(ins, "oww", 16)
                ins = sync.dma_start(out=ob_row[:],
                                     in_=bass.AP(EXT["ob_b"].tensor, 0, [[D, 1], [1, D]]))
                b.inc(ins, "oww", 16)
            ow_v = b.val("oww")

            @blk.tensor
            def _(tensor, ow_v=ow_v):
                b.wait(tensor, "oww", ow_v)
                for j in range(4):
                    ins = tensor.matmul(pjB[:, j * 512:(j + 1) * 512], ones_row_sb[:],
                                        ob_row[0:1, j * 512:(j + 1) * 512],
                                        start=True, stop=True)
                b.inc(ins, "pe")
            obv = b.val("pe")

            @blk.vector
            def _(vector, obv=obv):
                b.wait(vector, "pe", obv)
                ins = vector.tensor_copy(ob_bc[:], pjB[:, 0:2048])
                b.inc(ins, "dv")
            ob_v = b.val("dv")

            op_free2 = []
            at_ev = []
            zr_ev = []
            zb_ev = []
            for th in range(2):
                for hg in range(4):
                    for hh in range(4):
                        h = hg * 4 + hh

                        @blk.tensor
                        def _(tensor, h=h, hh=hh, th=th):
                            if th == 0 and h == 0:
                                b.wait(tensor, "dv", ksb_v)
                            if hh == 0 and zb_ev:
                                b.wait(tensor, "dv", zb_ev[-1])
                            if len(zr_ev) >= 2:
                                b.wait(tensor, "dv", zr_ev[-2])
                            ins = tensor.matmul(
                                smB[0:1, (h % 2) * 512:(h % 2) * 512 + 512],
                                ksb[:, h:h + 1],
                                qp_fm[:, h * 1024 + th * 512:h * 1024 + (th + 1) * 512],
                                start=True, stop=True)
                            b.inc(ins, "pe")
                        zv = b.val("pe")

                        @blk.vector
                        def _(vector, h=h, hh=hh, zv=zv):
                            b.wait(vector, "pe", zv)
                            ins = vector.tensor_scalar_add(
                                zrows[0:1, hh * 512:(hh + 1) * 512],
                                smB[0:1, (h % 2) * 512:(h % 2) * 512 + 512], Z_EPS)
                            zr_ev.append(b.inc(ins, "dv"))

                    @blk.vector
                    def _(vector):
                        b.wait(vector, "dv", zr_ev[-1])
                        ins = vector.reciprocal(zrows[:], zrows[:])
                        b.inc(ins, "dv")
                    zrc_v = b.val("dv")

                    for hh in range(4):
                        h = hg * 4 + hh

                        @blk.tensor
                        def _(tensor, h=h, hh=hh, zrc_v=zrc_v):
                            b.wait(tensor, "dv", zrc_v)
                            if len(zb_ev) >= 2:
                                b.wait(tensor, "dv", zb_ev[-2])
                            ins = tensor.matmul(
                                smB[:, (h % 2) * 512:(h % 2) * 512 + 512],
                                ones_row_sb[:], zrows[0:1, hh * 512:(hh + 1) * 512],
                                start=True, stop=True)
                            b.inc(ins, "pe")
                        zbv = b.val("pe")

                        @blk.vector
                        def _(vector, h=h, th=th, zbv=zbv):
                            b.wait(vector, "pe", zbv)
                            qsl = qp_fm[:, h * 1024 + th * 512:h * 1024 + (th + 1) * 512]
                            ins = vector.tensor_tensor(
                                qsl, qsl, smB[:, (h % 2) * 512:(h % 2) * 512 + 512],
                                ALU.mult)
                            zb_ev.append(b.inc(ins, "dv"))

                for h in range(NH):
                    qsc = zb_ev[-(NH - h)]

                    @blk.tensor
                    def _(tensor, h=h, th=th, qsc=qsc):
                        b.wait(tensor, "dv", qsc)
                        if h == 0:
                            b.wait(tensor, "dv", zb_ev[-1])
                        if len(at_ev) >= 2:
                            b.wait(tensor, "dv", at_ev[-2])
                        ins = tensor.matmul(
                            smB[0:128, (h % 2) * 512:(h % 2) * 512 + 512],
                            kvb[:, h * 128:(h + 1) * 128],
                            qp_fm[:, h * 1024 + th * 512:h * 1024 + (th + 1) * 512],
                            start=True, stop=True)
                        b.inc(ins, "pe")
                    nv = b.val("pe")

                    @blk.vector
                    def _(vector, h=h, nv=nv):
                        b.wait(vector, "pe", nv)
                        ins = vector.tensor_copy(at_fm[:, h * 512:(h + 1) * 512],
                                                 smB[0:128, (h % 2) * 512:(h % 2) * 512 + 512])
                        at_ev.append(b.inc(ins, "dv"))
                at_v = at_ev[-1]

                for j in range(4):
                    tb = th * 4 + j

                    @blk.tensor
                    def _(tensor, tb=tb, j=j, th=th, at_v=at_v, ob_v=ob_v):
                        if j == 0:
                            b.wait(tensor, "dv", at_v)
                            b.wait(tensor, "dv", ob_v)
                        if len(op_free2) >= 1:
                            b.wait(tensor, "dv", op_free2[-1])
                        for ec in range(NDC):
                            for ods in range(4):
                                ins = tensor.matmul(
                                    pjB[:, ods * 512:(ods + 1) * 512],
                                    at_fm[:, ec * 512 + j * 128:ec * 512 + (j + 1) * 128],
                                    owT_b[:, ec * D + ods * 512:ec * D + (ods + 1) * 512],
                                    start=(ec == 0), stop=(ec == NDC - 1))
                        b.inc(ins, "pe")
                    ov = b.val("pe")

                    @blk.vector
                    def _(vector, tb=tb, ov=ov):
                        b.wait(vector, "pe", ov)
                        for ods in range(4):
                            xsl = x_res[:, tb * D + ods * 512:tb * D + (ods + 1) * 512]
                            if ods > 0 or tb > 0:
                                b.wait(vector, "dv", b.val("dv"))
                            ins = vector.tensor_add(xsl, xsl,
                                                    pjB[:, ods * 512:(ods + 1) * 512])
                            b.inc(ins, "dv")
                            b.wait(vector, "dv", b.val("dv"))
                            ins = vector.tensor_add(xsl, xsl,
                                                    ob_bc[:, ods * 512:(ods + 1) * 512])
                            b.inc(ins, "dv")
                        op_free2.append(b.val("dv"))
            ps3.close()
            ps2.close()
            psc.close()

        kv_ev = []

        if STAGE >= 2:
            barrier(dma_waits=preps_all)
            emit_ffn("F1")
            emit_dbg("x2")
        if STAGE >= 3:
            barrier()
            emit_attn_b()
            emit_dbg("x3")
        if STAGE >= 4:
            barrier()
            emit_ffn("F2")

        fin = b.val("dv")

        @blk.gpsimd
        def _(gpsimd):
            b.wait(gpsimd, "dv", fin)
            for tb in range(NTB):
                ins = gpsimd.dma_start(out=Y[tb * 128:(tb + 1) * 128, :],
                                       in_=x_res[:, tb * D:(tb + 1) * D])
                b.inc(ins, "st", 16)
            b.wait(gpsimd, "st", b.val("st"))

    stack.close()
    return nc, dbg_outs


def _in_maps(inputs):
    x = np.ascontiguousarray(inputs["L0_attn_a_x"])
    maps = []
    for c in range(8):
        bb, gg = c // 4, c % 4
        h0 = 4 * gg
        maps.append({
            "x0": np.ascontiguousarray(x[bb]),
            "x0m": np.ascontiguousarray(x[bb, gg * TOK:(gg + 1) * TOK]),
            "wq_a": np.ascontiguousarray(inputs["L0_attn_a_Wq"][h0 * DH:(h0 + 4) * DH]),
            "wk_a": np.ascontiguousarray(inputs["L0_attn_a_Wk"][h0 * DH:(h0 + 4) * DH]),
            "wv_a": np.ascontiguousarray(inputs["L0_attn_a_Wv"][h0 * DH:(h0 + 4) * DH]),
            "wo_a": np.ascontiguousarray(inputs["L0_attn_a_Wo"]),
            "qw_b": np.ascontiguousarray(inputs["L0_attn_b_q_w"]),
            "kw_b": np.ascontiguousarray(inputs["L0_attn_b_k_w"]),
            "vw_b": np.ascontiguousarray(inputs["L0_attn_b_v_w"]),
            "ow_b": np.ascontiguousarray(inputs["L0_attn_b_o_w"]),
            "qb_b": np.ascontiguousarray(inputs["L0_attn_b_q_b"]),
            "kb_b": np.ascontiguousarray(inputs["L0_attn_b_k_b"]),
            "vb_b": np.ascontiguousarray(inputs["L0_attn_b_v_b"]),
            "ob_b": np.ascontiguousarray(inputs["L0_attn_b_o_b"]),
            "wg_i": np.ascontiguousarray(inputs["L0_ffn_wq_gate"]),
            "wu_i": np.ascontiguousarray(inputs["L0_ffn_wq_up"]),
            "wd_i": np.ascontiguousarray(inputs["L0_ffn_wq_down"]),
            "sg": np.ascontiguousarray(inputs["L0_ffn_s_gate"]),
            "su": np.ascontiguousarray(inputs["L0_ffn_s_up"]),
            "sd": np.ascontiguousarray(inputs["L0_ffn_s_down"]),
        })
    return maps


def kernel(**inputs):
    nc, _ = build_core_program(0)
    res = run_bass_kernel_spmd(nc, _in_maps(inputs), core_ids=list(range(8)))
    out = np.empty((B_, T, D), dtype=np.float32)
    for c in range(8):
        bb, gg = c // 4, c % 4
        out[bb, gg * TOK:(gg + 1) * TOK, :] = res.results[c]["y"]
    return out


# version 07
# speedup vs baseline: 1.2061x; 1.0582x over previous
import contextlib
import numpy as np
import ml_dtypes
import concourse.bass as bass
import concourse.mybir as mybir
from concourse.bass_utils import run_bass_kernel_spmd

F32 = mybir.dt.float32
BF16 = mybir.dt.bfloat16
I32 = mybir.dt.int32
AF = mybir.ActivationFunctionType
ALU = mybir.AluOpType
AX = mybir.AxisListType

B_, T, D = 2, 4096, 2048
HID = 5632
NH, DH = 16, 128
TOK = 1024
NTB = TOK // 128
NDC = D // 128
NHC = HID // 128
NP = 4
LN_EPS = 1e-5
Z_EPS = 1e-6

STAGE = 4
DBG = ()


class Bld:
    def __init__(self, nc, stack):
        self.nc = nc
        self.stack = stack
        self.cnt = {}
        self.sems = {}

    def sem(self, name):
        if name not in self.sems:
            self.sems[name] = self.stack.enter_context(self.nc.semaphore(name))
            self.cnt[name] = 0
        return self.sems[name]

    def inc(self, ins, name, n=1):
        ins.then_inc(self.sem(name), n)
        self.cnt[name] += n
        return self.cnt[name]

    def val(self, name):
        self.sem(name)
        return self.cnt[name]

    def wait(self, eng, name, v):
        assert 0 <= v <= self.cnt.get(name, 0), ("wait-ahead", name, v, self.cnt.get(name))
        if v > 0:
            eng.wait_ge(self.sem(name), v)


def build_core_program(core):
    nc = bass.Bass("TRN2", target_bir_lowering=False)
    g = core % 4
    to0 = g * TOK
    RG = [[0, 1, 2, 3], [4, 5, 6, 7]]

    X0 = nc.dram_tensor("x0", [T, D], F32, kind="ExternalInput").ap()
    X0M = nc.dram_tensor("x0m", [TOK, D], F32, kind="ExternalInput").ap()
    EXT = {}
    for nm, sh, dt in [("wq_a", [512, D], F32), ("wk_a", [512, D], F32),
                       ("wv_a", [512, D], F32), ("wo_a", [D, D], F32),
                       ("qw_b", [D, D], F32), ("kw_b", [D, D], F32),
                       ("vw_b", [D, D], F32), ("ow_b", [D, D], F32),
                       ("qb_b", [128, 16], F32), ("kb_b", [128, 16], F32),
                       ("vb_b", [128, 16], F32), ("ob_b", [D], F32),
                       ("wg_i", [HID, D], I32), ("wu_i", [HID, D], I32),
                       ("wd_i", [D, HID], I32), ("sg", [128, NHC], F32),
                       ("su", [128, NHC], F32), ("sd", [128, 16], F32)]:
        EXT[nm] = nc.dram_tensor(nm, sh, dt, kind="ExternalInput").ap()
    Y = nc.dram_tensor("y", [TOK, D], F32, kind="ExternalOutput").ap()

    def dram(name, shape, dt=BF16):
        return nc.dram_tensor(name, shape, dt, kind="Internal")

    CAST_SRC = {"wqa": "wq_a", "wka": "wk_a", "wva": "wv_a", "woa": "wo_a",
                "g": "wg_i", "u": "wu_i", "d": "wd_i",
                "qwb": "qw_b", "kwb": "kw_b", "vwb": "vw_b", "owb": "ow_b"}
    WSHAPE = {"wqa": [512, D], "wka": [512, D], "wva": [512, D], "woa": [D, D],
              "g": [HID, D], "u": [HID, D], "d": [D, HID],
              "qwb": [D, D], "kwb": [D, D], "vwb": [D, D], "owb": [D, D]}
    wT = {k: dram(k + "_bf", s) for k, s in WSHAPE.items()}
    ag_in = dram("ag_in", [NP * DH, T])
    ag_out = dram("ag_out", [NH * DH, T])
    ar_in = dram("ar_in", [NH * DH + NH, DH], F32)
    ar_out = dram("ar_out", [NH * DH + NH, DH], F32)

    ident_bf = nc.inline_tensor(np.eye(128, dtype=ml_dtypes.bfloat16), "c_ident_bf")
    ident_f = nc.inline_tensor(np.eye(128, dtype=np.float32), "c_ident_f")
    ones_row_f = nc.inline_tensor(np.ones((1, 128), dtype=np.float32), "c_ones_row")
    row128_f = nc.inline_tensor(np.full((1, 128), 128.0, dtype=np.float32), "c_row128")
    one11_f = nc.inline_tensor(np.ones((1, 1), dtype=np.float32), "c_one11")
    ones_col_bf = nc.inline_tensor(np.ones((128, 1), dtype=ml_dtypes.bfloat16), "c_ones_col")
    eps_col_f = nc.inline_tensor(np.full((128, 1), LN_EPS, dtype=np.float32), "c_eps_col")

    stack = contextlib.ExitStack()
    b = Bld(nc, stack)
    bko_ev = []

    def sball(name, shape, dtype=BF16, st=None):
        return (st or stack).enter_context(nc.sbuf_tensor(name, shape, dtype))

    def psall(name, shape, st=None, dtype=F32):
        return (st or stack).enter_context(nc.psum_tensor(name, shape, dtype))

    ident_bf_sb = sball("ident_bf_sb", [128, 128])
    ident_f_sb = sball("ident_f_sb", [128, 128], F32)
    ones_row_sb = sball("ones_row_sb", [1, 128], F32)
    row128_sb = sball("row128_sb", [1, 128], F32)
    one11_sb = sball("one11_sb", [1, 1], F32)
    ones_col_sb = sball("ones_col_sb", [128, 1])
    eps_sb = sball("eps_sb", [128, 1], F32)
    qb_sb = sball("qb_sb", [128, 16], F32)
    kb_sb = sball("kb_sb", [128, 16], F32)
    vb_sb = sball("vb_sb", [128, 16], F32)
    sg_sb = sball("sg_sb", [128, NHC], F32)
    su_sb = sball("su_sb", [128, NHC], F32)
    sd_sb = sball("sd_sb", [128, 16], F32)
    st_a = sball("st_a", [128, 8], F32)


    dbg_outs = {}

    with nc.Block() as blk:
        @blk.sync
        def _(sync):
            for dst, src in [(ident_bf_sb, ident_bf), (ident_f_sb, ident_f),
                             (ones_row_sb, ones_row_f), (row128_sb, row128_f),
                             (one11_sb, one11_f), (ones_col_sb, ones_col_bf),
                             (eps_sb, eps_col_f)]:
                ins = sync.dma_start(out=dst[:], in_=src.ap())
                b.inc(ins, "ko", 16)
        konst_v = b.val("ko")

        @blk.sync
        def _(sync):
            for dst, srcnm, n in [(qb_sb, "qb_b", 16), (kb_sb, "kb_b", 16),
                                  (vb_sb, "vb_b", 16), (sg_sb, "sg", NHC),
                                  (su_sb, "su", NHC), (sd_sb, "sd", 16)]:
                ins = sync.dma_start(out=dst[:, 0:n], in_=EXT[srcnm])
                b.inc(ins, "bko", 16)
        bias_v = b.val("bko")
        for engname in ("vector", "scalar"):
            @getattr(blk, engname)
            def _(eng):
                b.wait(eng, "bko", bias_v)

        cast_done = {}
        prep_done = {}

        @blk.gpsimd
        def _(gpsimd):
            for key in ["wqa", "wka", "wva"]:
                ins = gpsimd.dma_start(out=wT[key].ap(), in_=EXT[CAST_SRC[key]])
                cast_done[key] = b.inc(ins, "ca_" + key, 16)
                prep_done[key] = [("ca_" + key, 16)]

        for engname in ("tensor", "vector", "scalar"):
            @getattr(blk, engname)
            def _(eng):
                b.wait(eng, "ko", konst_v)

        @blk.gpsimd
        def _(gpsimd):
            for key in ["g", "woa", "u", "d", "qwb", "kwb", "vwb", "owb"]:
                ins = gpsimd.dma_start(out=wT[key].ap(), in_=EXT[CAST_SRC[key]])
                cast_done[key] = b.inc(ins, "ca_" + key, 16)
                prep_done[key] = [("ca_" + key, 16)]

        def ln_tile(x_ap, out_tile, par, sqt, colsum_to=None, scalar_gate=0,
                    x_gate=None):
            c = par * 4

            @blk.vector
            def _(vector):
                ins = vector.reduce_sum(st_a[:, c:c + 1], x_ap, axis=AX.X)
                b.inc(ins, "dv")

            @blk.scalar
            def _(scalar):
                b.wait(scalar, "pe", scalar_gate)
                if x_gate is not None:
                    b.wait(scalar, x_gate[0], x_gate[1])
                ins = scalar.activation(sqt[:], x_ap, AF.Square,
                                        accum_out=st_a[:, c + 1:c + 2])
                b.inc(ins, "ac")
            v0, a0 = b.val("dv"), b.val("ac")

            @blk.vector
            def _(vector):
                b.wait(vector, "dv", v0)
                b.wait(vector, "ac", a0)
                ins = vector.tensor_scalar_mul(st_a[:, c:c + 1], st_a[:, c:c + 1], 1.0 / D)
                b.inc(ins, "dv")
                b.wait(vector, "dv", b.val("dv"))
                ins = vector.tensor_mul(st_a[:, c + 2:c + 3], st_a[:, c:c + 1],
                                        st_a[:, c:c + 1])
                b.inc(ins, "dv")
                b.wait(vector, "dv", b.val("dv"))
                ins = vector.tensor_scalar(st_a[:, c + 1:c + 2], st_a[:, c + 1:c + 2],
                                           1.0 / D, None, ALU.mult)
                b.inc(ins, "dv")
                b.wait(vector, "dv", b.val("dv"))
                ins = vector.tensor_sub(st_a[:, c + 1:c + 2], st_a[:, c + 1:c + 2],
                                        st_a[:, c + 2:c + 3])
                b.inc(ins, "dv")
            v1 = b.val("dv")

            @blk.scalar
            def _(scalar):
                b.wait(scalar, "dv", v1)
                ins = scalar.activation(st_a[:, c + 2:c + 3], st_a[:, c + 1:c + 2],
                                        AF.Ln, bias=eps_sb[:, 0:1])
                b.inc(ins, "ac")
                b.wait(scalar, "ac", b.val("ac"))
                ins = scalar.activation(st_a[:, c + 3:c + 4], st_a[:, c + 2:c + 3],
                                        AF.Exp, scale=-0.5)
                b.inc(ins, "ac")
            a1 = b.val("ac")

            @blk.vector
            def _(vector):
                b.wait(vector, "ac", a1)
                ins = vector.tensor_scalar(out_tile[:], x_ap, st_a[:, c:c + 1],
                                           st_a[:, c + 3:c + 4], ALU.subtract, ALU.mult)
                b.inc(ins, "dv")
                if colsum_to is not None:
                    t, cc = colsum_to
                    b.wait(vector, "dv", b.val("dv"))
                    ins = vector.reduce_sum(t[:, cc:cc + 1], out_tile[:], axis=AX.X)
                    b.inc(ins, "dv")
            return b.val("dv")

        tstate = {"n": 0, "hist": []}

        def transpose_blocks(tp_ps, srcs, dsts, pe_waits=(), f32=False):
            ident = ident_f_sb if f32 else ident_bf_sb
            for i, ((s_t, s_c), (d_t, d_c)) in enumerate(zip(srcs, dsts)):
                par = tstate["n"] % 2
                tstate["n"] += 1

                @blk.tensor
                def _(tensor, s_t=s_t, s_c=s_c, par=par, i=i):
                    for nm, v in (pe_waits if i == 0 else ()):
                        b.wait(tensor, nm, v)
                    if len(tstate["hist"]) >= 2:
                        b.wait(tensor, "dv", tstate["hist"][-2])
                    ins = tensor.matmul(tp_ps[:, par * 512:par * 512 + 128],
                                        s_t[:, s_c:s_c + 128], ident[:],
                                        start=True, stop=True)
                    b.inc(ins, "pe")
                pv = b.val("pe")

                @blk.vector
                def _(vector, d_t=d_t, d_c=d_c, par=par, pv=pv):
                    b.wait(vector, "pe", pv)
                    ins = vector.tensor_copy(d_t[:, d_c:d_c + 128],
                                             tp_ps[:, par * 512:par * 512 + 128])
                    tstate["hist"].append(b.inc(ins, "dv"))
            return b.val("dv")

        def barrier(dma_waits=()):
            for engname in ("tensor", "vector", "scalar"):
                @getattr(blk, engname)
                def _(eng):
                    ins = eng.nop()
                    b.inc(ins, "bar")
            t = b.val("bar")
            for engname in ("tensor", "vector", "scalar", "sync", "gpsimd"):
                @getattr(blk, engname)
                def _(eng):
                    b.wait(eng, "bar", t)
                    for nm, v in dma_waits:
                        b.wait(eng, nm, v)

        def emit_dbg(name):
            if name not in DBG:
                return
            out = nc.dram_tensor("dbg_" + name, [TOK, D], F32, kind="ExternalOutput").ap()
            dbg_outs[name] = out
            wv = b.val("dv")

            @blk.gpsimd
            def _(gpsimd):
                b.wait(gpsimd, "dv", wv)
                for tb in range(NTB):
                    ins = gpsimd.dma_start(out=out[tb * 128:(tb + 1) * 128, :],
                                           in_=x_res[:, tb * D:(tb + 1) * D])
                    b.inc(ins, "dbg", 16)

        pa = contextlib.ExitStack()
        q_fm = sball("q_fm", [128, NP * T], BF16, pa)
        k_fm = sball("k_fm", [128, NP * T], BF16, pa)
        v_tm = sball("v_tm", [128, NP * 32 * 128], BF16, pa)
        tp_ps = psall("tp_psA", [128, 1024], pa)

        pa1 = contextlib.ExitStack()
        lnx_fm2 = [sball(f"lnx_fm{i}", [128, NDC * 512], BF16, pa1) for i in range(2)]
        v_fms = sball("v_fms", [128, NP * 512], BF16, pa1)
        x0t = [sball(f"x0t{i}", [128, D], F32, pa1) for i in range(2)]
        sqt_a = sball("sqt_a", [128, D], BF16, pa1)
        lnq_tm = [sball(f"lnq{i}", [128, D], BF16, pa1) for i in range(2)]
        aring = [sball(f"aring{i}", [128, NDC * 128], BF16, pa1) for i in range(2)]
        pj_ps = psall("pj_psA", [128, 2048], pa1)

        x0_free = [0, 0]
        lnq_free = [0, 0]
        aring_free = []
        pj_free = []
        a2_done = [0, 0]

        for sl in range(8):
            lnx_fm = lnx_fm2[sl % 2]
            for tt in range(4):
                tile = sl * 4 + tt
                par = tile % 2

                @blk.scalar
                def _(scalar, tile=tile, par=par):
                    b.wait(scalar, "dv", x0_free[par])
                    ins = scalar.dma_start(out=x0t[par][:],
                                           in_=X0[tile * 128:(tile + 1) * 128, :])
                    b.inc(ins, "x0w%d" % par, 16)
                xv = b.val("x0w%d" % par)

                @blk.vector
                def _(vector, par=par, xv=xv):
                    b.wait(vector, "x0w%d" % par, xv)
                    b.wait(vector, "pe", lnq_free[par])
                lv = ln_tile(x0t[par][:], lnq_tm[par], par, sqt_a,
                             x_gate=("x0w%d" % par, xv))
                x0_free[par] = lv
                transpose_blocks(
                    tp_ps,
                    [(lnq_tm[par], dc * 128) for dc in range(NDC)],
                    [(lnx_fm, dc * 512 + tt * 128) for dc in range(NDC)],
                    pe_waits=[("dv", lv), ("pe", a2_done[sl % 2])],
                )

                lnq_free[par] = b.val("pe")
            slab_tp = b.val("dv")

            for iw, key in enumerate(["wqa", "wka", "wva"]):
                for oc in range(NP):
                    li = len(aring_free)
                    slot = li % 2

                    @blk.scalar
                    def _(scalar, key=key, oc=oc, li=li, slot=slot):
                        for nm, v in prep_done[key]:
                            b.wait(scalar, nm, v)
                        if li >= 2:
                            b.wait(scalar, "pe", aring_free[li - 2])
                        ins = scalar.dma_start(
                            out=aring[slot][:],
                            in_=wT[key].ap()[oc * 128:(oc + 1) * 128, 0:D])
                        b.inc(ins, "aw%d" % slot, 16)
                    wv = b.val("aw%d" % slot)
                    gi = sl * 12 + iw * NP + oc
                    bank = gi % 4

                    @blk.tensor
                    def _(tensor, slot=slot, gi=gi, bank=bank, wv=wv, slab_tp=slab_tp):
                        b.wait(tensor, "aw%d" % slot, wv)
                        b.wait(tensor, "dv", slab_tp)
                        if gi >= 4:
                            b.wait(tensor, "dv", pj_free[gi - 4])
                        for dc in range(NDC):
                            ins = tensor.matmul(
                                pj_ps[:, bank * 512:(bank + 1) * 512],
                                aring[slot][:, dc * 128:(dc + 1) * 128],
                                lnx_fm[:, dc * 512:(dc + 1) * 512],
                                start=(dc == 0), stop=(dc == NDC - 1))
                        b.inc(ins, "pe")
                    aring_free.append(b.val("pe"))
                    mv = b.val("pe")

                    @blk.vector
                    def _(vector, iw=iw, oc=oc, sl=sl, bank=bank, mv=mv):
                        b.wait(vector, "pe", mv)
                        dst = [q_fm, k_fm, v_fms][iw]
                        col = (oc * T + sl * 512) if iw < 2 else (oc * 512)
                        if iw == 0:
                            ins = vector.tensor_scalar_mul(
                                dst[:, col:col + 512],
                                pj_ps[:, bank * 512:(bank + 1) * 512],
                                1.0 / np.sqrt(float(DH)))
                        else:
                            ins = vector.tensor_copy(dst[:, col:col + 512],
                                                     pj_ps[:, bank * 512:(bank + 1) * 512])
                        pj_free.append(b.inc(ins, "dv"))
            vv = b.val("dv")
            transpose_blocks(
                tp_ps,
                [(v_fms, p * 512 + kk * 128) for p in range(NP) for kk in range(4)],
                [(v_tm, (p * 32 + (sl * 4 + kk)) * 128) for p in range(NP)
                 for kk in range(4)],
                pe_waits=[("dv", vv)],
            )

            a2_done[sl % 2] = b.val("pe")

        pa1.close()
        import os as _os
        if _os.environ.get("KMEM"):
            print("after A2 close:", nc.sbuf_bytes_remaining)
        flash_dv0 = b.val("dv")

        s_ps = [psall("s_ps0", [128, 512], pa), psall("s_ps1", [128, 512], pa)]
        st_ps = [psall("st_ps0", [128, 512], pa), psall("st_ps1", [128, 512], pa)]
        pv_ps = psall("pv_ps", [128, 512], pa)
        rb_ps = psall("rb_ps", [128, 512], pa)
        p_t = [sball(f"p_t{i}", [128, 512], BF16, pa) for i in range(4)]
        d_sb = [sball(f"d_sb{i}", [128, 512], F32, pa) for i in range(2)]
        maxbc_sb = [sball(f"maxbc{i}", [128, 512], F32, pa) for i in range(2)]
        mx8_sb = sball("mx8_sb", [128, 16], F32, pa)
        nmax_sb = sball("nmax_sb", [128, 8], F32, pa)
        nmrow_sb = sball("nmrow_sb", [1, 1024], F32, pa)
        ao_sb = [sball(f"ao_sb{i}", [128, 512], BF16, pa) for i in range(2)]
        rrow_sb = sball("rrow_sb", [1, 512], F32, pa)
        bcast_sb = sball("bcast_sb", [128, 512], F32, pa)

        s_use = []
        nm_dv = []
        nmT_pe = []
        nmrow_dv = []
        mbc_pe = []
        mbc_dv = []
        sub_dv = []
        exp_ac = []
        pvrs_pe = []
        sub_last = []
        rr_pe = []
        pv_ev = []
        ag_wr = []

        def emit_pass1(gqs):
            p, qs = gqs // 8, gqs % 8
            qc = p * T
            for qb in range(4):
                qi = qs * 4 + qb
                m = gqs * 4 + qb
                for kc8 in range(8):
                    n = m * 8 + kc8
                    grp = n % 2

                    @blk.tensor
                    def _(tensor, qi=qi, kc8=kc8, grp=grp, n=n, qc=qc):
                        if n >= 2:
                            nm, v = s_use[n - 2]
                            b.wait(tensor, nm, v)
                        else:
                            b.wait(tensor, "dv", flash_dv0)
                        ins = tensor.matmul(
                            s_ps[grp][:],
                            q_fm[:, qc + qi * 128:qc + (qi + 1) * 128],
                            k_fm[:, qc + kc8 * 512:qc + (kc8 + 1) * 512],
                            start=True, stop=True)
                        b.inc(ins, "pe")
                    sv = b.val("pe")

                    @blk.vector
                    def _(vector, sv=sv, grp=grp, qi=qi, kc8=kc8, m=m):
                        b.wait(vector, "pe", sv)
                        if m >= 2 and kc8 == 0:
                            b.wait(vector, "dv", nm_dv[m - 2])
                        ins = vector.reduce_max(
                            mx8_sb[:, (qi % 2) * 8 + kc8:(qi % 2) * 8 + kc8 + 1],
                            s_ps[grp][:], axis=AX.X)
                        s_use.append(("dv", b.inc(ins, "dv")))

                @blk.vector
                def _(vector, qi=qi, qb=qb, gqs=gqs):
                    b.wait(vector, "dv", b.val("dv"))
                    if qb == 0 and gqs >= 2:
                        b.wait(vector, "pe", nmT_pe[gqs - 2])
                    ins = vector.reduce_max(
                        nmax_sb[:, (gqs % 2) * 4 + qb:(gqs % 2) * 4 + qb + 1],
                        mx8_sb[:, (qi % 2) * 8:(qi % 2) * 8 + 8],
                        axis=AX.X, negate=True)
                    nm_dv.append(b.inc(ins, "dv"))

        def emit_pass2(gqs):
            p, qs = gqs // 8, gqs % 8
            g2 = gqs % 2
            c0 = g2 * 4

            @blk.tensor
            def _(tensor, gqs=gqs, c0=c0):
                b.wait(tensor, "dv", nm_dv[gqs * 4 + 3])
                if gqs >= 1:
                    b.wait(tensor, "dv", nmrow_dv[gqs - 1])
                for qb in range(4):
                    ins = tensor.matmul(tp_ps[0:1, qb * 128:(qb + 1) * 128],
                                        nmax_sb[:, c0 + qb:c0 + qb + 1],
                                        ident_f_sb[:], start=True, stop=True)
                nmT_pe.append(b.inc(ins, "pe"))

            @blk.vector
            def _(vector, gqs=gqs, g2=g2):
                b.wait(vector, "pe", nmT_pe[-1])
                if gqs >= 1:
                    b.wait(vector, "pe", mbc_pe[gqs - 1])
                ins = vector.tensor_copy(
                    nmrow_sb[0:1, g2 * 512:(g2 + 1) * 512], tp_ps[0:1, 0:512])
                nmrow_dv.append(b.inc(ins, "dv"))

            @blk.tensor
            def _(tensor, gqs=gqs, g2=g2):
                b.wait(tensor, "dv", nmrow_dv[-1])
                if gqs >= 1:
                    b.wait(tensor, "dv", sub_last[gqs - 1])
                ins = tensor.matmul(
                    st_ps[0][:], ones_row_sb[:],
                    nmrow_sb[0:1, g2 * 512:(g2 + 1) * 512],
                    start=True, stop=True)
                mbc_pe.append(b.inc(ins, "pe"))

            @blk.vector
            def _(vector, gqs=gqs, g2=g2):
                b.wait(vector, "pe", mbc_pe[-1])
                if gqs >= 2:
                    b.wait(vector, "dv", sub_last[gqs - 2])
                ins = vector.tensor_copy(maxbc_sb[g2][:], st_ps[0][:, 0:512])
                mbc_dv.append(b.inc(ins, "dv"))
            mv = b.val("dv")

            def emit_pvrs(kc, gqs=gqs, p=p):
                t = gqs * 32 + kc

                @blk.tensor
                def _(tensor, kc=kc, t=t, p=p, gqs=gqs):
                    b.wait(tensor, "ac", exp_ac[t])
                    if kc == 0 and gqs >= 1:
                        b.wait(tensor, "dv", pv_ev[gqs - 1])
                    ins = tensor.matmul(
                        pv_ps[:],
                        v_tm[:, (p * 32 + kc) * 128:(p * 32 + kc + 1) * 128],
                        p_t[t % 4][:], start=(kc == 0), stop=(kc == 31))
                    b.inc(ins, "pe")
                    ins = tensor.matmul(
                        rb_ps[0:1, 0:512], ones_col_sb[:], p_t[t % 4][:],
                        start=(kc == 0), stop=(kc == 31))
                    pvrs_pe.append(b.inc(ins, "pe"))

            for kc in range(32):
                par = kc % 2
                t = gqs * 32 + kc

                @blk.tensor
                def _(tensor, kc=kc, par=par, t=t, p=p, qs=qs, mv=mv):
                    if kc >= 2:
                        b.wait(tensor, "dv", sub_dv[t - 2])
                    else:
                        b.wait(tensor, "dv", mv)
                    ins = tensor.matmul(
                        st_ps[par][:],
                        k_fm[:, p * T + kc * 128:p * T + (kc + 1) * 128],
                        q_fm[:, p * T + qs * 512:p * T + (qs + 1) * 512],
                        start=True, stop=True)
                    b.inc(ins, "pe")
                stv = b.val("pe")

                @blk.vector
                def _(vector, stv=stv, par=par, t=t, g2=g2):
                    b.wait(vector, "pe", stv)
                    if t >= 2:
                        b.wait(vector, "ac", exp_ac[t - 2])
                    ins = vector.tensor_add(d_sb[par][:], st_ps[par][:],
                                            maxbc_sb[g2][:])
                    sub_dv.append(b.inc(ins, "dv"))

                @blk.scalar
                def _(scalar, par=par, t=t):
                    b.wait(scalar, "dv", sub_dv[t])
                    if t >= 4:
                        b.wait(scalar, "pe", pvrs_pe[t - 4])
                    ins = scalar.activation(p_t[t % 4][:], d_sb[par][:], AF.Exp)
                    exp_ac.append(b.inc(ins, "ac"))

                if kc >= 2:
                    emit_pvrs(kc - 2)
            sub_last.append(sub_dv[-1])
            emit_pvrs(30)
            emit_pvrs(31)
            pvv = pvrs_pe[-1]

            @blk.vector
            def _(vector, gqs=gqs, pvv=pvv):
                b.wait(vector, "pe", pvv)
                if gqs >= 1:
                    b.wait(vector, "pe", rr_pe[gqs - 1])
                ins = vector.reciprocal(rrow_sb[:], rb_ps[0:1, 0:512])
                b.inc(ins, "dv")
            rrv = b.val("dv")

            @blk.tensor
            def _(tensor, rrv=rrv):
                b.wait(tensor, "dv", rrv)
                ins = tensor.matmul(rb_ps[:, 0:512], ones_row_sb[:], rrow_sb[:],
                                    start=True, stop=True)
                rr_pe.append(b.inc(ins, "pe"))

            @blk.vector
            def _(vector, gqs=gqs):
                b.wait(vector, "pe", rr_pe[-1])
                ins = vector.tensor_copy(bcast_sb[:], rb_ps[:, 0:512])
                b.inc(ins, "dv")
                b.wait(vector, "dv", b.val("dv"))
                if gqs >= 2:
                    b.wait(vector, "agw", b.val("agw"))
                ins = vector.tensor_mul(ao_sb[gqs % 2][:], pv_ps[:],
                                        bcast_sb[:])
                pv_ev.append(b.inc(ins, "dv"))
            aov = pv_ev[-1]

            @blk.scalar
            def _(scalar, p=p, qs=qs, gqs=gqs, aov=aov):
                b.wait(scalar, "dv", aov)
                ins = scalar.dma_start(
                    out=ag_in.ap()[p * 128:(p + 1) * 128, qs * 512:(qs + 1) * 512],
                    in_=ao_sb[gqs % 2][:])
                ag_wr.append(b.inc(ins, "agw", 16))

        for gqs in range(NP * 8):
            emit_pass1(gqs)
            if gqs >= 1:
                emit_pass2(gqs - 1)
        emit_pass2(NP * 8 - 1)

        flash_done = pv_ev[-1]
        ag_all = ag_wr[-1]
        pa.close()
        preps_all = [("ca_" + k, 16) for k in ["g", "u", "d", "qwb", "kwb",
                                                "vwb", "owb"]]
        import os as _os2
        if _os2.environ.get("KMEM"):
            print("after flash close:", nc.sbuf_bytes_remaining)

        @blk.gpsimd
        def _(gpsimd):
            b.wait(gpsimd, "agw", ag_all)
            ins = gpsimd.collective_compute(
                "AllGather", ALU.bypass, RG, [ag_in.ap().opt()], [ag_out.ap().opt()])
            b.inc(ins, "coll")
        coll_ag = b.val("coll")

        x_res = sball("x_res", [128, NTB * D], F32)
        pa6 = contextlib.ExitStack()
        og_sb = sball("og_sb", [128, NH * 1024], BF16, pa6)
        woT_sb = sball("woT_sb", [128, NH * D], BF16, pa6)
        x0s = [sball(f"x0s{i}", [128, D], F32, pa6) for i in range(2)]
        op_ps = psall("op_psA", [128, 2048], pa6)

        @blk.sync
        def _(sync):
            for nm, v in preps_all:
                b.wait(sync, nm, v)
            b.wait(sync, "ca_woa", 16)
            b.wait(sync, "dv", flash_done)
            b.wait(sync, "agw", ag_all)
            for h in range(NH):
                ins = sync.dma_start(out=woT_sb[:, h * D:(h + 1) * D],
                                     in_=wT["woa"].ap()[h * 128:(h + 1) * 128, 0:D])
                b.inc(ins, "wow", 16)
        woT_v = b.val("wow")

        @blk.gpsimd
        def _(gpsimd):
            b.wait(gpsimd, "coll", coll_ag)
            pid = gpsimd.partition_id()
            to0v = gpsimd.compute_val((pid % 4) * TOK)
            for h in range(NH):
                ins = gpsimd.dma_start(
                    out=og_sb[:, h * 1024:(h + 1) * 1024],
                    in_=ag_out.ap()[h * 128:(h + 1) * 128, bass.ds(to0v, TOK)])
                b.inc(ins, "ogw", 16)
        og_v = b.val("ogw")

        op_free = []
        for tb in range(NTB):
            par = tb % 2

            @blk.scalar
            def _(scalar, tb=tb, par=par):
                b.wait(scalar, "dv", flash_done)
                b.wait(scalar, "agw", ag_all)
                if tb >= 2:
                    b.wait(scalar, "dv", op_free[tb - 2])
                ins = scalar.dma_start(out=x0s[par][:],
                                       in_=X0M[tb * 128:(tb + 1) * 128, :])
                b.inc(ins, "xsw%d" % par, 16)
            xv = b.val("xsw%d" % par)

            @blk.tensor
            def _(tensor, tb=tb, xv=xv):
                if tb == 0:
                    b.wait(tensor, "ogw", og_v)
                    b.wait(tensor, "wow", woT_v)
                if tb >= 1:
                    b.wait(tensor, "dv", op_free[tb - 1])
                for h in range(NH):
                    for ods in range(4):
                        ins = tensor.matmul(
                            op_ps[:, ods * 512:(ods + 1) * 512],
                            og_sb[:, h * 1024 + tb * 128:h * 1024 + (tb + 1) * 128],
                            woT_sb[:, h * D + ods * 512:h * D + (ods + 1) * 512],
                            start=(h == 0), stop=(h == NH - 1))
                b.inc(ins, "pe")
            ov = b.val("pe")

            @blk.vector
            def _(vector, tb=tb, par=par, ov=ov, xv=xv):
                b.wait(vector, "pe", ov)
                b.wait(vector, "xsw%d" % par, xv)
                for ods in range(4):
                    ins = vector.tensor_add(
                        x_res[:, tb * D + ods * 512:tb * D + (ods + 1) * 512],
                        op_ps[:, ods * 512:(ods + 1) * 512],
                        x0s[par][:, ods * 512:(ods + 1) * 512])
                op_free.append(b.inc(ins, "dv"))
        pa6.close()
        import os as _os3
        if _os3.environ.get("KMEM"):
            print("after A6 close:", nc.sbuf_bytes_remaining)
        emit_dbg("x1")

        def emit_ffn(tag):
            ps = contextlib.ExitStack()
            sb = lambda n, s, d=BF16: sball(n + tag, s, d, ps)
            lnt = [sb(f"lnt{i}", [128, D]) for i in range(2)]
            sqt = sb("sqt", [128, D])
            lnf = sb("lnf", [128, NDC * 1024])
            cs_cols = sb("cs_cols", [128, NTB], F32)
            csr = sb("csr", [1, 1024], F32)
            cs_bc = sb("cs_bc", [128, 1024], F32)
            cshr = sb("cshr", [1, 512], F32)
            csh_bc = sb("csh_bc", [128, 512], F32)
            h_fm = sb("h_fm", [128, NHC * 512])
            tg_sc = [sb(f"tg{i}", [128, 512], F32) for i in range(2)]
            tu_sc = [sb(f"tu{i}", [128, 512], F32) for i in range(2)]
            gsig = [sb(f"gsig{i}", [128, 512]) for i in range(2)]
            dnf = sb("dnf", [128, 2048], F32)
            wring = [sb(f"wring{i}", [128, NDC * 128]) for i in range(3)]
            dring = [sb(f"dring{i}", [128, (NHC // 2) * 128]) for i in range(2)]
            tp2 = psall("tp2" + tag, [128, 1024], ps)
            gu_ps = psall("gu" + tag, [128, 2048], ps)
            dn_ps = psall("dn" + tag, [128, 1024], ps)

            lnt_free = [0, 0]
            for tb in range(NTB):
                par = tb % 2

                @blk.vector
                def _(vector, par=par):
                    b.wait(vector, "pe", lnt_free[par])
                lv = ln_tile(x_res[:, tb * D:(tb + 1) * D], lnt[par], par, sqt,
                             colsum_to=(cs_cols, tb))
                transpose_blocks(
                    tp2,
                    [(lnt[par], dc * 128) for dc in range(NDC)],
                    [(lnf, dc * 1024 + tb * 128) for dc in range(NDC)],
                    pe_waits=[("dv", lv)],
                )
                lnt_free[par] = b.val("pe")
            lnf_v = b.val("dv")

            @blk.tensor
            def _(tensor, lnf_v=lnf_v):
                b.wait(tensor, "dv", lnf_v)
                for tb in range(NTB):
                    ins = tensor.matmul(gu_ps[0:1, tb * 128:(tb + 1) * 128],
                                        cs_cols[:, tb:tb + 1], ident_f_sb[:],
                                        start=True, stop=True)
                b.inc(ins, "pe")
            csv = b.val("pe")

            @blk.vector
            def _(vector, csv=csv):
                b.wait(vector, "pe", csv)
                ins = vector.tensor_copy(csr[:], gu_ps[0:1, 0:1024])
                b.inc(ins, "dv")
            csrv = b.val("dv")

            @blk.tensor
            def _(tensor, csrv=csrv):
                b.wait(tensor, "dv", csrv)
                tensor.matmul(gu_ps[:, 0:512], row128_sb[:], csr[0:1, 0:512],
                              start=True, stop=True)
                ins = tensor.matmul(gu_ps[:, 512:1024], row128_sb[:], csr[0:1, 512:1024],
                                    start=True, stop=True)
                b.inc(ins, "pe")
            csbv = b.val("pe")

            @blk.vector
            def _(vector, csbv=csbv):
                b.wait(vector, "pe", csbv)
                ins = vector.tensor_copy(cs_bc[:], gu_ps[:, 0:1024])
                b.inc(ins, "dv")
            csbc_v = b.val("dv")

            wring_free = []
            dring_free = []
            gu_free = []
            dn_free = []
            th_gate_wait = csbc_v
            for th in range(2):
                for oc in range(NHC):
                    idx = th * NHC + oc
                    for iw, key in enumerate(["g", "u"]):
                        li = idx * 2 + iw
                        slot = li % 3

                        @blk.sync
                        def _(sync, key=key, oc=oc, li=li, slot=slot):
                            for nm, v in prep_done[key]:
                                b.wait(sync, nm, v)
                            if li >= 3:
                                b.wait(sync, "pe", wring_free[li - 3])
                            ins = sync.dma_start(
                                out=wring[slot][:],
                                in_=wT[key].ap()[oc * 128:(oc + 1) * 128, 0:D])
                            b.inc(ins, "fw%d" % slot, 16)
                    wv = [b.val("fw%d" % ((idx * 2 + iw) % 3)) for iw in range(2)]
                    par = idx % 2

                    @blk.tensor
                    def _(tensor, th=th, idx=idx, par=par, wv=wv,
                          th_gate_wait=th_gate_wait):
                        for iw in range(2):
                            b.wait(tensor, "fw%d" % ((idx * 2 + iw) % 3), wv[iw])
                        if idx % NHC == 0:
                            b.wait(tensor, "dv", th_gate_wait)
                        if idx >= 2:
                            b.wait(tensor, "dv", gu_free[idx - 2])
                        for iw in range(2):
                            slot = (idx * 2 + iw) % 3
                            bank = iw * 2 + par
                            for dc in range(NDC):
                                ins = tensor.matmul(
                                    gu_ps[:, bank * 512:(bank + 1) * 512],
                                    wring[slot][:, dc * 128:(dc + 1) * 128],
                                    lnf[:, dc * 1024 + th * 512:dc * 1024 + (th + 1) * 512],
                                    start=(dc == 0), stop=(dc == NDC - 1))
                            b.inc(ins, "pe")
                        wring_free.append(b.val("pe"))
                        wring_free.append(b.val("pe"))
                    gv = b.val("pe")

                    @blk.vector
                    def _(vector, oc=oc, th=th, par=par, gv=gv):
                        b.wait(vector, "pe", gv)
                        ins = vector.scalar_tensor_tensor(
                            tg_sc[par][:], gu_ps[:, par * 512:(par + 1) * 512],
                            1.0, cs_bc[:, th * 512:(th + 1) * 512],
                            ALU.mult, ALU.subtract)
                        b.inc(ins, "dv")
                        b.wait(vector, "dv", b.val("dv"))
                        ins = vector.tensor_scalar_mul(tg_sc[par][:], tg_sc[par][:],
                                                       sg_sb[:, oc:oc + 1])
                        b.inc(ins, "dv")
                        ins = vector.tensor_sub(tu_sc[par][:],
                                                gu_ps[:, (2 + par) * 512:(3 + par) * 512],
                                                cs_bc[:, th * 512:(th + 1) * 512])
                        b.inc(ins, "dv")
                    sv = b.val("dv")

                    @blk.scalar
                    def _(scalar, par=par, sv=sv):
                        b.wait(scalar, "dv", sv)
                        ins = scalar.activation(gsig[par][:], tg_sc[par][:], AF.Silu)
                        b.inc(ins, "ac")
                    av = b.val("ac")

                    @blk.vector
                    def _(vector, oc=oc, par=par, av=av):
                        b.wait(vector, "ac", av)
                        ins = vector.scalar_tensor_tensor(
                            h_fm[:, oc * 512:(oc + 1) * 512], tu_sc[par][:],
                            su_sb[:, oc:oc + 1], gsig[par][:], ALU.mult, ALU.mult)
                        gu_free.append(b.inc(ins, "dv"))
                h_v = gu_free[-1]

                @blk.tensor
                def _(tensor, h_v=h_v):
                    b.wait(tensor, "dv", h_v)
                    for hc in range(NHC):
                        ins = tensor.matmul(gu_ps[0:1, 0:512], ones_col_sb[:],
                                            h_fm[:, hc * 512:(hc + 1) * 512],
                                            start=(hc == 0), stop=(hc == NHC - 1))
                    b.inc(ins, "pe")
                chv = b.val("pe")

                @blk.vector
                def _(vector, chv=chv):
                    b.wait(vector, "pe", chv)
                    ins = vector.tensor_scalar_mul(cshr[:], gu_ps[0:1, 0:512], 128.0)
                    b.inc(ins, "dv")
                chrv = b.val("dv")

                @blk.tensor
                def _(tensor, chrv=chrv):
                    b.wait(tensor, "dv", chrv)
                    ins = tensor.matmul(gu_ps[:, 512:1024], ones_row_sb[:], cshr[:],
                                        start=True, stop=True)
                    b.inc(ins, "pe")
                chbv = b.val("pe")

                @blk.vector
                def _(vector, chbv=chbv):
                    b.wait(vector, "pe", chbv)
                    ins = vector.tensor_copy(csh_bc[:], gu_ps[:, 512:1024])
                    b.inc(ins, "dv")
                cshbc_v = b.val("dv")
                th_gate_wait = cshbc_v

                HH = NHC // 2
                for dimc in range(NDC):
                    di = th * NDC + dimc
                    bank = di % 2
                    dwvs = []
                    for hf in range(2):
                        hi = di * 2 + hf

                        @blk.sync
                        def _(sync, dimc=dimc, hf=hf, hi=hi):
                            for nm, v in prep_done["d"]:
                                b.wait(sync, nm, v)
                            if hi >= 2:
                                b.wait(sync, "pe", dring_free[hi - 2])
                            ins = sync.dma_start(
                                out=dring[hf][:],
                                in_=wT["d"].ap()[dimc * 128:(dimc + 1) * 128,
                                                 hf * HH * 128:(hf + 1) * HH * 128])
                            b.inc(ins, "dw%d" % hf, 16)
                        dwvs.append(b.val("dw%d" % hf))

                    for hf in range(2):
                        @blk.tensor
                        def _(tensor, di=di, hf=hf, bank=bank, dwv=dwvs[hf],
                              cshbc_v=cshbc_v):
                            b.wait(tensor, "dw%d" % hf, dwv)
                            if di % NDC == 0 and hf == 0:
                                b.wait(tensor, "dv", cshbc_v)
                            if di >= 2 and hf == 0:
                                b.wait(tensor, "dv", dn_free[di - 2])
                            for hc2 in range(HH):
                                hc = hf * HH + hc2
                                ins = tensor.matmul(
                                    dn_ps[:, bank * 512:(bank + 1) * 512],
                                    dring[hf][:, hc2 * 128:(hc2 + 1) * 128],
                                    h_fm[:, hc * 512:(hc + 1) * 512],
                                    start=(hc == 0), stop=(hc == NHC - 1))
                            b.inc(ins, "pe")
                            dring_free.append(b.val("pe"))
                    dmv = b.val("pe")

                    @blk.vector
                    def _(vector, dimc=dimc, bank=bank, dmv=dmv):
                        b.wait(vector, "pe", dmv)
                        ins = vector.tensor_sub(dnf[:, bank * 1024:bank * 1024 + 512],
                                                dn_ps[:, bank * 512:(bank + 1) * 512],
                                                csh_bc[:])
                        b.inc(ins, "dv")
                        b.wait(vector, "dv", b.val("dv"))
                        ins = vector.tensor_scalar_mul(
                            dnf[:, bank * 1024 + 512:(bank + 1) * 1024],
                            dnf[:, bank * 1024:bank * 1024 + 512],
                            sd_sb[:, dimc:dimc + 1])
                        dn_free.append(b.inc(ins, "dv"))
                    dfv = b.val("dv")

                    for j in range(4):
                        tb = th * 4 + j
                        tpar = tstate["n"] % 2
                        tstate["n"] += 1

                        @blk.tensor
                        def _(tensor, j=j, tpar=tpar, dfv=dfv, bank=bank):
                            if len(tstate["hist"]) >= 2:
                                b.wait(tensor, "dv", tstate["hist"][-2])
                            if j == 0:
                                b.wait(tensor, "dv", dfv)
                            ins = tensor.matmul(
                                tp2[:, tpar * 512:tpar * 512 + 128],
                                dnf[:, bank * 1024 + 512 + j * 128:
                                    bank * 1024 + 512 + (j + 1) * 128],
                                ident_f_sb[:], start=True, stop=True)
                            b.inc(ins, "pe")
                        tpv = b.val("pe")

                        @blk.vector
                        def _(vector, tb=tb, dimc=dimc, tpar=tpar, tpv=tpv):
                            b.wait(vector, "pe", tpv)
                            ins = vector.tensor_add(
                                x_res[:, tb * D + dimc * 128:tb * D + (dimc + 1) * 128],
                                x_res[:, tb * D + dimc * 128:tb * D + (dimc + 1) * 128],
                                tp2[:, tpar * 512:tpar * 512 + 128])
                            tstate["hist"].append(b.inc(ins, "dv"))
            ps.close()

        def emit_attn_b():
            psc = contextlib.ExitStack()
            sb = lambda n, s, d=BF16: sball(n, s, d, psc)
            tpB = psall("tpB", [128, 1024], psc)
            pjB = psall("pjB", [128, 2048], psc)
            smB = psall("smB", [128, 1024], psc)

            qp_fm = sball("qp_fm", [128, NDC * 1024], BF16, psc)
            at_fm = sball("at_fm", [128, NDC * 512], BF16, psc)
            ps1b = contextlib.ExitStack()
            kp_fm = sball("kp_fm", [128, NDC * 1024], BF16, ps1b)
            v_fm = sball("bv_fm", [128, NDC * 1024], BF16, ps1b)
            ps1 = contextlib.ExitStack()
            lnt = [sball("blnt0", [128, D], BF16, ps1)]
            lnf = sball("blnf", [128, NDC * 512], BF16, ps1)
            t_el = [sball("t_el0", [128, 512], BF16, ps1)]
            wringB = [sball(f"wringB{i}", [128, NDC * 128], BF16, ps1) for i in range(2)]

            lnt_free = [0]
            wring_free = []
            pj_free2 = []
            el_ac = []
            b_slab_done = [0]
            for sl in range(2):
                for tt in range(4):
                    tb = sl * 4 + tt

                    @blk.vector
                    def _(vector):
                        b.wait(vector, "pe", lnt_free[0])
                    lv = ln_tile(x_res[:, tb * D:(tb + 1) * D], lnt[0], tb % 2,
                                 lnt[0], scalar_gate=lnt_free[0])
                    transpose_blocks(
                        tpB,
                        [(lnt[0], dc * 128) for dc in range(NDC)],
                        [(lnf, dc * 512 + tt * 128) for dc in range(NDC)],
                        pe_waits=[("dv", lv), ("pe", b_slab_done[0])],
                    )
                    lnt_free[0] = b.val("pe")
                lnf_v = b.val("dv")

                for iw, (key, bias_t) in enumerate([("qwb", qb_sb), ("kwb", kb_sb),
                                                    ("vwb", vb_sb)]):
                    dst = [qp_fm, kp_fm, v_fm][iw]
                    for oc in range(NDC):
                        li = len(wring_free)
                        slot = li % 2

                        @blk.sync
                        def _(sync, key=key, oc=oc, li=li, slot=slot):
                            for nm, v in prep_done[key]:
                                b.wait(sync, nm, v)
                            if li >= 2:
                                b.wait(sync, "pe", wring_free[li - 2])
                            ins = sync.dma_start(
                                out=wringB[slot][:],
                                in_=wT[key].ap()[oc * 128:(oc + 1) * 128, 0:D])
                            b.inc(ins, "bw%d" % slot, 16)
                        wv = b.val("bw%d" % slot)
                        gi = len(pj_free2)
                        bank = gi % 4

                        @blk.tensor
                        def _(tensor, slot=slot, gi=gi, bank=bank, wv=wv,
                              lnf_v=lnf_v):
                            b.wait(tensor, "bw%d" % slot, wv)
                            b.wait(tensor, "dv", lnf_v)
                            if gi >= 4:
                                b.wait(tensor, "dv", pj_free2[gi - 4])
                            for dc in range(NDC):
                                ins = tensor.matmul(
                                    pjB[:, bank * 512:(bank + 1) * 512],
                                    wringB[slot][:, dc * 128:(dc + 1) * 128],
                                    lnf[:, dc * 512:(dc + 1) * 512],
                                    start=(dc == 0), stop=(dc == NDC - 1))
                            b.inc(ins, "pe")
                        wring_free.append(b.val("pe"))
                        mv = b.val("pe")

                        if iw < 2:
                            dslice = dst[:, oc * 1024 + sl * 512:oc * 1024 + (sl + 1) * 512]

                            @blk.vector
                            def _(vector, oc=oc, bank=bank, mv=mv, bias_t=bias_t):
                                b.wait(vector, "pe", mv)
                                if len(el_ac) >= 1:
                                    b.wait(vector, "ac", el_ac[-1])
                                if pj_free2:
                                    b.wait(vector, "dv", pj_free2[-1])
                                ins = vector.tensor_scalar(
                                    t_el[0][:], pjB[:, bank * 512:(bank + 1) * 512],
                                    bias_t[:, oc:oc + 1], 0.0, ALU.add, ALU.min)
                                b.inc(ins, "dv")
                            elv = b.val("dv")

                            @blk.scalar
                            def _(scalar, dslice=dslice, elv=elv):
                                b.wait(scalar, "dv", elv)
                                ins = scalar.activation(dslice, t_el[0][:], AF.Exp)
                                el_ac.append(b.inc(ins, "ac"))
                            eav = b.val("ac")

                            @blk.vector
                            def _(vector, dslice=dslice, oc=oc, bank=bank,
                                  eav=eav, bias_t=bias_t):
                                b.wait(vector, "ac", eav)
                                ins = vector.tensor_scalar(
                                    t_el[0][:], pjB[:, bank * 512:(bank + 1) * 512],
                                    bias_t[:, oc:oc + 1], 0.0, ALU.add, ALU.max)
                                b.inc(ins, "dv")
                                b.wait(vector, "dv", b.val("dv"))
                                ins = vector.tensor_add(dslice, dslice, t_el[0][:])
                                pj_free2.append(b.inc(ins, "dv"))
                        else:
                            @blk.vector
                            def _(vector, dst=dst, oc=oc, sl=sl, bank=bank, mv=mv,
                                  bias_t=bias_t):
                                b.wait(vector, "pe", mv)
                                ins = vector.tensor_scalar(
                                    dst[:, oc * 1024 + sl * 512:oc * 1024 + (sl + 1) * 512],
                                    pjB[:, bank * 512:(bank + 1) * 512],
                                    bias_t[:, oc:oc + 1], None, ALU.add)
                                pj_free2.append(b.inc(ins, "dv"))

                b_slab_done[0] = b.val("pe")
            proj_v = b.val("dv")

            ps1.close()
            km_tm = sball("km_tm", [128, 8 * 128], BF16, ps1b)
            vm_tm = sball("vm_tm", [128, 8 * 128], BF16, ps1b)
            kvt = sball("kvt", [128, 128], F32, ps1b)
            ksrow = sball("ksrow", [1, 128], F32, ps1b)
            kv_wr = []
            for h in range(NH):
                tv = transpose_blocks(
                    tpB,
                    [(kp_fm, h * 1024 + j * 128) for j in range(8)]
                    + [(v_fm, h * 1024 + j * 128) for j in range(8)],
                    [(km_tm, j * 128) for j in range(8)]
                    + [(vm_tm, j * 128) for j in range(8)],
                    pe_waits=[("dv", proj_v)] + ([("kvw", kv_wr[-1])] if h else []),
                )

                @blk.tensor
                def _(tensor, h=h, tv=tv):
                    b.wait(tensor, "dv", tv)
                    if h >= 1:
                        b.wait(tensor, "dv", kv_ev[-1])
                    for j in range(8):
                        tensor.matmul(smB[:, 0:128], km_tm[:, j * 128:(j + 1) * 128],
                                      vm_tm[:, j * 128:(j + 1) * 128],
                                      start=(j == 0), stop=(j == 7))
                    for j in range(8):
                        ins = tensor.matmul(smB[0:1, 512:640], ones_col_sb[:],
                                            km_tm[:, j * 128:(j + 1) * 128],
                                            start=(j == 0), stop=(j == 7))
                    b.inc(ins, "pe")
                kvv = b.val("pe")

                @blk.vector
                def _(vector, kvv=kvv):
                    b.wait(vector, "pe", kvv)
                    vector.tensor_copy(kvt[:], smB[:, 0:128])
                    ins = vector.tensor_copy(ksrow[:], smB[0:1, 512:640])
                    kv_ev.append(b.inc(ins, "dv"))
                kev = kv_ev[-1]

                @blk.gpsimd
                def _(gpsimd, h=h, kev=kev):
                    b.wait(gpsimd, "dv", kev)
                    ins = gpsimd.dma_start(out=ar_in.ap()[h * 128:(h + 1) * 128, :],
                                           in_=kvt[:])
                    b.inc(ins, "kvw", 16)
                    ins = gpsimd.dma_start(
                        out=ar_in.ap()[NH * DH + h:NH * DH + h + 1, :], in_=ksrow[:])
                    kv_wr.append(b.inc(ins, "kvw", 16))
            ps1b.close()

            @blk.gpsimd
            def _(gpsimd):
                b.wait(gpsimd, "kvw", kv_wr[-1])
                ins = gpsimd.collective_compute(
                    "AllReduce", ALU.add, RG, [ar_in.ap().opt()], [ar_out.ap().opt()])
                b.inc(ins, "coll")
            coll_ar = b.val("coll")

            ps2 = contextlib.ExitStack()
            kvb = sball("kvb", [128, NH * 128], BF16, ps2)
            zrows = sball("zrows", [1, 4 * 512], F32, ps2)
            ksf = sball("ksf", [128, NH], F32, ps2)
            ksb = sball("ksb", [128, NH], BF16, ps2)

            @blk.gpsimd
            def _(gpsimd):
                b.wait(gpsimd, "coll", coll_ar)
                ins = gpsimd.dma_start(
                    out=kvb[:].rearrange("p (a c) -> p a c", c=128),
                    in_=ar_out.ap()[0:NH * DH, :].rearrange("(a p) c -> p a c", p=128))
                b.inc(ins, "kvl", 16)
                ins = gpsimd.dma_start(
                    out=ksf[:],
                    in_=bass.AP(ar_out.ap().tensor, (NH * DH) * DH, [[1, 128], [128, NH]]),
                    allow_slow_non_contiguous=True)
                b.inc(ins, "kvl", 16)
            kv_ld = b.val("kvl")

            @blk.vector
            def _(vector, kv_ld=kv_ld):
                b.wait(vector, "kvl", kv_ld)
                ins = vector.tensor_copy(ksb[:], ksf[:])
                b.inc(ins, "dv")
            ksb_v = b.val("dv")

            ps3 = contextlib.ExitStack()
            owT_b = sball("owT_b", [128, NDC * D], BF16, ps3)
            ob_row = sball("ob_row", [1, D], F32, ps3)
            ob_bc = sball("ob_bc", [128, D], F32, ps3)

            @blk.sync
            def _(sync):
                for nm, v in prep_done["owb"]:
                    b.wait(sync, nm, v)
                b.wait(sync, "kvw", kv_wr[-1])
                for ec in range(NDC):
                    ins = sync.dma_start(out=owT_b[:, ec * D:(ec + 1) * D],
                                         in_=wT["owb"].ap()[ec * 128:(ec + 1) * 128, 0:D])
                    b.inc(ins, "oww", 16)
                ins = sync.dma_start(out=ob_row[:],
                                     in_=bass.AP(EXT["ob_b"].tensor, 0, [[D, 1], [1, D]]))
                b.inc(ins, "oww", 16)
            ow_v = b.val("oww")

            @blk.tensor
            def _(tensor, ow_v=ow_v):
                b.wait(tensor, "oww", ow_v)
                for j in range(4):
                    ins = tensor.matmul(pjB[:, j * 512:(j + 1) * 512], ones_row_sb[:],
                                        ob_row[0:1, j * 512:(j + 1) * 512],
                                        start=True, stop=True)
                b.inc(ins, "pe")
            obv = b.val("pe")

            @blk.vector
            def _(vector, obv=obv):
                b.wait(vector, "pe", obv)
                ins = vector.tensor_copy(ob_bc[:], pjB[:, 0:2048])
                b.inc(ins, "dv")
            ob_v = b.val("dv")

            op_free2 = []
            at_ev = []
            zr_ev = []
            zb_ev = []
            for th in range(2):
                for hg in range(4):
                    for hh in range(4):
                        h = hg * 4 + hh

                        @blk.tensor
                        def _(tensor, h=h, hh=hh, th=th):
                            if th == 0 and h == 0:
                                b.wait(tensor, "dv", ksb_v)
                            if hh == 0 and zb_ev:
                                b.wait(tensor, "dv", zb_ev[-1])
                            if len(zr_ev) >= 2:
                                b.wait(tensor, "dv", zr_ev[-2])
                            ins = tensor.matmul(
                                smB[0:1, (h % 2) * 512:(h % 2) * 512 + 512],
                                ksb[:, h:h + 1],
                                qp_fm[:, h * 1024 + th * 512:h * 1024 + (th + 1) * 512],
                                start=True, stop=True)
                            b.inc(ins, "pe")
                        zv = b.val("pe")

                        @blk.vector
                        def _(vector, h=h, hh=hh, zv=zv):
                            b.wait(vector, "pe", zv)
                            ins = vector.tensor_scalar_add(
                                zrows[0:1, hh * 512:(hh + 1) * 512],
                                smB[0:1, (h % 2) * 512:(h % 2) * 512 + 512], Z_EPS)
                            zr_ev.append(b.inc(ins, "dv"))

                    @blk.vector
                    def _(vector):
                        b.wait(vector, "dv", zr_ev[-1])
                        ins = vector.reciprocal(zrows[:], zrows[:])
                        b.inc(ins, "dv")
                    zrc_v = b.val("dv")

                    for hh in range(4):
                        h = hg * 4 + hh

                        @blk.tensor
                        def _(tensor, h=h, hh=hh, zrc_v=zrc_v):
                            b.wait(tensor, "dv", zrc_v)
                            if len(zb_ev) >= 2:
                                b.wait(tensor, "dv", zb_ev[-2])
                            ins = tensor.matmul(
                                smB[:, (h % 2) * 512:(h % 2) * 512 + 512],
                                ones_row_sb[:], zrows[0:1, hh * 512:(hh + 1) * 512],
                                start=True, stop=True)
                            b.inc(ins, "pe")
                        zbv = b.val("pe")

                        @blk.vector
                        def _(vector, h=h, th=th, zbv=zbv):
                            b.wait(vector, "pe", zbv)
                            qsl = qp_fm[:, h * 1024 + th * 512:h * 1024 + (th + 1) * 512]
                            ins = vector.tensor_tensor(
                                qsl, qsl, smB[:, (h % 2) * 512:(h % 2) * 512 + 512],
                                ALU.mult)
                            zb_ev.append(b.inc(ins, "dv"))

                for h in range(NH):
                    qsc = zb_ev[-(NH - h)]

                    @blk.tensor
                    def _(tensor, h=h, th=th, qsc=qsc):
                        b.wait(tensor, "dv", qsc)
                        if h == 0:
                            b.wait(tensor, "dv", zb_ev[-1])
                        if len(at_ev) >= 2:
                            b.wait(tensor, "dv", at_ev[-2])
                        ins = tensor.matmul(
                            smB[0:128, (h % 2) * 512:(h % 2) * 512 + 512],
                            kvb[:, h * 128:(h + 1) * 128],
                            qp_fm[:, h * 1024 + th * 512:h * 1024 + (th + 1) * 512],
                            start=True, stop=True)
                        b.inc(ins, "pe")
                    nv = b.val("pe")

                    @blk.vector
                    def _(vector, h=h, nv=nv):
                        b.wait(vector, "pe", nv)
                        ins = vector.tensor_copy(at_fm[:, h * 512:(h + 1) * 512],
                                                 smB[0:128, (h % 2) * 512:(h % 2) * 512 + 512])
                        at_ev.append(b.inc(ins, "dv"))
                at_v = at_ev[-1]

                for j in range(4):
                    tb = th * 4 + j

                    @blk.tensor
                    def _(tensor, tb=tb, j=j, th=th, at_v=at_v, ob_v=ob_v):
                        if j == 0:
                            b.wait(tensor, "dv", at_v)
                            b.wait(tensor, "dv", ob_v)
                        if len(op_free2) >= 1:
                            b.wait(tensor, "dv", op_free2[-1])
                        for ec in range(NDC):
                            for ods in range(4):
                                ins = tensor.matmul(
                                    pjB[:, ods * 512:(ods + 1) * 512],
                                    at_fm[:, ec * 512 + j * 128:ec * 512 + (j + 1) * 128],
                                    owT_b[:, ec * D + ods * 512:ec * D + (ods + 1) * 512],
                                    start=(ec == 0), stop=(ec == NDC - 1))
                        b.inc(ins, "pe")
                    ov = b.val("pe")

                    @blk.vector
                    def _(vector, tb=tb, ov=ov):
                        b.wait(vector, "pe", ov)
                        for ods in range(4):
                            xsl = x_res[:, tb * D + ods * 512:tb * D + (ods + 1) * 512]
                            if ods > 0 or tb > 0:
                                b.wait(vector, "dv", b.val("dv"))
                            ins = vector.tensor_add(xsl, xsl,
                                                    pjB[:, ods * 512:(ods + 1) * 512])
                            b.inc(ins, "dv")
                            b.wait(vector, "dv", b.val("dv"))
                            ins = vector.tensor_add(xsl, xsl,
                                                    ob_bc[:, ods * 512:(ods + 1) * 512])
                            b.inc(ins, "dv")
                        op_free2.append(b.val("dv"))
            ps3.close()
            ps2.close()
            psc.close()

        kv_ev = []

        if STAGE >= 2:
            barrier(dma_waits=preps_all)
            emit_ffn("F1")
            emit_dbg("x2")
        if STAGE >= 3:
            barrier()
            emit_attn_b()
            emit_dbg("x3")
        if STAGE >= 4:
            barrier()
            emit_ffn("F2")

        fin = b.val("dv")

        @blk.gpsimd
        def _(gpsimd):
            b.wait(gpsimd, "dv", fin)
            for tb in range(NTB):
                ins = gpsimd.dma_start(out=Y[tb * 128:(tb + 1) * 128, :],
                                       in_=x_res[:, tb * D:(tb + 1) * D])
                b.inc(ins, "st", 16)
            b.wait(gpsimd, "st", b.val("st"))

    stack.close()
    return nc, dbg_outs


def _blk2(W):
    O, I = W.shape
    return np.ascontiguousarray(
        W.reshape(O // 128, 128, I // 128, 128).transpose(0, 3, 2, 1)
         .reshape(O, I))


def _cols(v):
    return np.ascontiguousarray(v.reshape(-1, 128).T)


def _in_maps(inputs):
    x = np.ascontiguousarray(inputs["L0_attn_a_x"])
    shared = {
        "wo_a": np.ascontiguousarray(inputs["L0_attn_a_Wo"].T),
        "qw_b": _blk2(inputs["L0_attn_b_q_w"]),
        "kw_b": _blk2(inputs["L0_attn_b_k_w"]),
        "vw_b": _blk2(inputs["L0_attn_b_v_w"]),
        "ow_b": np.ascontiguousarray(inputs["L0_attn_b_o_w"].T),
        "qb_b": _cols(inputs["L0_attn_b_q_b"]),
        "kb_b": _cols(inputs["L0_attn_b_k_b"]),
        "vb_b": _cols(inputs["L0_attn_b_v_b"]),
        "ob_b": np.ascontiguousarray(inputs["L0_attn_b_o_b"]),
        "wg_i": _blk2(inputs["L0_ffn_wq_gate"]),
        "wu_i": _blk2(inputs["L0_ffn_wq_up"]),
        "wd_i": _blk2(inputs["L0_ffn_wq_down"]),
        "sg": _cols(inputs["L0_ffn_s_gate"]),
        "su": _cols(inputs["L0_ffn_s_up"]),
        "sd": _cols(inputs["L0_ffn_s_down"]),
    }
    qkv = [{k: _blk2(inputs["L0_attn_a_W" + k][h0 * DH:(h0 + 4) * DH])
            for k in ("q", "k", "v")} for h0 in (0, 4, 8, 12)]
    maps = []
    for c in range(8):
        bb, gg = c // 4, c % 4
        m = {
            "x0": np.ascontiguousarray(x[bb]),
            "x0m": np.ascontiguousarray(x[bb, gg * TOK:(gg + 1) * TOK]),
            "wq_a": qkv[gg]["q"], "wk_a": qkv[gg]["k"], "wv_a": qkv[gg]["v"],
        }
        m.update(shared)
        maps.append(m)
    return maps


def kernel(**inputs):
    nc, _ = build_core_program(0)
    res = run_bass_kernel_spmd(nc, _in_maps(inputs), core_ids=list(range(8)))
    out = np.empty((B_, T, D), dtype=np.float32)
    for c in range(8):
        bb, gg = c // 4, c % 4
        out[bb, gg * TOK:(gg + 1) * TOK, :] = res.results[c]["y"]
    return out


# version 09
# speedup vs baseline: 1.2845x; 1.0650x over previous
import contextlib
import numpy as np
import ml_dtypes
import concourse.bass as bass
import concourse.mybir as mybir
from concourse.bass_utils import run_bass_kernel_spmd

F32 = mybir.dt.float32
BF16 = mybir.dt.bfloat16
I32 = mybir.dt.int32
AF = mybir.ActivationFunctionType
ALU = mybir.AluOpType
AX = mybir.AxisListType

B_, T, D = 2, 4096, 2048
HID = 5632
NH, DH = 16, 128
TOK = 1024
NTB = TOK // 128
NDC = D // 128
NHC = HID // 128
NP = 4
LN_EPS = 1e-5
Z_EPS = 1e-6

STAGE = 4
DBG = ()


class Bld:
    def __init__(self, nc, stack):
        self.nc = nc
        self.stack = stack
        self.cnt = {}
        self.sems = {}

    def sem(self, name):
        if name not in self.sems:
            self.sems[name] = self.stack.enter_context(self.nc.semaphore(name))
            self.cnt[name] = 0
        return self.sems[name]

    def inc(self, ins, name, n=1):
        ins.then_inc(self.sem(name), n)
        self.cnt[name] += n
        return self.cnt[name]

    def val(self, name):
        self.sem(name)
        return self.cnt[name]

    def wait(self, eng, name, v):
        assert 0 <= v <= self.cnt.get(name, 0), ("wait-ahead", name, v, self.cnt.get(name))
        if v > 0:
            eng.wait_ge(self.sem(name), v)


def build_core_program(core):
    nc = bass.Bass("TRN2", target_bir_lowering=False)
    g = core % 4
    to0 = g * TOK
    RG = [[0, 1, 2, 3], [4, 5, 6, 7]]

    X0 = nc.dram_tensor("x0", [T, D], F32, kind="ExternalInput").ap()
    X0M = nc.dram_tensor("x0m", [TOK, D], F32, kind="ExternalInput").ap()
    EXT = {}
    for nm, sh, dt in [("wq_a", [512, D], F32), ("wk_a", [512, D], F32),
                       ("wv_a", [512, D], F32), ("wo_a", [D, D], F32),
                       ("qw_b", [D, D], F32), ("kw_b", [D, D], F32),
                       ("vw_b", [D, D], F32), ("ow_b", [D, D], F32),
                       ("qb_b", [128, 16], F32), ("kb_b", [128, 16], F32),
                       ("vb_b", [128, 16], F32), ("ob_b", [D], F32),
                       ("wg_i", [HID, D], I32), ("wu_i", [HID, D], I32),
                       ("wd_i", [D, HID], I32), ("sg", [128, NHC], F32),
                       ("su", [128, NHC], F32), ("sd", [128, 16], F32)]:
        EXT[nm] = nc.dram_tensor(nm, sh, dt, kind="ExternalInput").ap()
    Y = nc.dram_tensor("y", [TOK, D], F32, kind="ExternalOutput").ap()

    def dram(name, shape, dt=BF16):
        return nc.dram_tensor(name, shape, dt, kind="Internal")

    CAST_SRC = {"wqa": "wq_a", "wka": "wk_a", "wva": "wv_a", "woa": "wo_a",
                "g": "wg_i", "u": "wu_i", "d": "wd_i",
                "qwb": "qw_b", "kwb": "kw_b", "vwb": "vw_b", "owb": "ow_b"}
    WSHAPE = {"wqa": [512, D], "wka": [512, D], "wva": [512, D], "woa": [D, D],
              "g": [HID, D], "u": [HID, D], "d": [D, HID],
              "qwb": [D, D], "kwb": [D, D], "vwb": [D, D], "owb": [D, D]}
    wT = {k: dram(k + "_bf", s) for k, s in WSHAPE.items()}
    ag_in = dram("ag_in", [NP * DH, T])
    ag_out = dram("ag_out", [NH * DH, T])
    ar_in = dram("ar_in", [NH * DH + NH, DH], F32)
    ar_out = dram("ar_out", [NH * DH + NH, DH], F32)

    ident_bf = nc.inline_tensor(np.eye(128, dtype=ml_dtypes.bfloat16), "c_ident_bf")
    ident_f = nc.inline_tensor(np.eye(128, dtype=np.float32), "c_ident_f")
    ones_row_f = nc.inline_tensor(np.ones((1, 128), dtype=np.float32), "c_ones_row")
    row128_f = nc.inline_tensor(np.full((1, 128), 128.0, dtype=np.float32), "c_row128")
    one11_f = nc.inline_tensor(np.ones((1, 1), dtype=np.float32), "c_one11")
    ones_col_bf = nc.inline_tensor(np.ones((128, 1), dtype=ml_dtypes.bfloat16), "c_ones_col")
    eps_col_f = nc.inline_tensor(np.full((128, 1), LN_EPS, dtype=np.float32), "c_eps_col")

    stack = contextlib.ExitStack()
    b = Bld(nc, stack)
    bko_ev = []

    def sball(name, shape, dtype=BF16, st=None):
        return (st or stack).enter_context(nc.sbuf_tensor(name, shape, dtype))

    def psall(name, shape, st=None, dtype=F32):
        return (st or stack).enter_context(nc.psum_tensor(name, shape, dtype))

    ident_bf_sb = sball("ident_bf_sb", [128, 128])
    ident_f_sb = sball("ident_f_sb", [128, 128], F32)
    ones_row_sb = sball("ones_row_sb", [1, 128], F32)
    row128_sb = sball("row128_sb", [1, 128], F32)
    one11_sb = sball("one11_sb", [1, 1], F32)
    ones_col_sb = sball("ones_col_sb", [128, 1])
    eps_sb = sball("eps_sb", [128, 1], F32)
    qb_sb = sball("qb_sb", [128, 16], F32)
    kb_sb = sball("kb_sb", [128, 16], F32)
    vb_sb = sball("vb_sb", [128, 16], F32)
    sg_sb = sball("sg_sb", [128, NHC], F32)
    su_sb = sball("su_sb", [128, NHC], F32)
    sd_sb = sball("sd_sb", [128, 16], F32)
    st_a = sball("st_a", [128, 8], F32)


    dbg_outs = {}

    with nc.Block() as blk:
        @blk.sync
        def _(sync):
            for dst, src in [(ident_bf_sb, ident_bf), (ident_f_sb, ident_f),
                             (ones_row_sb, ones_row_f), (row128_sb, row128_f),
                             (one11_sb, one11_f), (ones_col_sb, ones_col_bf),
                             (eps_sb, eps_col_f)]:
                ins = sync.dma_start(out=dst[:], in_=src.ap())
                b.inc(ins, "ko", 16)
        konst_v = b.val("ko")

        @blk.sync
        def _(sync):
            for dst, srcnm, n in [(qb_sb, "qb_b", 16), (kb_sb, "kb_b", 16),
                                  (vb_sb, "vb_b", 16), (sg_sb, "sg", NHC),
                                  (su_sb, "su", NHC), (sd_sb, "sd", 16)]:
                ins = sync.dma_start(out=dst[:, 0:n], in_=EXT[srcnm])
                b.inc(ins, "bko", 16)
        bias_v = b.val("bko")
        for engname in ("vector", "scalar"):
            @getattr(blk, engname)
            def _(eng):
                b.wait(eng, "bko", bias_v)

        cast_done = {}
        prep_done = {}

        @blk.gpsimd
        def _(gpsimd):
            for key in ["wqa", "wka", "wva"]:
                ins = gpsimd.dma_start(out=wT[key].ap(), in_=EXT[CAST_SRC[key]])
                cast_done[key] = b.inc(ins, "ca_" + key, 16)
                prep_done[key] = [("ca_" + key, 16)]

        for engname in ("tensor", "vector", "scalar"):
            @getattr(blk, engname)
            def _(eng):
                b.wait(eng, "ko", konst_v)

        def ln_tile(x_ap, out_tile, par, sqt, colsum_to=None, scalar_gate=0,
                    x_gate=None):
            c = par * 4

            @blk.vector
            def _(vector):
                ins = vector.reduce_sum(st_a[:, c:c + 1], x_ap, axis=AX.X)
                b.inc(ins, "dv")

            @blk.scalar
            def _(scalar):
                b.wait(scalar, "pe", scalar_gate)
                if x_gate is not None:
                    b.wait(scalar, x_gate[0], x_gate[1])
                ins = scalar.activation(sqt[:], x_ap, AF.Square,
                                        accum_out=st_a[:, c + 1:c + 2])
                b.inc(ins, "ac")
            v0, a0 = b.val("dv"), b.val("ac")

            @blk.vector
            def _(vector):
                b.wait(vector, "dv", v0)
                b.wait(vector, "ac", a0)
                ins = vector.tensor_scalar_mul(st_a[:, c:c + 1], st_a[:, c:c + 1], 1.0 / D)
                b.inc(ins, "dv")
                b.wait(vector, "dv", b.val("dv"))
                ins = vector.tensor_mul(st_a[:, c + 2:c + 3], st_a[:, c:c + 1],
                                        st_a[:, c:c + 1])
                b.inc(ins, "dv")
                b.wait(vector, "dv", b.val("dv"))
                ins = vector.tensor_scalar(st_a[:, c + 1:c + 2], st_a[:, c + 1:c + 2],
                                           1.0 / D, None, ALU.mult)
                b.inc(ins, "dv")
                b.wait(vector, "dv", b.val("dv"))
                ins = vector.tensor_sub(st_a[:, c + 1:c + 2], st_a[:, c + 1:c + 2],
                                        st_a[:, c + 2:c + 3])
                b.inc(ins, "dv")
            v1 = b.val("dv")

            @blk.scalar
            def _(scalar):
                b.wait(scalar, "dv", v1)
                ins = scalar.activation(st_a[:, c + 2:c + 3], st_a[:, c + 1:c + 2],
                                        AF.Ln, bias=eps_sb[:, 0:1])
                b.inc(ins, "ac")
                b.wait(scalar, "ac", b.val("ac"))
                ins = scalar.activation(st_a[:, c + 3:c + 4], st_a[:, c + 2:c + 3],
                                        AF.Exp, scale=-0.5)
                b.inc(ins, "ac")
            a1 = b.val("ac")

            @blk.vector
            def _(vector):
                b.wait(vector, "ac", a1)
                ins = vector.tensor_scalar(out_tile[:], x_ap, st_a[:, c:c + 1],
                                           st_a[:, c + 3:c + 4], ALU.subtract, ALU.mult)
                b.inc(ins, "dv")
                if colsum_to is not None:
                    t, cc = colsum_to
                    b.wait(vector, "dv", b.val("dv"))
                    ins = vector.reduce_sum(t[:, cc:cc + 1], out_tile[:], axis=AX.X)
                    b.inc(ins, "dv")
            return b.val("dv")

        tstate = {"n": 0, "hist": []}

        def transpose_blocks(tp_ps, srcs, dsts, pe_waits=(), f32=False):
            ident = ident_f_sb if f32 else ident_bf_sb
            for i, ((s_t, s_c), (d_t, d_c)) in enumerate(zip(srcs, dsts)):
                par = tstate["n"] % 2
                tstate["n"] += 1

                @blk.tensor
                def _(tensor, s_t=s_t, s_c=s_c, par=par, i=i):
                    for nm, v in (pe_waits if i == 0 else ()):
                        b.wait(tensor, nm, v)
                    if len(tstate["hist"]) >= 2:
                        b.wait(tensor, "dv", tstate["hist"][-2])
                    ins = tensor.matmul(tp_ps[:, par * 512:par * 512 + 128],
                                        s_t[:, s_c:s_c + 128], ident[:],
                                        start=True, stop=True)
                    b.inc(ins, "pe")
                pv = b.val("pe")

                @blk.vector
                def _(vector, d_t=d_t, d_c=d_c, par=par, pv=pv):
                    b.wait(vector, "pe", pv)
                    ins = vector.tensor_copy(d_t[:, d_c:d_c + 128],
                                             tp_ps[:, par * 512:par * 512 + 128])
                    tstate["hist"].append(b.inc(ins, "dv"))
            return b.val("dv")

        def barrier(dma_waits=()):
            for engname in ("tensor", "vector", "scalar"):
                @getattr(blk, engname)
                def _(eng):
                    ins = eng.nop()
                    b.inc(ins, "bar")
            t = b.val("bar")
            for engname in ("tensor", "vector", "scalar", "sync", "gpsimd"):
                @getattr(blk, engname)
                def _(eng):
                    b.wait(eng, "bar", t)
                    for nm, v in dma_waits:
                        b.wait(eng, nm, v)

        def emit_dbg(name):
            if name not in DBG:
                return
            out = nc.dram_tensor("dbg_" + name, [TOK, D], F32, kind="ExternalOutput").ap()
            dbg_outs[name] = out
            wv = b.val("dv")

            @blk.gpsimd
            def _(gpsimd):
                b.wait(gpsimd, "dv", wv)
                for tb in range(NTB):
                    ins = gpsimd.dma_start(out=out[tb * 128:(tb + 1) * 128, :],
                                           in_=x_res[:, tb * D:(tb + 1) * D])
                    b.inc(ins, "dbg", 16)

        pa = contextlib.ExitStack()
        q_fm = sball("q_fm", [128, NP * T], BF16, pa)
        k_fm = sball("k_fm", [128, NP * T], BF16, pa)
        v_tm = sball("v_tm", [128, NP * 32 * 128], BF16, pa)
        tp_ps = psall("tp_psA", [128, 1024], pa)

        pa1 = contextlib.ExitStack()
        lnx_fm2 = [sball(f"lnx_fm{i}", [128, NDC * 512], BF16, pa1) for i in range(2)]
        v_fms = sball("v_fms", [128, NP * 512], BF16, pa1)
        x0t = [sball(f"x0t{i}", [128, D], F32, pa1) for i in range(2)]
        sqt_a = sball("sqt_a", [128, D], BF16, pa1)
        lnq_tm = [sball(f"lnq{i}", [128, D], BF16, pa1) for i in range(2)]
        aring = [sball(f"aring{i}", [128, NDC * 128], BF16, pa1) for i in range(2)]
        pj_ps = psall("pj_psA", [128, 2048], pa1)

        x0_free = [0, 0]
        lnq_free = [0, 0]
        aring_free = []
        pj_free = []
        a2_done = [0, 0]

        for sl in range(8):
            lnx_fm = lnx_fm2[sl % 2]
            for tt in range(4):
                tile = sl * 4 + tt
                par = tile % 2

                def _ld_x0(eng, tile=tile, par=par):
                    b.wait(eng, "dv", x0_free[par])
                    ins = eng.dma_start(out=x0t[par][:],
                                        in_=X0[tile * 128:(tile + 1) * 128, :])
                    b.inc(ins, "x0w%d" % par, 16)
                if par == 0:
                    blk.scalar(_ld_x0)
                else:
                    blk.sync(_ld_x0)
                xv = b.val("x0w%d" % par)

                @blk.vector
                def _(vector, par=par, xv=xv):
                    b.wait(vector, "x0w%d" % par, xv)
                    b.wait(vector, "pe", lnq_free[par])
                lv = ln_tile(x0t[par][:], lnq_tm[par], par, sqt_a,
                             x_gate=("x0w%d" % par, xv))
                x0_free[par] = lv
                transpose_blocks(
                    tp_ps,
                    [(lnq_tm[par], dc * 128) for dc in range(NDC)],
                    [(lnx_fm, dc * 512 + tt * 128) for dc in range(NDC)],
                    pe_waits=[("dv", lv), ("pe", a2_done[sl % 2])],
                )

                lnq_free[par] = b.val("pe")
            slab_tp = b.val("dv")

            for iw, key in enumerate(["wqa", "wka", "wva"]):
                for oc in range(NP):
                    li = len(aring_free)
                    slot = li % 2

                    def _ld_ar(eng, key=key, oc=oc, li=li, slot=slot):
                        for nm, v in prep_done[key]:
                            b.wait(eng, nm, v)
                        if li >= 2:
                            b.wait(eng, "pe", aring_free[li - 2])
                        ins = eng.dma_start(
                            out=aring[slot][:],
                            in_=wT[key].ap()[oc * 128:(oc + 1) * 128, 0:D])
                        b.inc(ins, "aw%d" % slot, 16)
                    if slot == 0:
                        blk.scalar(_ld_ar)
                    else:
                        blk.sync(_ld_ar)
                    wv = b.val("aw%d" % slot)
                    gi = sl * 12 + iw * NP + oc
                    bank = gi % 4

                    @blk.tensor
                    def _(tensor, slot=slot, gi=gi, bank=bank, wv=wv, slab_tp=slab_tp):
                        b.wait(tensor, "aw%d" % slot, wv)
                        b.wait(tensor, "dv", slab_tp)
                        if gi >= 4:
                            b.wait(tensor, "dv", pj_free[gi - 4])
                        for dc in range(NDC):
                            ins = tensor.matmul(
                                pj_ps[:, bank * 512:(bank + 1) * 512],
                                aring[slot][:, dc * 128:(dc + 1) * 128],
                                lnx_fm[:, dc * 512:(dc + 1) * 512],
                                start=(dc == 0), stop=(dc == NDC - 1))
                        b.inc(ins, "pe")
                    aring_free.append(b.val("pe"))
                    mv = b.val("pe")

                    @blk.vector
                    def _(vector, iw=iw, oc=oc, sl=sl, bank=bank, mv=mv):
                        b.wait(vector, "pe", mv)
                        dst = [q_fm, k_fm, v_fms][iw]
                        col = (oc * T + sl * 512) if iw < 2 else (oc * 512)
                        if iw == 0:
                            ins = vector.tensor_scalar_mul(
                                dst[:, col:col + 512],
                                pj_ps[:, bank * 512:(bank + 1) * 512],
                                1.0 / np.sqrt(float(DH)))
                        else:
                            ins = vector.tensor_copy(dst[:, col:col + 512],
                                                     pj_ps[:, bank * 512:(bank + 1) * 512])
                        pj_free.append(b.inc(ins, "dv"))
            vv = b.val("dv")
            transpose_blocks(
                tp_ps,
                [(v_fms, p * 512 + kk * 128) for p in range(NP) for kk in range(4)],
                [(v_tm, (p * 32 + (sl * 4 + kk)) * 128) for p in range(NP)
                 for kk in range(4)],
                pe_waits=[("dv", vv)],
            )

            a2_done[sl % 2] = b.val("pe")

        pa1.close()

        @blk.gpsimd
        def _(gpsimd):
            b.wait(gpsimd, "x0w0", b.val("x0w0"))
            b.wait(gpsimd, "x0w1", b.val("x0w1"))
            for key in ["g", "woa", "u", "d", "qwb", "kwb", "vwb", "owb"]:
                ins = gpsimd.dma_start(out=wT[key].ap(), in_=EXT[CAST_SRC[key]])
                cast_done[key] = b.inc(ins, "ca_" + key, 16)
                prep_done[key] = [("ca_" + key, 16)]
        import os as _os
        if _os.environ.get("KMEM"):
            print("after A2 close:", nc.sbuf_bytes_remaining)
        flash_dv0 = b.val("dv")

        s_ps = [psall("s_ps0", [128, 512], pa), psall("s_ps1", [128, 512], pa)]
        st_ps = [psall("st_ps0", [128, 512], pa), psall("st_ps1", [128, 512], pa)]
        pv_ps = psall("pv_ps", [128, 512], pa)
        rb_ps = psall("rb_ps", [128, 512], pa)
        p_t = [sball(f"p_t{i}", [128, 512], BF16, pa) for i in range(4)]
        d_sb = [sball(f"d_sb{i}", [128, 512], F32, pa) for i in range(2)]
        maxbc_sb = [sball(f"maxbc{i}", [128, 512], F32, pa) for i in range(2)]
        mx8_sb = sball("mx8_sb", [128, 16], F32, pa)
        nmax_sb = sball("nmax_sb", [128, 8], F32, pa)
        nmrow_sb = sball("nmrow_sb", [1, 1024], F32, pa)
        ao_sb = [sball(f"ao_sb{i}", [128, 512], BF16, pa) for i in range(2)]
        rrow_sb = sball("rrow_sb", [1, 512], F32, pa)
        bcast_sb = sball("bcast_sb", [128, 512], F32, pa)

        s_use = []
        nm_dv = []
        nmT_pe = []
        nmrow_dv = []
        mbc_pe = []
        mbc_dv = []
        sub_dv = []
        exp_ac = []
        pvrs_pe = []
        sub_last = []
        rr_pe = []
        pv_ev = []
        ag_wr = []

        def emit_pass1(gqs):
            p, qs = gqs // 8, gqs % 8
            qc = p * T
            for qb in range(4):
                qi = qs * 4 + qb
                m = gqs * 4 + qb
                for kc8 in range(8):
                    n = m * 8 + kc8
                    grp = n % 2

                    @blk.tensor
                    def _(tensor, qi=qi, kc8=kc8, grp=grp, n=n, qc=qc):
                        if n >= 2:
                            nm, v = s_use[n - 2]
                            b.wait(tensor, nm, v)
                        else:
                            b.wait(tensor, "dv", flash_dv0)
                        ins = tensor.matmul(
                            s_ps[grp][:],
                            q_fm[:, qc + qi * 128:qc + (qi + 1) * 128],
                            k_fm[:, qc + kc8 * 512:qc + (kc8 + 1) * 512],
                            start=True, stop=True)
                        b.inc(ins, "pe")
                    sv = b.val("pe")

                    @blk.vector
                    def _(vector, sv=sv, grp=grp, qi=qi, kc8=kc8, m=m):
                        b.wait(vector, "pe", sv)
                        if m >= 2 and kc8 == 0:
                            b.wait(vector, "dv", nm_dv[m - 2])
                        ins = vector.reduce_max(
                            mx8_sb[:, (qi % 2) * 8 + kc8:(qi % 2) * 8 + kc8 + 1],
                            s_ps[grp][:], axis=AX.X)
                        s_use.append(("dv", b.inc(ins, "dv")))

                @blk.vector
                def _(vector, qi=qi, qb=qb, gqs=gqs):
                    b.wait(vector, "dv", b.val("dv"))
                    if qb == 0 and gqs >= 2:
                        b.wait(vector, "pe", nmT_pe[gqs - 2])
                    ins = vector.reduce_max(
                        nmax_sb[:, (gqs % 2) * 4 + qb:(gqs % 2) * 4 + qb + 1],
                        mx8_sb[:, (qi % 2) * 8:(qi % 2) * 8 + 8],
                        axis=AX.X, negate=True)
                    nm_dv.append(b.inc(ins, "dv"))

        def emit_pass2(gqs):
            p, qs = gqs // 8, gqs % 8
            g2 = gqs % 2
            c0 = g2 * 4

            @blk.tensor
            def _(tensor, gqs=gqs, c0=c0):
                b.wait(tensor, "dv", nm_dv[gqs * 4 + 3])
                if gqs >= 1:
                    b.wait(tensor, "dv", nmrow_dv[gqs - 1])
                for qb in range(4):
                    ins = tensor.matmul(tp_ps[0:1, qb * 128:(qb + 1) * 128],
                                        nmax_sb[:, c0 + qb:c0 + qb + 1],
                                        ident_f_sb[:], start=True, stop=True)
                nmT_pe.append(b.inc(ins, "pe"))

            @blk.vector
            def _(vector, gqs=gqs, g2=g2):
                b.wait(vector, "pe", nmT_pe[-1])
                if gqs >= 1:
                    b.wait(vector, "pe", mbc_pe[gqs - 1])
                ins = vector.tensor_copy(
                    nmrow_sb[0:1, g2 * 512:(g2 + 1) * 512], tp_ps[0:1, 0:512])
                nmrow_dv.append(b.inc(ins, "dv"))

            @blk.tensor
            def _(tensor, gqs=gqs, g2=g2):
                b.wait(tensor, "dv", nmrow_dv[-1])
                if gqs >= 1:
                    b.wait(tensor, "dv", sub_last[gqs - 1])
                ins = tensor.matmul(
                    st_ps[0][:], ones_row_sb[:],
                    nmrow_sb[0:1, g2 * 512:(g2 + 1) * 512],
                    start=True, stop=True)
                mbc_pe.append(b.inc(ins, "pe"))

            @blk.vector
            def _(vector, gqs=gqs, g2=g2):
                b.wait(vector, "pe", mbc_pe[-1])
                if gqs >= 2:
                    b.wait(vector, "dv", sub_last[gqs - 2])
                ins = vector.tensor_copy(maxbc_sb[g2][:], st_ps[0][:, 0:512])
                mbc_dv.append(b.inc(ins, "dv"))
            mv = b.val("dv")

            def emit_pvrs(kc, gqs=gqs, p=p):
                t = gqs * 32 + kc

                @blk.tensor
                def _(tensor, kc=kc, t=t, p=p, gqs=gqs):
                    b.wait(tensor, "ac", exp_ac[t])
                    if kc == 0 and gqs >= 1:
                        b.wait(tensor, "dv", pv_ev[gqs - 1])
                    ins = tensor.matmul(
                        pv_ps[:],
                        v_tm[:, (p * 32 + kc) * 128:(p * 32 + kc + 1) * 128],
                        p_t[t % 4][:], start=(kc == 0), stop=(kc == 31))
                    b.inc(ins, "pe")
                    ins = tensor.matmul(
                        rb_ps[0:1, 0:512], ones_col_sb[:], p_t[t % 4][:],
                        start=(kc == 0), stop=(kc == 31))
                    pvrs_pe.append(b.inc(ins, "pe"))

            for kc in range(32):
                par = kc % 2
                t = gqs * 32 + kc

                @blk.tensor
                def _(tensor, kc=kc, par=par, t=t, p=p, qs=qs, mv=mv):
                    if kc >= 2:
                        b.wait(tensor, "dv", sub_dv[t - 2])
                    else:
                        b.wait(tensor, "dv", mv)
                    ins = tensor.matmul(
                        st_ps[par][:],
                        k_fm[:, p * T + kc * 128:p * T + (kc + 1) * 128],
                        q_fm[:, p * T + qs * 512:p * T + (qs + 1) * 512],
                        start=True, stop=True)
                    b.inc(ins, "pe")
                stv = b.val("pe")

                @blk.vector
                def _(vector, stv=stv, par=par, t=t, g2=g2):
                    b.wait(vector, "pe", stv)
                    if t >= 2:
                        b.wait(vector, "ac", exp_ac[t - 2])
                    ins = vector.tensor_add(d_sb[par][:], st_ps[par][:],
                                            maxbc_sb[g2][:])
                    sub_dv.append(b.inc(ins, "dv"))

                @blk.scalar
                def _(scalar, par=par, t=t):
                    b.wait(scalar, "dv", sub_dv[t])
                    if t >= 4:
                        b.wait(scalar, "pe", pvrs_pe[t - 4])
                    ins = scalar.activation(p_t[t % 4][:], d_sb[par][:], AF.Exp)
                    exp_ac.append(b.inc(ins, "ac"))

                if kc >= 2:
                    emit_pvrs(kc - 2)
            sub_last.append(sub_dv[-1])
            emit_pvrs(30)
            emit_pvrs(31)
            pvv = pvrs_pe[-1]

            @blk.vector
            def _(vector, gqs=gqs, pvv=pvv):
                b.wait(vector, "pe", pvv)
                if gqs >= 1:
                    b.wait(vector, "pe", rr_pe[gqs - 1])
                ins = vector.reciprocal(rrow_sb[:], rb_ps[0:1, 0:512])
                b.inc(ins, "dv")
            rrv = b.val("dv")

            @blk.tensor
            def _(tensor, rrv=rrv):
                b.wait(tensor, "dv", rrv)
                ins = tensor.matmul(rb_ps[:, 0:512], ones_row_sb[:], rrow_sb[:],
                                    start=True, stop=True)
                rr_pe.append(b.inc(ins, "pe"))

            @blk.vector
            def _(vector, gqs=gqs):
                b.wait(vector, "pe", rr_pe[-1])
                ins = vector.tensor_copy(bcast_sb[:], rb_ps[:, 0:512])
                b.inc(ins, "dv")
                b.wait(vector, "dv", b.val("dv"))
                if gqs >= 2:
                    b.wait(vector, "agw", b.val("agw"))
                ins = vector.tensor_mul(ao_sb[gqs % 2][:], pv_ps[:],
                                        bcast_sb[:])
                pv_ev.append(b.inc(ins, "dv"))
            aov = pv_ev[-1]

            @blk.scalar
            def _(scalar, p=p, qs=qs, gqs=gqs, aov=aov):
                b.wait(scalar, "dv", aov)
                ins = scalar.dma_start(
                    out=ag_in.ap()[p * 128:(p + 1) * 128, qs * 512:(qs + 1) * 512],
                    in_=ao_sb[gqs % 2][:])
                ag_wr.append(b.inc(ins, "agw", 16))

        for gqs in range(NP * 8):
            emit_pass1(gqs)
            if gqs >= 1:
                emit_pass2(gqs - 1)
        emit_pass2(NP * 8 - 1)

        flash_done = pv_ev[-1]
        ag_all = ag_wr[-1]
        pa.close()
        preps_all = [("ca_" + k, 16) for k in ["g", "u", "d", "qwb", "kwb",
                                                "vwb", "owb"]]
        import os as _os2
        if _os2.environ.get("KMEM"):
            print("after flash close:", nc.sbuf_bytes_remaining)

        @blk.gpsimd
        def _(gpsimd):
            b.wait(gpsimd, "agw", ag_all)
            ins = gpsimd.collective_compute(
                "AllGather", ALU.bypass, RG, [ag_in.ap().opt()], [ag_out.ap().opt()])
            b.inc(ins, "coll")
        coll_ag = b.val("coll")

        x_res = sball("x_res", [128, NTB * D], F32)
        pa6 = contextlib.ExitStack()
        og_sb = sball("og_sb", [128, NH * 1024], BF16, pa6)
        woT_sb = sball("woT_sb", [128, NH * D], BF16, pa6)
        x0s = [sball(f"x0s{i}", [128, D], F32, pa6) for i in range(2)]
        op_ps = psall("op_psA", [128, 2048], pa6)

        @blk.sync
        def _(sync):
            for nm, v in preps_all:
                b.wait(sync, nm, v)
            b.wait(sync, "ca_woa", 16)
            b.wait(sync, "dv", flash_done)
            b.wait(sync, "agw", ag_all)
            for h in range(NH):
                ins = sync.dma_start(out=woT_sb[:, h * D:(h + 1) * D],
                                     in_=wT["woa"].ap()[h * 128:(h + 1) * 128, 0:D])
                b.inc(ins, "wow", 16)
        woT_v = b.val("wow")

        @blk.gpsimd
        def _(gpsimd):
            b.wait(gpsimd, "coll", coll_ag)
            pid = gpsimd.partition_id()
            to0v = gpsimd.compute_val((pid % 4) * TOK)
            for h in range(NH):
                ins = gpsimd.dma_start(
                    out=og_sb[:, h * 1024:(h + 1) * 1024],
                    in_=ag_out.ap()[h * 128:(h + 1) * 128, bass.ds(to0v, TOK)])
                b.inc(ins, "ogw", 16)
        og_v = b.val("ogw")

        op_free = []
        for tb in range(NTB):
            par = tb % 2

            @blk.scalar
            def _(scalar, tb=tb, par=par):
                b.wait(scalar, "dv", flash_done)
                b.wait(scalar, "agw", ag_all)
                if tb >= 2:
                    b.wait(scalar, "dv", op_free[tb - 2])
                ins = scalar.dma_start(out=x0s[par][:],
                                       in_=X0M[tb * 128:(tb + 1) * 128, :])
                b.inc(ins, "xsw%d" % par, 16)
            xv = b.val("xsw%d" % par)

            @blk.tensor
            def _(tensor, tb=tb, xv=xv):
                if tb == 0:
                    b.wait(tensor, "ogw", og_v)
                    b.wait(tensor, "wow", woT_v)
                if tb >= 1:
                    b.wait(tensor, "dv", op_free[tb - 1])
                for h in range(NH):
                    for ods in range(4):
                        ins = tensor.matmul(
                            op_ps[:, ods * 512:(ods + 1) * 512],
                            og_sb[:, h * 1024 + tb * 128:h * 1024 + (tb + 1) * 128],
                            woT_sb[:, h * D + ods * 512:h * D + (ods + 1) * 512],
                            start=(h == 0), stop=(h == NH - 1))
                b.inc(ins, "pe")
            ov = b.val("pe")

            @blk.vector
            def _(vector, tb=tb, par=par, ov=ov, xv=xv):
                b.wait(vector, "pe", ov)
                b.wait(vector, "xsw%d" % par, xv)
                for ods in range(4):
                    ins = vector.tensor_add(
                        x_res[:, tb * D + ods * 512:tb * D + (ods + 1) * 512],
                        op_ps[:, ods * 512:(ods + 1) * 512],
                        x0s[par][:, ods * 512:(ods + 1) * 512])
                op_free.append(b.inc(ins, "dv"))
        pa6.close()
        import os as _os3
        if _os3.environ.get("KMEM"):
            print("after A6 close:", nc.sbuf_bytes_remaining)
        emit_dbg("x1")

        def emit_ffn(tag):
            ps = contextlib.ExitStack()
            sb = lambda n, s, d=BF16: sball(n + tag, s, d, ps)
            lnt = [sb(f"lnt{i}", [128, D]) for i in range(2)]
            sqt = sb("sqt", [128, D])
            lnf = sb("lnf", [128, NDC * 1024])
            cs_cols = sb("cs_cols", [128, NTB], F32)
            csr = sb("csr", [1, 1024], F32)
            cs_bc = sb("cs_bc", [128, 1024], F32)
            cshr = sb("cshr", [1, 512], F32)
            csh_bc = sb("csh_bc", [128, 512], F32)
            h_fm = sb("h_fm", [128, NHC * 512])
            tg_sc = [sb(f"tg{i}", [128, 512], F32) for i in range(2)]
            tu_sc = [sb(f"tu{i}", [128, 512], F32) for i in range(2)]
            gsig = [sb(f"gsig{i}", [128, 512]) for i in range(2)]
            dnf = sb("dnf", [128, 2048], F32)
            wring = [sb(f"wring{i}", [128, NDC * 128]) for i in range(3)]
            dring = [sb(f"dring{i}", [128, (NHC // 2) * 128]) for i in range(2)]
            tp2 = psall("tp2" + tag, [128, 1024], ps)
            gu_ps = psall("gu" + tag, [128, 2048], ps)
            dn_ps = psall("dn" + tag, [128, 1024], ps)

            lnt_free = [0, 0]
            for tb in range(NTB):
                par = tb % 2

                @blk.vector
                def _(vector, par=par):
                    b.wait(vector, "pe", lnt_free[par])
                lv = ln_tile(x_res[:, tb * D:(tb + 1) * D], lnt[par], par, sqt,
                             colsum_to=(cs_cols, tb))
                transpose_blocks(
                    tp2,
                    [(lnt[par], dc * 128) for dc in range(NDC)],
                    [(lnf, dc * 1024 + tb * 128) for dc in range(NDC)],
                    pe_waits=[("dv", lv)],
                )
                lnt_free[par] = b.val("pe")
            lnf_v = b.val("dv")

            @blk.tensor
            def _(tensor, lnf_v=lnf_v):
                b.wait(tensor, "dv", lnf_v)
                for tb in range(NTB):
                    ins = tensor.matmul(gu_ps[0:1, tb * 128:(tb + 1) * 128],
                                        cs_cols[:, tb:tb + 1], ident_f_sb[:],
                                        start=True, stop=True)
                b.inc(ins, "pe")
            csv = b.val("pe")

            @blk.vector
            def _(vector, csv=csv):
                b.wait(vector, "pe", csv)
                ins = vector.tensor_copy(csr[:], gu_ps[0:1, 0:1024])
                b.inc(ins, "dv")
            csrv = b.val("dv")

            @blk.tensor
            def _(tensor, csrv=csrv):
                b.wait(tensor, "dv", csrv)
                tensor.matmul(gu_ps[:, 0:512], row128_sb[:], csr[0:1, 0:512],
                              start=True, stop=True)
                ins = tensor.matmul(gu_ps[:, 512:1024], row128_sb[:], csr[0:1, 512:1024],
                                    start=True, stop=True)
                b.inc(ins, "pe")
            csbv = b.val("pe")

            @blk.vector
            def _(vector, csbv=csbv):
                b.wait(vector, "pe", csbv)
                ins = vector.tensor_copy(cs_bc[:], gu_ps[:, 0:1024])
                b.inc(ins, "dv")
            csbc_v = b.val("dv")

            wring_free = []
            dring_free = []
            gu_free = []
            dn_free = []
            th_gate_wait = csbc_v
            for th in range(2):
                for oc in range(NHC):
                    idx = th * NHC + oc
                    for iw, key in enumerate(["g", "u"]):
                        li = idx * 2 + iw
                        slot = li % 3

                        @blk.sync
                        def _(sync, key=key, oc=oc, li=li, slot=slot):
                            for nm, v in prep_done[key]:
                                b.wait(sync, nm, v)
                            if li >= 3:
                                b.wait(sync, "pe", wring_free[li - 3])
                            ins = sync.dma_start(
                                out=wring[slot][:],
                                in_=wT[key].ap()[oc * 128:(oc + 1) * 128, 0:D])
                            b.inc(ins, "fw%d" % slot, 16)
                    wv = [b.val("fw%d" % ((idx * 2 + iw) % 3)) for iw in range(2)]
                    par = idx % 2

                    @blk.tensor
                    def _(tensor, th=th, idx=idx, par=par, wv=wv,
                          th_gate_wait=th_gate_wait):
                        for iw in range(2):
                            b.wait(tensor, "fw%d" % ((idx * 2 + iw) % 3), wv[iw])
                        if idx % NHC == 0:
                            b.wait(tensor, "dv", th_gate_wait)
                        if idx >= 2:
                            b.wait(tensor, "dv", gu_free[idx - 2])
                        for iw in range(2):
                            slot = (idx * 2 + iw) % 3
                            bank = iw * 2 + par
                            for dc in range(NDC):
                                ins = tensor.matmul(
                                    gu_ps[:, bank * 512:(bank + 1) * 512],
                                    wring[slot][:, dc * 128:(dc + 1) * 128],
                                    lnf[:, dc * 1024 + th * 512:dc * 1024 + (th + 1) * 512],
                                    start=(dc == 0), stop=(dc == NDC - 1))
                            b.inc(ins, "pe")
                        wring_free.append(b.val("pe"))
                        wring_free.append(b.val("pe"))
                    gv = b.val("pe")

                    @blk.vector
                    def _(vector, oc=oc, th=th, par=par, gv=gv):
                        b.wait(vector, "pe", gv)
                        ins = vector.scalar_tensor_tensor(
                            tg_sc[par][:], gu_ps[:, par * 512:(par + 1) * 512],
                            1.0, cs_bc[:, th * 512:(th + 1) * 512],
                            ALU.mult, ALU.subtract)
                        b.inc(ins, "dv")
                        b.wait(vector, "dv", b.val("dv"))
                        ins = vector.tensor_scalar_mul(tg_sc[par][:], tg_sc[par][:],
                                                       sg_sb[:, oc:oc + 1])
                        b.inc(ins, "dv")
                        ins = vector.tensor_sub(tu_sc[par][:],
                                                gu_ps[:, (2 + par) * 512:(3 + par) * 512],
                                                cs_bc[:, th * 512:(th + 1) * 512])
                        b.inc(ins, "dv")
                    sv = b.val("dv")

                    @blk.scalar
                    def _(scalar, par=par, sv=sv):
                        b.wait(scalar, "dv", sv)
                        ins = scalar.activation(gsig[par][:], tg_sc[par][:], AF.Silu)
                        b.inc(ins, "ac")
                    av = b.val("ac")

                    @blk.vector
                    def _(vector, oc=oc, par=par, av=av):
                        b.wait(vector, "ac", av)
                        ins = vector.scalar_tensor_tensor(
                            h_fm[:, oc * 512:(oc + 1) * 512], tu_sc[par][:],
                            su_sb[:, oc:oc + 1], gsig[par][:], ALU.mult, ALU.mult)
                        gu_free.append(b.inc(ins, "dv"))
                h_v = gu_free[-1]

                @blk.tensor
                def _(tensor, h_v=h_v):
                    b.wait(tensor, "dv", h_v)
                    for hc in range(NHC):
                        ins = tensor.matmul(gu_ps[0:1, 0:512], ones_col_sb[:],
                                            h_fm[:, hc * 512:(hc + 1) * 512],
                                            start=(hc == 0), stop=(hc == NHC - 1))
                    b.inc(ins, "pe")
                chv = b.val("pe")

                @blk.vector
                def _(vector, chv=chv):
                    b.wait(vector, "pe", chv)
                    ins = vector.tensor_scalar_mul(cshr[:], gu_ps[0:1, 0:512], 128.0)
                    b.inc(ins, "dv")
                chrv = b.val("dv")

                @blk.tensor
                def _(tensor, chrv=chrv):
                    b.wait(tensor, "dv", chrv)
                    ins = tensor.matmul(gu_ps[:, 512:1024], ones_row_sb[:], cshr[:],
                                        start=True, stop=True)
                    b.inc(ins, "pe")
                chbv = b.val("pe")

                @blk.vector
                def _(vector, chbv=chbv):
                    b.wait(vector, "pe", chbv)
                    ins = vector.tensor_copy(csh_bc[:], gu_ps[:, 512:1024])
                    b.inc(ins, "dv")
                cshbc_v = b.val("dv")
                th_gate_wait = cshbc_v

                HH = NHC // 2
                for dimc in range(NDC):
                    di = th * NDC + dimc
                    bank = di % 2
                    dwvs = []
                    for hf in range(2):
                        hi = di * 2 + hf

                        @blk.sync
                        def _(sync, dimc=dimc, hf=hf, hi=hi):
                            for nm, v in prep_done["d"]:
                                b.wait(sync, nm, v)
                            if hi >= 2:
                                b.wait(sync, "pe", dring_free[hi - 2])
                            ins = sync.dma_start(
                                out=dring[hf][:],
                                in_=wT["d"].ap()[dimc * 128:(dimc + 1) * 128,
                                                 hf * HH * 128:(hf + 1) * HH * 128])
                            b.inc(ins, "dw%d" % hf, 16)
                        dwvs.append(b.val("dw%d" % hf))

                    for hf in range(2):
                        @blk.tensor
                        def _(tensor, di=di, hf=hf, bank=bank, dwv=dwvs[hf],
                              cshbc_v=cshbc_v):
                            b.wait(tensor, "dw%d" % hf, dwv)
                            if di % NDC == 0 and hf == 0:
                                b.wait(tensor, "dv", cshbc_v)
                            if di >= 2 and hf == 0:
                                b.wait(tensor, "dv", dn_free[di - 2])
                            for hc2 in range(HH):
                                hc = hf * HH + hc2
                                ins = tensor.matmul(
                                    dn_ps[:, bank * 512:(bank + 1) * 512],
                                    dring[hf][:, hc2 * 128:(hc2 + 1) * 128],
                                    h_fm[:, hc * 512:(hc + 1) * 512],
                                    start=(hc == 0), stop=(hc == NHC - 1))
                            b.inc(ins, "pe")
                            dring_free.append(b.val("pe"))
                    dmv = b.val("pe")

                    @blk.vector
                    def _(vector, dimc=dimc, bank=bank, dmv=dmv):
                        b.wait(vector, "pe", dmv)
                        ins = vector.tensor_sub(dnf[:, bank * 1024:bank * 1024 + 512],
                                                dn_ps[:, bank * 512:(bank + 1) * 512],
                                                csh_bc[:])
                        b.inc(ins, "dv")
                        b.wait(vector, "dv", b.val("dv"))
                        ins = vector.tensor_scalar_mul(
                            dnf[:, bank * 1024 + 512:(bank + 1) * 1024],
                            dnf[:, bank * 1024:bank * 1024 + 512],
                            sd_sb[:, dimc:dimc + 1])
                        dn_free.append(b.inc(ins, "dv"))
                    dfv = b.val("dv")

                    for j in range(4):
                        tb = th * 4 + j
                        tpar = tstate["n"] % 2
                        tstate["n"] += 1

                        @blk.tensor
                        def _(tensor, j=j, tpar=tpar, dfv=dfv, bank=bank):
                            if len(tstate["hist"]) >= 2:
                                b.wait(tensor, "dv", tstate["hist"][-2])
                            if j == 0:
                                b.wait(tensor, "dv", dfv)
                            ins = tensor.matmul(
                                tp2[:, tpar * 512:tpar * 512 + 128],
                                dnf[:, bank * 1024 + 512 + j * 128:
                                    bank * 1024 + 512 + (j + 1) * 128],
                                ident_f_sb[:], start=True, stop=True)
                            b.inc(ins, "pe")
                        tpv = b.val("pe")

                        @blk.vector
                        def _(vector, tb=tb, dimc=dimc, tpar=tpar, tpv=tpv):
                            b.wait(vector, "pe", tpv)
                            ins = vector.tensor_add(
                                x_res[:, tb * D + dimc * 128:tb * D + (dimc + 1) * 128],
                                x_res[:, tb * D + dimc * 128:tb * D + (dimc + 1) * 128],
                                tp2[:, tpar * 512:tpar * 512 + 128])
                            tstate["hist"].append(b.inc(ins, "dv"))
            ps.close()

        def emit_attn_b():
            psc = contextlib.ExitStack()
            sb = lambda n, s, d=BF16: sball(n, s, d, psc)
            tpB = psall("tpB", [128, 1024], psc)
            pjB = psall("pjB", [128, 2048], psc)
            smB = psall("smB", [128, 1024], psc)

            qp_fm = sball("qp_fm", [128, NDC * 1024], BF16, psc)
            at_fm = sball("at_fm", [128, NDC * 512], BF16, psc)
            ps1b = contextlib.ExitStack()
            kp_fm = sball("kp_fm", [128, NDC * 1024], BF16, ps1b)
            v_fm = sball("bv_fm", [128, NDC * 1024], BF16, ps1b)
            ps1 = contextlib.ExitStack()
            lnt = [sball("blnt0", [128, D], BF16, ps1)]
            lnf = sball("blnf", [128, NDC * 512], BF16, ps1)
            t_el = [sball("t_el0", [128, 512], BF16, ps1)]
            wringB = [sball(f"wringB{i}", [128, NDC * 128], BF16, ps1) for i in range(2)]

            lnt_free = [0]
            wring_free = []
            pj_free2 = []
            el_ac = []
            b_slab_done = [0]
            for sl in range(2):
                for tt in range(4):
                    tb = sl * 4 + tt

                    @blk.vector
                    def _(vector):
                        b.wait(vector, "pe", lnt_free[0])
                    lv = ln_tile(x_res[:, tb * D:(tb + 1) * D], lnt[0], tb % 2,
                                 lnt[0], scalar_gate=lnt_free[0])
                    transpose_blocks(
                        tpB,
                        [(lnt[0], dc * 128) for dc in range(NDC)],
                        [(lnf, dc * 512 + tt * 128) for dc in range(NDC)],
                        pe_waits=[("dv", lv), ("pe", b_slab_done[0])],
                    )
                    lnt_free[0] = b.val("pe")
                lnf_v = b.val("dv")

                for iw, (key, bias_t) in enumerate([("qwb", qb_sb), ("kwb", kb_sb),
                                                    ("vwb", vb_sb)]):
                    dst = [qp_fm, kp_fm, v_fm][iw]
                    for oc in range(NDC):
                        li = len(wring_free)
                        slot = li % 2

                        @blk.sync
                        def _(sync, key=key, oc=oc, li=li, slot=slot):
                            for nm, v in prep_done[key]:
                                b.wait(sync, nm, v)
                            if li >= 2:
                                b.wait(sync, "pe", wring_free[li - 2])
                            ins = sync.dma_start(
                                out=wringB[slot][:],
                                in_=wT[key].ap()[oc * 128:(oc + 1) * 128, 0:D])
                            b.inc(ins, "bw%d" % slot, 16)
                        wv = b.val("bw%d" % slot)
                        gi = len(pj_free2)
                        bank = gi % 4

                        @blk.tensor
                        def _(tensor, slot=slot, gi=gi, bank=bank, wv=wv,
                              lnf_v=lnf_v):
                            b.wait(tensor, "bw%d" % slot, wv)
                            b.wait(tensor, "dv", lnf_v)
                            if gi >= 4:
                                b.wait(tensor, "dv", pj_free2[gi - 4])
                            for dc in range(NDC):
                                ins = tensor.matmul(
                                    pjB[:, bank * 512:(bank + 1) * 512],
                                    wringB[slot][:, dc * 128:(dc + 1) * 128],
                                    lnf[:, dc * 512:(dc + 1) * 512],
                                    start=(dc == 0), stop=(dc == NDC - 1))
                            b.inc(ins, "pe")
                        wring_free.append(b.val("pe"))
                        mv = b.val("pe")

                        if iw < 2:
                            dslice = dst[:, oc * 1024 + sl * 512:oc * 1024 + (sl + 1) * 512]

                            @blk.vector
                            def _(vector, oc=oc, bank=bank, mv=mv, bias_t=bias_t):
                                b.wait(vector, "pe", mv)
                                if len(el_ac) >= 1:
                                    b.wait(vector, "ac", el_ac[-1])
                                if pj_free2:
                                    b.wait(vector, "dv", pj_free2[-1])
                                ins = vector.tensor_scalar(
                                    t_el[0][:], pjB[:, bank * 512:(bank + 1) * 512],
                                    bias_t[:, oc:oc + 1], 0.0, ALU.add, ALU.min)
                                b.inc(ins, "dv")
                            elv = b.val("dv")

                            @blk.scalar
                            def _(scalar, dslice=dslice, elv=elv):
                                b.wait(scalar, "dv", elv)
                                ins = scalar.activation(dslice, t_el[0][:], AF.Exp)
                                el_ac.append(b.inc(ins, "ac"))
                            eav = b.val("ac")

                            @blk.vector
                            def _(vector, dslice=dslice, oc=oc, bank=bank,
                                  eav=eav, bias_t=bias_t):
                                b.wait(vector, "ac", eav)
                                ins = vector.tensor_scalar(
                                    t_el[0][:], pjB[:, bank * 512:(bank + 1) * 512],
                                    bias_t[:, oc:oc + 1], 0.0, ALU.add, ALU.max)
                                b.inc(ins, "dv")
                                b.wait(vector, "dv", b.val("dv"))
                                ins = vector.tensor_add(dslice, dslice, t_el[0][:])
                                pj_free2.append(b.inc(ins, "dv"))
                        else:
                            @blk.vector
                            def _(vector, dst=dst, oc=oc, sl=sl, bank=bank, mv=mv,
                                  bias_t=bias_t):
                                b.wait(vector, "pe", mv)
                                ins = vector.tensor_scalar(
                                    dst[:, oc * 1024 + sl * 512:oc * 1024 + (sl + 1) * 512],
                                    pjB[:, bank * 512:(bank + 1) * 512],
                                    bias_t[:, oc:oc + 1], None, ALU.add)
                                pj_free2.append(b.inc(ins, "dv"))

                b_slab_done[0] = b.val("pe")
            proj_v = b.val("dv")

            ps1.close()
            km_tm = sball("km_tm", [128, 8 * 128], BF16, ps1b)
            vm_tm = sball("vm_tm", [128, 8 * 128], BF16, ps1b)
            kvt = sball("kvt", [128, 128], F32, ps1b)
            ksrow = sball("ksrow", [1, 128], F32, ps1b)
            kv_wr = []
            for h in range(NH):
                tv = transpose_blocks(
                    tpB,
                    [(kp_fm, h * 1024 + j * 128) for j in range(8)]
                    + [(v_fm, h * 1024 + j * 128) for j in range(8)],
                    [(km_tm, j * 128) for j in range(8)]
                    + [(vm_tm, j * 128) for j in range(8)],
                    pe_waits=[("dv", proj_v)] + ([("kvw", kv_wr[-1])] if h else []),
                )

                @blk.tensor
                def _(tensor, h=h, tv=tv):
                    b.wait(tensor, "dv", tv)
                    if h >= 1:
                        b.wait(tensor, "dv", kv_ev[-1])
                    for j in range(8):
                        tensor.matmul(smB[:, 0:128], km_tm[:, j * 128:(j + 1) * 128],
                                      vm_tm[:, j * 128:(j + 1) * 128],
                                      start=(j == 0), stop=(j == 7))
                    for j in range(8):
                        ins = tensor.matmul(smB[0:1, 512:640], ones_col_sb[:],
                                            km_tm[:, j * 128:(j + 1) * 128],
                                            start=(j == 0), stop=(j == 7))
                    b.inc(ins, "pe")
                kvv = b.val("pe")

                @blk.vector
                def _(vector, kvv=kvv):
                    b.wait(vector, "pe", kvv)
                    vector.tensor_copy(kvt[:], smB[:, 0:128])
                    ins = vector.tensor_copy(ksrow[:], smB[0:1, 512:640])
                    kv_ev.append(b.inc(ins, "dv"))
                kev = kv_ev[-1]

                @blk.gpsimd
                def _(gpsimd, h=h, kev=kev):
                    b.wait(gpsimd, "dv", kev)
                    ins = gpsimd.dma_start(out=ar_in.ap()[h * 128:(h + 1) * 128, :],
                                           in_=kvt[:])
                    b.inc(ins, "kvw", 16)
                    ins = gpsimd.dma_start(
                        out=ar_in.ap()[NH * DH + h:NH * DH + h + 1, :], in_=ksrow[:])
                    kv_wr.append(b.inc(ins, "kvw", 16))
            ps1b.close()

            @blk.gpsimd
            def _(gpsimd):
                b.wait(gpsimd, "kvw", kv_wr[-1])
                ins = gpsimd.collective_compute(
                    "AllReduce", ALU.add, RG, [ar_in.ap().opt()], [ar_out.ap().opt()])
                b.inc(ins, "coll")
            coll_ar = b.val("coll")

            ps2 = contextlib.ExitStack()
            kvb = sball("kvb", [128, NH * 128], BF16, ps2)
            zrows = sball("zrows", [1, 4 * 512], F32, ps2)
            ksf = sball("ksf", [128, NH], F32, ps2)
            ksb = sball("ksb", [128, NH], BF16, ps2)

            @blk.gpsimd
            def _(gpsimd):
                b.wait(gpsimd, "coll", coll_ar)
                ins = gpsimd.dma_start(
                    out=kvb[:].rearrange("p (a c) -> p a c", c=128),
                    in_=ar_out.ap()[0:NH * DH, :].rearrange("(a p) c -> p a c", p=128))
                b.inc(ins, "kvl", 16)
                ins = gpsimd.dma_start(
                    out=ksf[:],
                    in_=bass.AP(ar_out.ap().tensor, (NH * DH) * DH, [[1, 128], [128, NH]]),
                    allow_slow_non_contiguous=True)
                b.inc(ins, "kvl", 16)
            kv_ld = b.val("kvl")

            @blk.vector
            def _(vector, kv_ld=kv_ld):
                b.wait(vector, "kvl", kv_ld)
                ins = vector.tensor_copy(ksb[:], ksf[:])
                b.inc(ins, "dv")
            ksb_v = b.val("dv")

            ps3 = contextlib.ExitStack()
            owT_b = sball("owT_b", [128, NDC * D], BF16, ps3)
            ob_row = sball("ob_row", [1, D], F32, ps3)
            ob_bc = sball("ob_bc", [128, D], F32, ps3)

            @blk.sync
            def _(sync):
                for nm, v in prep_done["owb"]:
                    b.wait(sync, nm, v)
                b.wait(sync, "kvw", kv_wr[-1])
                for ec in range(NDC):
                    ins = sync.dma_start(out=owT_b[:, ec * D:(ec + 1) * D],
                                         in_=wT["owb"].ap()[ec * 128:(ec + 1) * 128, 0:D])
                    b.inc(ins, "oww", 16)
                ins = sync.dma_start(out=ob_row[:],
                                     in_=bass.AP(EXT["ob_b"].tensor, 0, [[D, 1], [1, D]]))
                b.inc(ins, "oww", 16)
            ow_v = b.val("oww")

            @blk.tensor
            def _(tensor, ow_v=ow_v):
                b.wait(tensor, "oww", ow_v)
                for j in range(4):
                    ins = tensor.matmul(pjB[:, j * 512:(j + 1) * 512], ones_row_sb[:],
                                        ob_row[0:1, j * 512:(j + 1) * 512],
                                        start=True, stop=True)
                b.inc(ins, "pe")
            obv = b.val("pe")

            @blk.vector
            def _(vector, obv=obv):
                b.wait(vector, "pe", obv)
                ins = vector.tensor_copy(ob_bc[:], pjB[:, 0:2048])
                b.inc(ins, "dv")
            ob_v = b.val("dv")

            op_free2 = []
            at_ev = []
            zr_ev = []
            zb_ev = []
            for th in range(2):
                for hg in range(4):
                    for hh in range(4):
                        h = hg * 4 + hh

                        @blk.tensor
                        def _(tensor, h=h, hh=hh, th=th):
                            if th == 0 and h == 0:
                                b.wait(tensor, "dv", ksb_v)
                            if hh == 0 and zb_ev:
                                b.wait(tensor, "dv", zb_ev[-1])
                            if len(zr_ev) >= 2:
                                b.wait(tensor, "dv", zr_ev[-2])
                            ins = tensor.matmul(
                                smB[0:1, (h % 2) * 512:(h % 2) * 512 + 512],
                                ksb[:, h:h + 1],
                                qp_fm[:, h * 1024 + th * 512:h * 1024 + (th + 1) * 512],
                                start=True, stop=True)
                            b.inc(ins, "pe")
                        zv = b.val("pe")

                        @blk.vector
                        def _(vector, h=h, hh=hh, zv=zv):
                            b.wait(vector, "pe", zv)
                            ins = vector.tensor_scalar_add(
                                zrows[0:1, hh * 512:(hh + 1) * 512],
                                smB[0:1, (h % 2) * 512:(h % 2) * 512 + 512], Z_EPS)
                            zr_ev.append(b.inc(ins, "dv"))

                    @blk.vector
                    def _(vector):
                        b.wait(vector, "dv", zr_ev[-1])
                        ins = vector.reciprocal(zrows[:], zrows[:])
                        b.inc(ins, "dv")
                    zrc_v = b.val("dv")

                    for hh in range(4):
                        h = hg * 4 + hh

                        @blk.tensor
                        def _(tensor, h=h, hh=hh, zrc_v=zrc_v):
                            b.wait(tensor, "dv", zrc_v)
                            if len(zb_ev) >= 2:
                                b.wait(tensor, "dv", zb_ev[-2])
                            ins = tensor.matmul(
                                smB[:, (h % 2) * 512:(h % 2) * 512 + 512],
                                ones_row_sb[:], zrows[0:1, hh * 512:(hh + 1) * 512],
                                start=True, stop=True)
                            b.inc(ins, "pe")
                        zbv = b.val("pe")

                        @blk.vector
                        def _(vector, h=h, th=th, zbv=zbv):
                            b.wait(vector, "pe", zbv)
                            qsl = qp_fm[:, h * 1024 + th * 512:h * 1024 + (th + 1) * 512]
                            ins = vector.tensor_tensor(
                                qsl, qsl, smB[:, (h % 2) * 512:(h % 2) * 512 + 512],
                                ALU.mult)
                            zb_ev.append(b.inc(ins, "dv"))

                for h in range(NH):
                    qsc = zb_ev[-(NH - h)]

                    @blk.tensor
                    def _(tensor, h=h, th=th, qsc=qsc):
                        b.wait(tensor, "dv", qsc)
                        if h == 0:
                            b.wait(tensor, "dv", zb_ev[-1])
                        if len(at_ev) >= 2:
                            b.wait(tensor, "dv", at_ev[-2])
                        ins = tensor.matmul(
                            smB[0:128, (h % 2) * 512:(h % 2) * 512 + 512],
                            kvb[:, h * 128:(h + 1) * 128],
                            qp_fm[:, h * 1024 + th * 512:h * 1024 + (th + 1) * 512],
                            start=True, stop=True)
                        b.inc(ins, "pe")
                    nv = b.val("pe")

                    @blk.vector
                    def _(vector, h=h, nv=nv):
                        b.wait(vector, "pe", nv)
                        ins = vector.tensor_copy(at_fm[:, h * 512:(h + 1) * 512],
                                                 smB[0:128, (h % 2) * 512:(h % 2) * 512 + 512])
                        at_ev.append(b.inc(ins, "dv"))
                at_v = at_ev[-1]

                for j in range(4):
                    tb = th * 4 + j

                    @blk.tensor
                    def _(tensor, tb=tb, j=j, th=th, at_v=at_v, ob_v=ob_v):
                        if j == 0:
                            b.wait(tensor, "dv", at_v)
                            b.wait(tensor, "dv", ob_v)
                        if len(op_free2) >= 1:
                            b.wait(tensor, "dv", op_free2[-1])
                        for ec in range(NDC):
                            for ods in range(4):
                                ins = tensor.matmul(
                                    pjB[:, ods * 512:(ods + 1) * 512],
                                    at_fm[:, ec * 512 + j * 128:ec * 512 + (j + 1) * 128],
                                    owT_b[:, ec * D + ods * 512:ec * D + (ods + 1) * 512],
                                    start=(ec == 0), stop=(ec == NDC - 1))
                        b.inc(ins, "pe")
                    ov = b.val("pe")

                    @blk.vector
                    def _(vector, tb=tb, ov=ov):
                        b.wait(vector, "pe", ov)
                        for ods in range(4):
                            xsl = x_res[:, tb * D + ods * 512:tb * D + (ods + 1) * 512]
                            if ods > 0 or tb > 0:
                                b.wait(vector, "dv", b.val("dv"))
                            ins = vector.tensor_add(xsl, xsl,
                                                    pjB[:, ods * 512:(ods + 1) * 512])
                            b.inc(ins, "dv")
                            b.wait(vector, "dv", b.val("dv"))
                            ins = vector.tensor_add(xsl, xsl,
                                                    ob_bc[:, ods * 512:(ods + 1) * 512])
                            b.inc(ins, "dv")
                        op_free2.append(b.val("dv"))
            ps3.close()
            ps2.close()
            psc.close()

        kv_ev = []

        if STAGE >= 2:
            barrier(dma_waits=preps_all)
            emit_ffn("F1")
            emit_dbg("x2")
        if STAGE >= 3:
            barrier()
            emit_attn_b()
            emit_dbg("x3")
        if STAGE >= 4:
            barrier()
            emit_ffn("F2")

        fin = b.val("dv")

        @blk.gpsimd
        def _(gpsimd):
            b.wait(gpsimd, "dv", fin)
            for tb in range(NTB):
                ins = gpsimd.dma_start(out=Y[tb * 128:(tb + 1) * 128, :],
                                       in_=x_res[:, tb * D:(tb + 1) * D])
                b.inc(ins, "st", 16)
            b.wait(gpsimd, "st", b.val("st"))

    stack.close()
    return nc, dbg_outs


def _blk2(W):
    O, I = W.shape
    return np.ascontiguousarray(
        W.reshape(O // 128, 128, I // 128, 128).transpose(0, 3, 2, 1)
         .reshape(O, I))


def _cols(v):
    return np.ascontiguousarray(v.reshape(-1, 128).T)


def _in_maps(inputs):
    x = np.ascontiguousarray(inputs["L0_attn_a_x"])
    shared = {
        "wo_a": np.ascontiguousarray(inputs["L0_attn_a_Wo"].T),
        "qw_b": _blk2(inputs["L0_attn_b_q_w"]),
        "kw_b": _blk2(inputs["L0_attn_b_k_w"]),
        "vw_b": _blk2(inputs["L0_attn_b_v_w"]),
        "ow_b": np.ascontiguousarray(inputs["L0_attn_b_o_w"].T),
        "qb_b": _cols(inputs["L0_attn_b_q_b"]),
        "kb_b": _cols(inputs["L0_attn_b_k_b"]),
        "vb_b": _cols(inputs["L0_attn_b_v_b"]),
        "ob_b": np.ascontiguousarray(inputs["L0_attn_b_o_b"]),
        "wg_i": _blk2(inputs["L0_ffn_wq_gate"]),
        "wu_i": _blk2(inputs["L0_ffn_wq_up"]),
        "wd_i": _blk2(inputs["L0_ffn_wq_down"]),
        "sg": _cols(inputs["L0_ffn_s_gate"]),
        "su": _cols(inputs["L0_ffn_s_up"]),
        "sd": _cols(inputs["L0_ffn_s_down"]),
    }
    qkv = [{k: _blk2(inputs["L0_attn_a_W" + k][h0 * DH:(h0 + 4) * DH])
            for k in ("q", "k", "v")} for h0 in (0, 4, 8, 12)]
    maps = []
    for c in range(8):
        bb, gg = c // 4, c % 4
        m = {
            "x0": np.ascontiguousarray(x[bb]),
            "x0m": np.ascontiguousarray(x[bb, gg * TOK:(gg + 1) * TOK]),
            "wq_a": qkv[gg]["q"], "wk_a": qkv[gg]["k"], "wv_a": qkv[gg]["v"],
        }
        m.update(shared)
        maps.append(m)
    return maps


def kernel(**inputs):
    nc, _ = build_core_program(0)
    res = run_bass_kernel_spmd(nc, _in_maps(inputs), core_ids=list(range(8)))
    out = np.empty((B_, T, D), dtype=np.float32)
    for c in range(8):
        bb, gg = c // 4, c % 4
        out[bb, gg * TOK:(gg + 1) * TOK, :] = res.results[c]["y"]
    return out
